# version 02
# baseline (speedup 1.0000x reference)
import contextlib
import os
import numpy as np
import concourse.bass as bass
import concourse.mybir as mybir
from concourse.bass_utils import run_bass_kernel_spmd

F32 = mybir.dt.float32
BF16 = mybir.dt.bfloat16
AF = mybir.ActivationFunctionType
ALU = mybir.AluOpType
AX = mybir.AxisListType.X

D = 1024
T = 2048
NTT = 16
DT = 8
FF = 4096
NE = 9
NQ = 4
FQ = 1024
FT = 8
WINDOWS = [(0, 384), (384, 384), (768, 384), (1152, 384), (1536, 384), (1920, 128)]
NW = len(WINDOWS)
NQTOT_PASS = NW * NE * NQ

STAGE = "full"


class Prog:
    def __init__(self):
        self.nc = bass.Bass("TRN2", target_bir_lowering=False, debug=False,
                            detect_race_conditions=False)
        self.stack = contextlib.ExitStack()
        self.sems = {}
        self.cnt = {}
        self.nbar = 0

    def sem(self, name):
        if name not in self.sems:
            self.sems[name] = self.stack.enter_context(self.nc.semaphore(name))
            self.cnt[name] = 0
        return self.sems[name]

    def bump(self, name, n=1):
        self.cnt[name] += n
        return self.cnt[name]

    def sbuf(self, name, shape, dt=F32, scope=None):
        st = scope if scope is not None else self.stack
        return st.enter_context(self.nc.sbuf_tensor(name, shape, dt))

    def psum(self, name, shape, dt=F32, scope=None):
        st = scope if scope is not None else self.stack
        return st.enter_context(self.nc.psum_tensor(name, shape, dt))

    def edge(self, inst, eng):
        se = self.sem("selfe")
        n = self.bump("selfe")
        inst.then_inc(se, 1)
        eng.wait_ge(se, n)

    def setup_consts(self, values):
        self.const_vals = list(values)
        self.const_sb = self.sbuf("const_sb", [128, max(1, len(values))], F32)
        for j, v in enumerate(self.const_vals):
            self.nc.const_aps.register(F32, float(v), self.const_sb[:, j:j + 1])

    def emit_const_init(self, v):
        for j, val in enumerate(self.const_vals):
            self.nc.vector.memset(self.const_sb[:, j:j + 1], float(val)) if False else \
                v.memset(self.const_sb[:, j:j + 1], float(val))


def barrier(p, block, tensor_done=None):
    p.nbar += 1
    tgt = 2 * p.nbar
    s = p.sem("bar")

    @block.scalar
    def _(e):
        if tensor_done is not None:
            e.wait_ge(p.sems[tensor_done[0]], tensor_done[1])
        e.nop().then_inc(s, 1)
        e.wait_ge(s, tgt)

    @block.vector
    def _(e):
        if tensor_done is not None:
            e.wait_ge(p.sems[tensor_done[0]], tensor_done[1])
        e.nop().then_inc(s, 1)
        e.wait_ge(s, tgt)

    @block.tensor
    def _(e):
        e.wait_ge(s, tgt)


def emit_ln_router(p, block, G, pass_idx, with_router, sub=99):
    nc = p.nc
    s_vln = p.sem("vln"); s_lnf = p.sem("dlnf"); s_var = p.sem("vvar"); s_rstd = p.sem("srstd")
    b_vln = p.cnt["vln"]; b_lnf = p.cnt["dlnf"]


    xr = G["x_res"]
    ln_tok = G["ln_tok"]
    ln_f = G["ln_f"]
    st = G["stats"]
    mean_all = G["mean_all"]; var_all = G["var_all"]
    rstd_all = G["rstd_all"]; nmr_all = G["nmr_all"]
    sq = G["sq"]
    s1_all = G["s1_all"]; s2_all = G["s2_all"]

    @block.vector
    def _(v):
        for tt in range(NTT):
            xt = xr[:, tt * D:(tt + 1) * D]
            v.tensor_tensor(sq[:], xt, xt, ALU.mult)
            v.reduce_sum(s1_all[:, tt:tt + 1], xt, AX)
            v.reduce_sum(s2_all[:, tt:tt + 1], sq[:], AX)
        v.tensor_scalar(mean_all[:], s1_all[:], 1.0 / D, None, ALU.mult)
        v.tensor_scalar(s2_all[:], s2_all[:], 1.0 / D, None, ALU.mult)
        p.edge(v.tensor_tensor(st[:, 0:16], mean_all[:], mean_all[:], ALU.mult), v)
        v.tensor_tensor(var_all[:], s2_all[:], st[:, 0:16], ALU.subtract).then_inc(s_var, 1)
    p.bump("vvar")

    @block.scalar
    def _(s):
        s.wait_ge(s_var, p.cnt["vvar"])
        p.edge(s.activation(rstd_all[:], var_all[:], AF.Ln, bias=1e-5), s)
        s.activation(rstd_all[:], rstd_all[:], AF.Exp, scale=-0.5).then_inc(s_rstd, 1)
    p.bump("srstd")

    lnd = G["lnd"]
    s_lnst = p.sem("dlnst")
    b_lnst = p.cnt["dlnst"]

    @block.vector
    def _(v):
        v.wait_ge(s_rstd, p.cnt["srstd"])
        p.edge(v.tensor_tensor(nmr_all[:], mean_all[:], rstd_all[:], ALU.mult), v)
        p.edge(v.tensor_scalar(nmr_all[:], nmr_all[:], -1.0, None, ALU.mult), v)
        for tt in range(NTT):
            if tt >= 2:
                v.wait_ge(s_lnst, b_lnst + 16 * (tt - 1))
            xt = xr[:, tt * D:(tt + 1) * D]
            v.tensor_scalar(ln_tok[:, (tt % 2) * D:(tt % 2 + 1) * D], xt,
                            rstd_all[:, tt:tt + 1], nmr_all[:, tt:tt + 1],
                            ALU.mult, ALU.add).then_inc(s_vln, 1)
    p.bump("vln", NTT)

    @block.sync
    def _(s):
        for tt in range(NTT):
            s.wait_ge(s_vln, b_vln + tt + 1)
            s.dma_start(out=lnd[tt * 128:(tt + 1) * 128, :],
                        in_=ln_tok[:, (tt % 2) * D:(tt % 2 + 1) * D]).then_inc(s_lnst, 16)
        for dt in range(DT):
            s.dma_start(out=ln_f[:, dt * T:(dt + 1) * T],
                        in_=lnd[:, dt * 128:(dt + 1) * 128],
                        transpose=True).then_inc(s_lnf, 16)
    p.bump("dlnst", 16 * NTT)
    p.bump("dlnf", 16 * DT)

    if not with_router:
        return

    s_lg = p.sem("tlg"); s_nm = p.sem("vnm"); s_exp = p.sem("sexp"); s_cmb = p.sem("vcmb")
    b_lg = p.cnt["tlg"]; b_nm = p.cnt["vnm"]; b_exp = p.cnt["sexp"]; b_cmb = p.cnt["vcmb"]

    ps_lg = G["ps_lg"]
    rw = G["rw_bf"]; rb = G["rb_bc"]
    cwall = G["cwall"]; cw2all = G["cw2all"]
    pw = G["comb_w"]; pw2 = G["comb_w2"]; c_m = G["c_m"]
    comb_half = G["comb_half"]

    @block.tensor
    def _(t):
        t.wait_ge(s_lnf, b_lnf + 16 * DT)
        for tt in range(NTT):
            if tt >= 2:
                t.wait_ge(s_nm, b_nm + tt - 1)
            ps = ps_lg[tt % 2]
            for kt in range(DT):
                mi = t.matmul(ps[:, :], ln_f[:, kt * T + tt * 128: kt * T + (tt + 1) * 128],
                              rw[:, kt * 8:(kt + 1) * 8], start=(kt == 0), stop=(kt == DT - 1))
            mi.then_inc(s_lg, 1)
    p.bump("tlg", NTT)

    @block.vector
    def _(v):
        for tt in range(NTT):
            v.wait_ge(s_lg, b_lg + tt + 1)
            ps = ps_lg[tt % 2]
            v.tensor_tensor(cwall[:, tt * 8:(tt + 1) * 8], ps[:, :], rb[:],
                            ALU.add).then_inc(s_nm, 1)
    p.bump("vnm", NTT)

    @block.scalar
    def _(s):
        s.wait_ge(s_nm, b_nm + NTT)
        s.activation(cw2all[:], cwall[:], AF.Exp).then_inc(s_exp, 1)
    p.bump("sexp", 1)

    @block.vector
    def _(v):
        v.wait_ge(s_exp, b_exp + 1)
        for tt in range(NTT):
            E = lambda inst: p.edge(inst, v)
            ce = cw2all[:, tt * 8:(tt + 1) * 8]
            E(v.reduce_sum(c_m[:, 3:4], ce, AX))
            E(v.reciprocal(c_m[:, 3:4], c_m[:, 3:4]))
            E(v.tensor_scalar(pw[:], ce, c_m[:, 3:4], None, ALU.mult))
            E(v.reduce_max(c_m[:, 0:1], pw[:], AX))
            E(v.tensor_scalar(pw2[:], pw[:], c_m[:, 0:1], None, ALU.is_equal))
            E(v.tensor_scalar(pw2[:], pw2[:], -1.0, 1.0, ALU.mult, ALU.add))
            E(v.tensor_tensor(pw2[:], pw[:], pw2[:], ALU.mult))
            E(v.reduce_max(c_m[:, 2:3], pw2[:], AX))
            E(v.tensor_scalar(pw2[:], pw[:], c_m[:, 2:3], None, ALU.is_ge))
            E(v.tensor_tensor(pw[:], pw[:], pw2[:], ALU.mult))
            E(v.reduce_sum(c_m[:, 3:4], pw[:], AX))
            E(v.reciprocal(c_m[:, 3:4], c_m[:, 3:4]))
            v.tensor_scalar(comb_half[:, tt * 9 + 1:(tt + 1) * 9], pw[:],
                            c_m[:, 3:4], 0.5, ALU.mult, ALU.mult)
            v.memset(comb_half[:, tt * 9:tt * 9 + 1], 0.5)
            v.engine_nop().then_inc(s_cmb, 1)
    p.bump("vcmb", NTT)


def emit_ffn_experts(p, block, G, pass_idx):
    nc = p.nc
    s_w1g = [p.sem("dw1g0"), p.sem("dw1g1")]; s_w2g = [p.sem("dw2g0"), p.sem("dw2g1")]
    s_w1s = p.sem("dw1s"); s_w2s = p.sem("dw2s")
    s_cache = p.sem("dcache")
    s_mmH = p.sem("tmmH"); s_gel = p.sem("sgel"); s_mm2 = p.sem("tmm2"); s_ev = p.sem("vev")
    s_lnf = p.sem("dlnf")
    b_w1g = [p.cnt["dw1g0"], p.cnt["dw1g1"]]; b_w2g = [p.cnt["dw2g0"], p.cnt["dw2g1"]]
    b_w1s = p.cnt["dw1s"]; b_w2s = p.cnt["dw2s"]
    b_mmH = p.cnt["tmmH"]; b_gel = p.cnt["sgel"]; b_mm2 = p.cnt["tmm2"]; b_ev = p.cnt["vev"]
    b_cache = p.cnt["dcache"]
    b_lnf = p.cnt["dlnf"] - 16 * DT

    xr = G["x_res"]; ln_f = G["ln_f"]; pool_A = G["pool_A"]; pool_B = G["pool_B"]
    h1q = G["h1q"]; b1col = G["b1col"]
    comb_half = G["comb_half"]; ones_bf = G["ones_bf"]; b2row_bf = G["b2row_bf"]
    ps_y = G["ps_y"]; ps_h1 = G["ps_h1"]
    w1T = G["in_w1T"]; w2T = G["in_w2T"]; w1c = G["w1c"]; w2c = G["w2c"]

    QL = [(w, e, q) for w in range(NW) for e in range(NE) for q in range(NQ)]
    NQT = len(QL)
    cached = pass_idx > 0
    n_w0 = NE * NQ if not cached else 0
    def w1_ready(i):
        if not cached and i < n_w0:
            return s_w1g[i % 2], b_w1g[i % 2] + 16 * (i // 2 + 1)
        off = i - n_w0 if not cached else i
        return s_w1s, b_w1s + 16 * (off + 1)

    def w2_ready(i):
        if not cached and i < n_w0:
            return s_w2g[i % 2], b_w2g[i % 2] + 16 * (i // 2 + 1)
        off = i - n_w0 if not cached else i
        return s_w2s, b_w2s + 16 * (off + 1)

    tts_of_w = [wd // 128 for (_, wd) in WINDOWS]
    ev_before = {}
    acc = 0
    for w in range(NW):
        for e in range(NE):
            ev_before[(w, e)] = acc
            acc += tts_of_w[w]
    ev_total = acc

    def wslc(i):
        return (i % 2) * 8192

    @block.gpsimd
    def _(g):
        if cached:
            return
        for i, (w, e, q) in enumerate(QL):
            if w != 0:
                continue
            if i >= 2:
                g.wait_ge(s_mmH, b_mmH + 8 * (i - 1))
                g.wait_ge(s_mm2, b_mm2 + i - 1)
                g.wait_ge(s_cache, b_cache + 16 * 2 * (i - 1))
            src1 = w1T[e].rearrange("(kt p) f -> p kt f", p=128)[:, :, q * FQ:(q + 1) * FQ]
            dst1 = pool_A[:, wslc(i):wslc(i) + 8192].rearrange("p (kt f) -> p kt f", f=FQ)
            g.dma_start(out=dst1, in_=src1).then_inc(s_w1g[i % 2], 16)
            src2 = w2T[e, q * FQ:(q + 1) * FQ, :].rearrange("(kt p) d -> p kt d", p=128)
            dst2 = pool_B[:, wslc(i):wslc(i) + 8192].rearrange("p (kt d) -> p kt d", d=D)
            g.dma_start(out=dst2, in_=src2).then_inc(s_w2g[i % 2], 16)
    if not cached:
        p.bump("dw1g0", 16 * ((n_w0 + 1) // 2)); p.bump("dw1g1", 16 * (n_w0 // 2))
        p.bump("dw2g0", 16 * ((n_w0 + 1) // 2)); p.bump("dw2g1", 16 * (n_w0 // 2))

    @block.sync
    def _(s):
        if not cached:
            for e in range(NE):
                for q in range(NQ):
                    i = e * NQ + q
                    s.wait_ge(s_w1g[i % 2], b_w1g[i % 2] + 16 * (i // 2 + 1))
                    s.wait_ge(s_w2g[i % 2], b_w2g[i % 2] + 16 * (i // 2 + 1))
                    s.dma_start(out=w1c[e, q], in_=pool_A[:, wslc(i):wslc(i) + 8192]).then_inc(s_cache, 16)
                    s.dma_start(out=w2c[e, q], in_=pool_B[:, wslc(i):wslc(i) + 8192]).then_inc(s_cache, 16)
        for i, (w, e, q) in enumerate(QL):
            if (not cached) and w == 0:
                continue
            if i >= 2:
                s.wait_ge(s_mmH, b_mmH + 8 * (i - 1))
                s.wait_ge(s_mm2, b_mm2 + i - 1)
            s.dma_start(out=pool_A[:, wslc(i):wslc(i) + 8192], in_=w1c[e, q]).then_inc(s_w1s, 16)
            s.dma_start(out=pool_B[:, wslc(i):wslc(i) + 8192], in_=w2c[e, q]).then_inc(s_w2s, 16)
    if not cached:
        p.bump("dcache", 16 * 2 * NE * NQ)
        p.bump("dw1s", 16 * (NQT - n_w0))
        p.bump("dw2s", 16 * (NQT - n_w0))
    else:
        p.bump("dw1s", 16 * NQT)
        p.bump("dw2s", 16 * NQT)

    def lnf_tiles_needed(w):
        w0, wd = WINDOWS[w]
        return (w0 + wd + 127) // 128

    def emit_mm1(t, i, w, e, q):
        w0, wd = WINDOWS[w]
        sw, tw = w1_ready(i)
        t.wait_ge(sw, tw)
        t.wait_ge(s_lnf, b_lnf + 16 * DT)
        for ft in range(FT):
            gidx = 8 * i + ft
            if gidx >= 2:
                t.wait_ge(s_gel, b_gel + gidx - 1)
            ps = ps_h1[gidx % 2]
            base = wslc(i)
            for kt in range(DT):
                mi = t.matmul(ps[:, :wd],
                              pool_A[:, base + kt * FQ + ft * 128: base + kt * FQ + (ft + 1) * 128],
                              ln_f[:, kt * T + w0: kt * T + w0 + wd],
                              start=(kt == 0), stop=(kt == DT - 1))
            mi.then_inc(s_mmH, 1)

    def emit_mm2(t, i, w, e, q):
        w0, wd = WINDOWS[w]
        ntt = wd // 128
        sw, tw = w2_ready(i)
        t.wait_ge(sw, tw)
        t.wait_ge(s_gel, b_gel + 8 * (i + 1))
        if q == 0:
            t.wait_ge(s_ev, b_ev + ev_before[(w, e)])
        base = wslc(i)
        for ttl in range(ntt):
            tt = w0 // 128 + ttl
            psy = ps_y[ttl]
            for dh in range(2):
                first = (q == 0)
                if q == 0:
                    t.matmul(psy[:, dh * 512:(dh + 1) * 512],
                             ones_bf[0:1, 0:128],
                             b2row_bf[0:1, e * D + dh * 512: e * D + (dh + 1) * 512],
                             start=True, stop=False)
                    first = False
                for kf in range(FT):
                    mi = t.matmul(psy[:, dh * 512:(dh + 1) * 512],
                                  h1q[:, (i % 2) * 3072 + kf * 384 + ttl * 128:
                                      (i % 2) * 3072 + kf * 384 + ttl * 128 + 128],
                                  pool_B[:, base + kf * D + dh * 512: base + kf * D + (dh + 1) * 512],
                                  start=(first and kf == 0), stop=(q == NQ - 1 and kf == FT - 1))
        mi.then_inc(s_mm2, 1)

    @block.tensor
    def _(t):
        for i, (w, e, q) in enumerate(QL):
            emit_mm1(t, i, w, e, q)
            if i >= 1:
                emit_mm2(t, i - 1, *QL[i - 1])
        emit_mm2(t, NQT - 1, *QL[NQT - 1])
    p.bump("tmmH", 8 * NQT)
    p.bump("tmm2", NQT)

    @block.scalar
    def _(s):
        for i, (w, e, q) in enumerate(QL):
            w0, wd = WINDOWS[w]
            for ft in range(FT):
                gidx = 8 * i + ft
                s.wait_ge(s_mmH, b_mmH + gidx + 1)
                if i >= 2 and ft == 0:
                    s.wait_ge(s_mm2, b_mm2 + i - 1)
                ftg = q * FT + ft
                s.activation(h1q[:, (i % 2) * 3072 + ft * 384: (i % 2) * 3072 + ft * 384 + wd],
                             ps_h1[gidx % 2][:, :wd], AF.Gelu_apprx_tanh,
                             bias=b1col[:, ftg * 9 + e: ftg * 9 + e + 1]).then_inc(s_gel, 1)
    p.bump("sgel", 8 * NQT)

    @block.vector
    def _(v):
        for i, (w, e, q) in enumerate(QL):
            if q != NQ - 1:
                continue
            w0, wd = WINDOWS[w]
            ntt = wd // 128
            v.wait_ge(s_mm2, b_mm2 + i + 1)
            for ttl in range(ntt):
                tt = w0 // 128 + ttl
                xt = xr[:, tt * D:(tt + 1) * D]
                v.scalar_tensor_tensor(xt, ps_y[ttl][:, :],
                                       comb_half[:, tt * 9 + e: tt * 9 + e + 1],
                                       xt, ALU.mult, ALU.add).then_inc(s_ev, 1)
    p.bump("vev", ev_total)


def mbar(p, block, tensor_done=None, dma_waits=()):
    if not hasattr(p, "mcnt"):
        p.mcnt = 0
    p.mcnt += 3
    tgt = p.mcnt
    s = p.sem("mbar")

    def mk(e):
        if tensor_done is not None:
            e.wait_ge(p.sems[tensor_done[0]], tensor_done[1])
        for nm, c in dma_waits:
            e.wait_ge(p.sems[nm], c)
        e.nop().then_inc(s, 1)
        e.wait_ge(s, tgt)

    @block.scalar
    def _(e):
        mk(e)

    @block.vector
    def _(e):
        mk(e)

    @block.gpsimd
    def _(e):
        mk(e)

    @block.tensor
    def _(e):
        e.wait_ge(s, tgt)

    @block.sync
    def _(e):
        e.wait_ge(s, tgt)


NDT = 8
NST = 16


def emit_m_proj(p, block, G, parts=("w", "mm", "pay")):
    nc = p.nc
    s_lnf = p.sem("dlnf")
    s_mw = p.sem("dmw")
    s_mm = p.sem("txz"); s_ev = p.sem("sxe"); s_st = p.sem("dxs"); s_py = p.sem("dpy")
    s_sp = p.sem("dsp"); s_vln = p.sem("vln")
    b_lnf = p.cnt["dlnf"] - 16 * DT
    b_mm = p.cnt["txz"]; b_ev = p.cnt["sxe"]; b_st = p.cnt["dxs"]

    win = G["w_in_sb"]; lnf = G["ln_f"]; stg = G["xz_stg"]; pay = G["pay_sb"]
    ps = G["ps_xz"]

    @block.gpsimd
    def _(g):
        g.wait_ge(p.sems["dcache"], p.cnt["dcache"])
        g.dma_start(out=win[:], in_=G["in_winT"]).then_inc(s_mw, 16)
    p.bump("dmw", 16)


    @block.sync
    def _(s):
        s.wait_ge(s_vln, p.cnt["vln"])
        s.dma_start(out=G["xres_dram"].ap(), in_=G["x_res"][:]).then_inc(s_sp, 16)
    p.bump("dsp", 16)

    QL = [(it, w) for it in range(16) for w in range(4)]
    if "mm" not in parts:
        return

    @block.tensor
    def _(t):
        t.wait_ge(s_lnf, b_lnf + 16 * DT)
        t.wait_ge(s_mw, p.cnt["dmw"])
        for i, (it, w) in enumerate(QL):
            if i >= 2:
                t.wait_ge(s_ev, b_ev + i - 1)
            pp = ps[i % 2]
            for kt in range(DT):
                mi = t.matmul(pp[:, :512],
                              win[:, kt * 2048 + it * 128: kt * 2048 + (it + 1) * 128],
                              lnf[:, kt * T + w * 512: kt * T + (w + 1) * 512],
                              start=(kt == 0), stop=(kt == DT - 1))
            mi.then_inc(s_mm, 1)
    p.bump("txz", 64)

    s_pc = p.sem("spy")

    @block.scalar
    def _(s):
        for i, (it, w) in enumerate(QL):
            s.wait_ge(s_mm, b_mm + i + 1)
            if i >= 2:
                s.wait_ge(s_st, b_st + 16 * (i - 1))
            if "pay" in parts and w == 0:
                s.copy(out=pay[:, it * 3: it * 3 + 2],
                       in_=ps[i % 2][:, 0:2]).then_inc(s_pc, 1)
            if "pay" in parts and w == 3:
                s.copy(out=pay[:, it * 3 + 2: it * 3 + 3],
                       in_=ps[i % 2][:, 511:512]).then_inc(s_pc, 1)
            s.copy(out=stg[i % 2][:], in_=ps[i % 2][:, :512]).then_inc(s_ev, 1)
    p.bump("sxe", 64)
    if "pay" in parts:
        p.bump("spy", 32)

    @block.sync
    def _(s):
        for i, (it, w) in enumerate(QL):
            s.wait_ge(s_ev, b_ev + i + 1)
            s.dma_start(out=G["xz_dram"].ap()[:, it * 2048 + w * 512: it * 2048 + (w + 1) * 512],
                        in_=stg[i % 2][:]).then_inc(s_st, 16)
    p.bump("dxs", 16 * 64)

    if "pay" not in parts:
        return

    @block.sync
    def _(s):
        s.wait_ge(s_pc, p.cnt["spy"])
        s.dma_start(out=G["cc1_in"].ap(), in_=pay[:]).then_inc(s_py, 16)
    p.bump("dpy", 16)


def emit_m_halo(p, block, G):
    s_py = p.sem("dpy"); s_cc = p.sem("cc1"); s_g1 = p.sem("dg1"); s_hl = p.sem("vhl")

    @block.gpsimd
    def _(g):
        g.wait_ge(s_py, p.cnt["dpy"])
        g.collective_compute(
            "AllGather", mybir.AluOpType.bypass,
            replica_groups=[list(range(8))],
            ins=[G["cc1_in"].ap().opt()],
            outs=[G["cc1_out"].ap().opt()],
        ).then_inc(s_cc, 1)
        g.wait_ge(s_cc, p.cnt["cc1"] + 1)
        g.dma_start(out=G["gath1_sb"][:].rearrange("p (r c) -> p r c", c=48),
                    in_=G["cc1_out"].ap().rearrange("(r p) c -> p r c", p=128)).then_inc(s_g1, 16)
    p.bump("cc1", 1)
    p.bump("dg1", 16)

    @block.vector
    def _(v):
        v.wait_ge(s_g1, p.cnt["dg1"])
        hL = G["haloL"]; hR = G["haloR"]; gg = G["gath1_sb"]; m2 = G["msel2_sb"]
        p.edge(v.memset(hL[:], 0.0), v)
        p.edge(v.memset(hR[:], 0.0), v)
        for r in range(8):
            p.edge(v.scalar_tensor_tensor(hL[:], gg[:, r * 48:(r + 1) * 48],
                                          m2[:, r:r + 1], hL[:],
                                          ALU.mult, ALU.add), v)
            p.edge(v.scalar_tensor_tensor(hR[:], gg[:, r * 48:(r + 1) * 48],
                                          m2[:, 8 + r:9 + r], hR[:],
                                          ALU.mult, ALU.add), v)
        v.engine_nop().then_inc(s_hl, 1)
    p.bump("vhl", 1)


def emit_m_conv(p, block, G):
    s_xl = p.sem("dxl"); s_hc = p.sem("ghl"); s_cv = p.sem("vcv")
    s_sl = p.sem("ssl"); s_zs = p.sem("dzs"); s_hl = p.sem("vhl")
    b_xl = p.cnt["dxl"]; b_hc = p.cnt["ghl"]; b_cv = p.cnt["vcv"]
    b_sl = p.cnt["ssl"]; b_zs = p.cnt["dzs"]
    xpre = G["xpre"]; cva = G["cv_acc"]; zst = G["z_stg"]
    cc = G["conv_c_sb"]; cb = G["conv_b_sb"]

    @block.sync
    def _(s):
        for it in range(16):
            if it >= 2:
                s.wait_ge(s_cv, b_cv + it - 1)
            s.dma_start(out=xpre[it % 2][:, 1:2049],
                        in_=G["xz_dram"].ap()[:, it * 2048:(it + 1) * 2048]).then_inc(s_xl, 16)
    p.bump("dxl", 16 * 16)

    @block.gpsimd
    def _(g):
        g.wait_ge(s_hl, p.cnt["vhl"])
        for it in range(16):
            g.wait_ge(s_xl, b_xl + 16 * (it + 1))
            if it >= 2:
                g.wait_ge(s_cv, b_cv + it - 1)
            g.tensor_copy(out=xpre[it % 2][:, 0:1], in_=G["haloL"][:, it * 3 + 2: it * 3 + 3])
            g.tensor_copy(out=xpre[it % 2][:, 2049:2051],
                          in_=G["haloR"][:, it * 3: it * 3 + 2]).then_inc(s_hc, 1)
    p.bump("ghl", 16)

    @block.vector
    def _(v):
        v.memset(G["zeros_bf"][:], 0.0)
        for it in range(16):
            v.wait_ge(s_xl, b_xl + 16 * (it + 1))
            v.wait_ge(s_hc, b_hc + it + 1)
            if it >= 2:
                v.wait_ge(s_sl, b_sl + it - 1)
            x = xpre[it % 2]
            a = cva[it % 2]
            v.tensor_scalar(a[:], x[:, 0:2048], cc[:, it * 4: it * 4 + 1], None, ALU.mult)
            for j in range(1, 4):
                v.scalar_tensor_tensor(a[:], x[:, j:j + 2048], cc[:, it * 4 + j: it * 4 + j + 1],
                                       a[:], ALU.mult, ALU.add)
            v.engine_nop().then_inc(s_cv, 1)
    p.bump("vcv", 16)

    @block.scalar
    def _(s):
        for it in range(16):
            s.wait_ge(s_cv, b_cv + it + 1)
            if it < 8:
                s.activation(out=G["u_bf"][:, it * T:(it + 1) * T], in_=cva[it % 2][:],
                             func=AF.Silu, bias=cb[:, it:it + 1]).then_inc(s_sl, 1)
            else:
                if it >= 12:
                    s.wait_ge(s_zs, b_zs + 16 * (it - 11))
                s.activation(out=zst[it % 4][:], in_=cva[it % 2][:],
                             func=AF.Silu, bias=cb[:, it:it + 1]).then_inc(s_sl, 1)
    p.bump("ssl", 16)

    @block.sync
    def _(s):
        for dt in range(8):
            it = 8 + dt
            s.wait_ge(s_sl, b_sl + it + 1)
            s.dma_start(out=G["z_dram"].ap()[:, dt * T:(dt + 1) * T],
                        in_=zst[it % 4][:]).then_inc(s_zs, 16)
    p.bump("dzs", 16 * 8)


def emit_m_xdbl(p, block, G):
    nc = p.nc
    s_sl = p.sem("ssl")
    s_x2 = p.sem("tx2"); s_xe = p.sem("sxd")
    s_dt = p.sem("tdt"); s_sab = p.sem("ssp"); s_vsp = p.sem("vsp")
    s_dd = p.sem("vdd"); s_dm = p.sem("ddm"); s_gt = p.sem("gdt"); s_pp = p.sem("spp")
    b_sl = p.cnt["ssl"] - 16
    b_x2 = p.cnt["tx2"]; b_xe = p.cnt["sxd"]; b_dt = p.cnt["tdt"]
    b_sab = p.cnt["ssp"]; b_vsp = p.cnt["vsp"]; b_dd = p.cnt["vdd"]
    b_dm = p.cnt["ddm"]; b_gt = p.cnt["gdt"]

    psx = G["ps_xd"]; psd = G["ps_dt"]
    dtl = G["dtlow_bf"]; bc = G["bc_bf"]
    u = G["u_bf"]; dlt = G["dlt_bf"]
    inv = G["invdt_sb"]; inv2 = G["inv2_sb"]

    s_iv = p.sem("viv")
    @block.vector
    def _(v):
        p.edge(v.tensor_tensor(inv2[:], inv[:], inv[:], ALU.add), v)
        v.engine_nop().then_inc(s_iv, 1)
    p.bump("viv", 1)

    @block.tensor
    def _(t):
        t.wait_ge(s_sl, b_sl + 8)
        t.wait_ge(p.sems["dmw"], p.cnt["dmw"])
        for w in range(4):
            if w >= 2:
                t.wait_ge(s_xe, b_xe + 2 * (w - 1))
            pp = psx[w % 2]
            for kt in range(DT):
                mi = t.matmul(pp[0:96, :512],
                              G["w_xd_sb"][:, kt * 96:(kt + 1) * 96],
                              u[:, kt * T + w * 512: kt * T + (w + 1) * 512],
                              start=(kt == 0), stop=(kt == DT - 1))
            mi.then_inc(s_x2, 1)
    p.bump("tx2", 4)

    @block.scalar
    def _(s):
        for w in range(4):
            s.wait_ge(s_x2, b_x2 + w + 1)
            s.copy(out=dtl[0:64, w * 512:(w + 1) * 512],
                   in_=psx[w % 2][0:64, :512]).then_inc(s_xe, 1)
            s.copy(out=bc[0:32, w * 512:(w + 1) * 512],
                   in_=psx[w % 2][64:96, :512]).then_inc(s_xe, 1)
    p.bump("sxd", 8)

    QL = [(mt, w) for mt in range(8) for w in range(4)]

    @block.tensor
    def _(t):
        t.wait_ge(s_xe, b_xe + 8)
        for i, (mt, w) in enumerate(QL):
            if i >= 2:
                t.wait_ge(s_sab, b_sab + i - 1)
                t.wait_ge(s_vsp, b_vsp + i - 1)
            t.matmul(psd[i % 2][:, :512],
                     G["w_dt_sb"][:, mt * 128:(mt + 1) * 128],
                     dtl[0:64, w * 512:(w + 1) * 512],
                     start=True, stop=True).then_inc(s_dt, 1)
    p.bump("tdt", 32)

    @block.scalar
    def _(s):
        s.wait_ge(s_iv, p.cnt["viv"])
        for i, (mt, w) in enumerate(QL):
            s.wait_ge(s_dt, b_dt + i + 1)
            if i >= 2:
                s.wait_ge(s_vsp, b_vsp + i - 1)
            ab = G["sp_ab"]
            p.edge(s.activation(out=ab[i % 2][:], in_=psd[i % 2][:, :512], func=AF.Abs,
                                bias=inv2[:, mt:mt + 1]), s)
            p.edge(s.activation(out=ab[i % 2][:], in_=ab[i % 2][:], func=AF.Exp, scale=-1.0), s)
            s.activation(out=ab[i % 2][:], in_=ab[i % 2][:], func=AF.Ln,
                         bias=1.0).then_inc(s_sab, 1)
    p.bump("ssp", 32)

    @block.vector
    def _(v):
        for i, (mt, w) in enumerate(QL):
            v.wait_ge(s_dt, b_dt + i + 1)
            rl = G["sp_rl"]
            v.tensor_scalar(rl[:], psd[i % 2][:, :512], inv2[:, mt:mt + 1], 0.0,
                            ALU.add, ALU.max)
            v.wait_ge(s_sab, b_sab + i + 1)
            v.tensor_tensor(dlt[:, mt * T + w * 512: mt * T + (w + 1) * 512],
                            rl[:], G["sp_ab"][i % 2][:], ALU.add).then_inc(s_vsp, 1)
        for mt in range(8):
            if mt >= 2:
                v.wait_ge(s_dm, b_dm + 16 * (mt - 1))
                v.wait_ge(s_gt, b_gt + mt - 1)
            v.tensor_tensor_scan(out=G["dl_stg"][mt % 2][:],
                                 data0=G["zeros_bf"][:],
                                 data1=dlt[:, mt * T:(mt + 1) * T],
                                 initial=0.0, op0=ALU.add, op1=ALU.add).then_inc(s_dd, 1)
    p.bump("vsp", 32)
    p.bump("vdd", 8)

    @block.gpsimd
    def _(g):
        for mt in range(8):
            g.wait_ge(s_dd, b_dd + mt + 1)
            g.tensor_copy(out=G["dTot_col"][:, mt:mt + 1],
                          in_=G["dl_stg"][mt % 2][:, 2047:2048]).then_inc(s_gt, 1)
    p.bump("gdt", 8)

    @block.sync
    def _(s):
        for mt in range(8):
            s.wait_ge(s_dd, b_dd + mt + 1)
            s.dma_start(out=G["dlt_dram"].ap()[:, mt * T:(mt + 1) * T],
                        in_=G["dl_stg"][mt % 2][:]).then_inc(s_dm, 16)
    p.bump("ddm", 16 * 8)

    @block.scalar
    def _(s):
        for mt in range(8):
            s.wait_ge(s_gt, b_gt + mt + 1)
            for n in range(1, NST + 1):
                mi = s.activation(out=G["p_all"][:, mt * 16 + n - 1: mt * 16 + n],
                                  in_=G["dTot_col"][:, mt:mt + 1], func=AF.Exp,
                                  scale=-float(n))
        mi.then_inc(s_pp, 1)
    p.bump("spp", 1)


def emit_m_scan(p, block, G):
    nc = p.nc
    s_bp = p.sem("tbp"); s_cp = p.sem("tcp"); s_da = p.sem("sda"); s_ce = p.sem("sce")
    s_du = p.sem("vdu"); s_sc = p.sem("vsc"); s_hc = p.sem("ghc"); s_ye = p.sem("sye")
    b_bp = p.cnt["tbp"]; b_cp = p.cnt["tcp"]; b_da = p.cnt["sda"]; b_ce = p.cnt["sce"]
    b_du = p.cnt["vdu"]; b_sc = p.cnt["vsc"]; b_hc = p.cnt["ghc"]; b_ye = p.cnt["sye"]

    psB = G["ps_B"]; psC = G["ps_C"]
    sel = G["sel_sb"]; bc = G["bc_bf"]; dlt = G["dlt_bf"]; u = G["u_bf"]
    dA = G["dA_bf"]; dBu = G["dBu_bf"]; hh = G["h_bf"]; hC = G["hC_bf"]
    du = G["du_f"]; yd = G["y_dt"]; yl = G["y_locbf"]
    QL = [(dt, n) for dt in range(8) for n in range(1, NST + 1)]

    @block.tensor
    def _(t):
        for i, (dt, n) in enumerate(QL):
            if i >= 1:
                t.wait_ge(s_du, b_du + i)
            for c in range(4):
                mi = t.matmul(psB[:, c * 512:(c + 1) * 512],
                              sel[:, (n - 1) * 128: n * 128],
                              bc[0:32, c * 512:(c + 1) * 512],
                              start=True, stop=True)
            mi.then_inc(s_bp, 1)
            if i >= 1:
                t.wait_ge(s_hc, b_hc + i)
            for c in range(4):
                mi = t.matmul(psC[:, c * 512:(c + 1) * 512],
                              sel[:, (16 + n - 1) * 128: (16 + n) * 128],
                              bc[0:32, c * 512:(c + 1) * 512],
                              start=True, stop=True)
            mi.then_inc(s_cp, 1)
    p.bump("tbp", 128)
    p.bump("tcp", 128)

    @block.scalar
    def _(s):
        for i, (dt, n) in enumerate(QL):
            if i >= 2:
                s.wait_ge(s_sc, b_sc + i - 1)
            s.activation(out=dA[i % 2][:], in_=dlt[:, dt * T:(dt + 1) * T],
                         func=AF.Exp, scale=-float(n)).then_inc(s_da, 1)
            s.wait_ge(s_cp, b_cp + i + 1)
            if i >= 2:
                s.wait_ge(s_hc, b_hc + i - 1)
            s.copy(out=G["Cbc_bf"][i % 2][:], in_=psC[:]).then_inc(s_ce, 1)
            if n == NST:
                s.wait_ge(s_hc, b_hc + i + 1)
                s.copy(out=yl[:, dt * T:(dt + 1) * T], in_=yd[:]).then_inc(s_ye, 1)
    p.bump("sda", 128)
    p.bump("sce", 128)
    p.bump("sye", 8)

    @block.vector
    def _(v):
        for i, (dt, n) in enumerate(QL):
            if n == 1:
                v.tensor_tensor(du[:], dlt[:, dt * T:(dt + 1) * T],
                                u[:, dt * T:(dt + 1) * T], ALU.mult)
            v.wait_ge(s_bp, b_bp + i + 1)
            v.tensor_tensor(dBu[i % 2][:], psB[:], du[:], ALU.mult).then_inc(s_du, 1)
            v.wait_ge(s_da, b_da + i + 1)
            if i >= 2:
                v.wait_ge(s_hc, b_hc + i - 1)
            v.tensor_tensor_scan(out=hh[i % 2][:], data0=dA[i % 2][:], data1=dBu[i % 2][:],
                                 initial=0.0, op0=ALU.mult, op1=ALU.add).then_inc(s_sc, 1)
    p.bump("vdu", 128)
    p.bump("vsc", 128)

    @block.gpsimd
    def _(g):
        for i, (dt, n) in enumerate(QL):
            g.wait_ge(s_sc, b_sc + i + 1)
            g.wait_ge(s_ce, b_ce + i + 1)
            g.tensor_copy(out=G["s_all"][:, i:i + 1], in_=hh[i % 2][:, 2047:2048])
            g.tensor_tensor(hC[:], hh[i % 2][:], G["Cbc_bf"][i % 2][:], ALU.mult)
            if n == 1:
                g.wait_ge(s_ye, b_ye + dt)
                g.tensor_copy(out=yd[:], in_=hC[:]).then_inc(s_hc, 1)
            else:
                g.tensor_tensor(yd[:], yd[:], hC[:], ALU.add).then_inc(s_hc, 1)
    p.bump("ghc", 128)


def emit_m_exchange(p, block, G):
    s_hc = p.sem("ghc"); s_pp = p.sem("spp"); s_sf = p.sem("gsf")
    s_ps = p.sem("dps"); s_cc = p.sem("cc2"); s_g2 = p.sem("dg2"); s_h0 = p.sem("vh0")

    @block.gpsimd
    def _(g):
        g.wait_ge(s_hc, p.cnt["ghc"])
        g.tensor_copy(out=G["s_fin"][:], in_=G["s_all"][:]).then_inc(s_sf, 1)
    p.bump("gsf", 1)

    @block.sync
    def _(s):
        s.wait_ge(s_sf, p.cnt["gsf"])
        s.wait_ge(s_pp, p.cnt["spp"])
        s.dma_start(out=G["cc2_in"].ap()[:, 0:128], in_=G["s_fin"][:]).then_inc(s_ps, 16)
        s.dma_start(out=G["cc2_in"].ap()[:, 128:256], in_=G["p_all"][:]).then_inc(s_ps, 16)
    p.bump("dps", 32)

    @block.gpsimd
    def _(g):
        g.wait_ge(s_ps, p.cnt["dps"])
        g.collective_compute(
            "AllGather", mybir.AluOpType.bypass,
            replica_groups=[list(range(8))],
            ins=[G["cc2_in"].ap().opt()],
            outs=[G["cc2_out"].ap().opt()],
        ).then_inc(s_cc, 1)
        g.wait_ge(s_cc, p.cnt["cc2"] + 1)
        g.dma_start(out=G["gath2_sb"][:].rearrange("p (r c) -> p r c", c=256),
                    in_=G["cc2_out"].ap().rearrange("(r p) c -> p r c", p=128)).then_inc(s_g2, 16)
    p.bump("cc2", 1)
    p.bump("dg2", 16)

    @block.vector
    def _(v):
        v.wait_ge(s_g2, p.cnt["dg2"])
        h0 = G["h0_all"]; gg = G["gath2_sb"]; ms = G["msel_sb"]
        Ps = G["psel"]; Ss = G["ssel"]
        p.edge(v.memset(h0[:], 0.0), v)
        for j in (2, 1, 0):
            p.edge(v.memset(Ps[:], 0.0), v)
            p.edge(v.memset(Ss[:], 0.0), v)
            for r in range(8):
                p.edge(v.scalar_tensor_tensor(Ps[:], gg[:, r * 256 + 128: r * 256 + 256],
                                              ms[:, j * 8 + r: j * 8 + r + 1], Ps[:],
                                              ALU.mult, ALU.add), v)
                p.edge(v.scalar_tensor_tensor(Ss[:], gg[:, r * 256: r * 256 + 128],
                                              ms[:, j * 8 + r: j * 8 + r + 1], Ss[:],
                                              ALU.mult, ALU.add), v)
            p.edge(v.tensor_tensor(h0[:], h0[:], Ps[:], ALU.mult), v)
            p.edge(v.tensor_tensor(h0[:], h0[:], Ss[:], ALU.add), v)
        v.engine_nop().then_inc(s_h0, 1)
    p.bump("vh0", 1)


def emit_m_corr(p, block, G):
    nc = p.nc
    s_h0 = p.sem("vh0")
    s_dl = p.sem("ddl"); s_wn = p.sem("swn"); s_c2 = p.sem("tc2")
    s_ac = p.sem("vac"); s_fn = p.sem("gfn"); s_yw = p.sem("dyw")
    b_dl = p.cnt["ddl"]; b_wn = p.cnt["swn"]; b_c2 = p.cnt["tc2"]
    b_ac = p.cnt["vac"]; b_fn = p.cnt["gfn"]

    psC = G["ps_c2"]; sel = G["sel_sb"]; bc = G["bc_bf"]
    dl = G["dl_t"]; wn = G["wn_bf"]; acc = G["acc_f"]; t1 = G["t1_bf"]
    h0 = G["h0_all"]; u = G["u_bf"]; yl = G["y_locbf"]
    NSEQ = [(dt, n) for dt in range(8) for n in range(1, NST + 1)]

    @block.sync
    def _(s):
        for dt in range(8):
            if dt >= 2:
                s.wait_ge(s_wn, b_wn + 16 * (dt - 1))
            s.dma_start(out=dl[dt % 2][:],
                        in_=G["dlt_dram"].ap()[:, dt * T:(dt + 1) * T]).then_inc(s_dl, 16)
    p.bump("ddl", 16 * 8)

    @block.scalar
    def _(s):
        for i, (dt, n) in enumerate(NSEQ):
            if n == 1:
                s.wait_ge(s_dl, b_dl + 16 * (dt + 1))
            if i >= 2:
                s.wait_ge(s_ac, b_ac + i - 1)
            s.activation(out=wn[i % 2][:], in_=dl[dt % 2][:], func=AF.Exp,
                         scale=-float(n)).then_inc(s_wn, 1)
    p.bump("swn", 128)

    @block.tensor
    def _(t):
        for i, (dt, n) in enumerate(NSEQ):
            if i >= 2:
                t.wait_ge(s_ac, b_ac + i - 1)
            for c in range(4):
                mi = t.matmul(psC[i % 2][:, c * 512:(c + 1) * 512],
                              sel[:, (16 + n - 1) * 128: (16 + n) * 128],
                              bc[0:32, c * 512:(c + 1) * 512],
                              start=True, stop=True)
            mi.then_inc(s_c2, 1)
    p.bump("tc2", 128)

    @block.vector
    def _(v):
        v.wait_ge(s_h0, p.cnt["vh0"])
        for i, (dt, n) in enumerate(NSEQ):
            v.wait_ge(s_c2, b_c2 + i + 1)
            v.wait_ge(s_wn, b_wn + i + 1)
            v.tensor_tensor(t1[:], psC[i % 2][:], wn[i % 2][:], ALU.mult)
            col = h0[:, dt * 16 + n - 1: dt * 16 + n]
            if n == 1:
                v.scalar_tensor_tensor(acc[:], t1[:], col, G["zeros_bf"][:],
                                       ALU.mult, ALU.add).then_inc(s_ac, 1)
            else:
                v.scalar_tensor_tensor(acc[:], t1[:], col, acc[:],
                                       ALU.mult, ALU.add).then_inc(s_ac, 1)
            if n == NST:
                v.scalar_tensor_tensor(G["tmp_f"][:], u[:, dt * T:(dt + 1) * T],
                                       G["d_col_sb"][:, dt:dt + 1], acc[:],
                                       ALU.mult, ALU.add)
                v.tensor_tensor(yl[:, dt * T:(dt + 1) * T], yl[:, dt * T:(dt + 1) * T],
                                G["tmp_f"][:], ALU.add).then_inc(s_fn, 1)
    p.bump("vac", 128)
    p.bump("gfn", 8)

    @block.sync
    def _(s):
        for dt in range(8):
            s.wait_ge(s_fn, b_fn + dt + 1)
            s.dma_start(out=G["y_dram"].ap()[:, dt * T:(dt + 1) * T],
                        in_=yl[:, dt * T:(dt + 1) * T]).then_inc(s_yw, 16)
    p.bump("dyw", 16 * 8)


def emit_m_out(p, block, G):
    nc = p.nc
    s_ld = p.sem("dm8"); s_mw = p.sem("dmw")
    s_om = p.sem("tom"); s_oe = p.sem("voe")
    b_om = p.cnt["tom"]; b_oe = p.cnt["voe"]

    @block.sync
    def _(s):
        s.dma_start(out=G["x_res"][:], in_=G["xres_dram"].ap()).then_inc(s_ld, 16)
        s.dma_start(out=G["y_sb"][:], in_=G["y_dram"].ap()).then_inc(s_ld, 16)
        s.dma_start(out=G["z_sb"][:], in_=G["z_dram"].ap()).then_inc(s_ld, 16)
    p.bump("dm8", 16 * 3)

    @block.gpsimd
    def _(g):
        g.dma_start(out=G["w_out_sb"][:], in_=G["in_woutT"]).then_inc(s_mw, 16)
        g.dma_start(out=G["bout_bf"][:], in_=G["in_bout"]).then_inc(s_mw, 16)
    p.bump("dmw", 16 * 2)

    ps = G["ps_o"]

    @block.tensor
    def _(t):
        t.wait_ge(s_ld, p.cnt["dm8"])
        t.wait_ge(s_mw, p.cnt["dmw"])
        for tt in range(NTT):
            if tt >= 2:
                t.wait_ge(s_oe, b_oe + tt - 1)
            pp = ps[tt % 2]
            for dh in range(2):
                t.matmul(pp[:, dh * 512:(dh + 1) * 512],
                         G["ones_bf"][0:1, 0:128],
                         G["bout_bf"][0:1, dh * 512:(dh + 1) * 512],
                         start=True, stop=False)
                for kt in range(16):
                    src = G["y_sb"] if kt < 8 else G["z_sb"]
                    mi = t.matmul(pp[:, dh * 512:(dh + 1) * 512],
                                  src[:, (kt % 8) * T + tt * 128: (kt % 8) * T + (tt + 1) * 128],
                                  G["w_out_sb"][:, kt * 1024 + dh * 512: kt * 1024 + (dh + 1) * 512],
                                  start=False, stop=(kt == 15))
            mi.then_inc(s_om, 1)
    p.bump("tom", NTT)

    @block.vector
    def _(v):
        v.wait_ge(s_ld, p.cnt["dm8"])
        for tt in range(NTT):
            v.wait_ge(s_om, b_om + tt + 1)
            xt = G["x_res"][:, tt * D:(tt + 1) * D]
            v.tensor_tensor(xt, ps[tt % 2][:, :], xt, ALU.add).then_inc(s_oe, 1)
    p.bump("voe", NTT)


def emit_output_dram(p, block, G):
    s_st = p.sem("dst")

    @block.sync
    def _(s):
        for nm, c in sorted(p.cnt.items()):
            if c > 0 and nm != "dst":
                s.wait_ge(p.sems[nm], c)
        src = G["xres_dram"].ap().rearrange("p (tt d) -> p tt d", d=D)
        dst = G["in_out"].rearrange("(tt p) d -> p tt d", p=128)
        s.dma_start(out=dst, in_=src).then_inc(s_st, 16)
        s.wait_ge(s_st, 16)
    p.bump("dst", 16)


def emit_output(p, block, G):
    s_st = p.sem("dst")
    s_ev = p.sem("vev")
    dma_final = [(nm, c) for nm, c in sorted(p.cnt.items())
                 if nm.startswith("d") and nm != "dst" and c > 0]

    @block.sync
    def _(s):
        s.wait_ge(s_ev, p.cnt["vev"])
        for nm, c in dma_final:
            s.wait_ge(p.sems[nm], c)
        src = G["x_res"][:].rearrange("p (tt d) -> p tt d", d=D)
        dst = G["in_out"].rearrange("(tt p) d -> p tt d", p=128)
        s.dma_start(out=dst, in_=src).then_inc(s_st, 16)
        s.wait_ge(s_st, 16)
    p.bump("dst", 16)


def alloc_ln(p, G, scope, tag):
    G["ln_tok"] = p.sbuf(f"ln_tok{tag}", [128, 2 * D], BF16, scope)
    G["ln_f"] = p.sbuf(f"ln_f{tag}", [128, DT * T], BF16, scope)
    for nm, w in [("stats", 16), ("s1_all", 16), ("s2_all", 16), ("mean_all", 16),
                  ("var_all", 16), ("rstd_all", 16), ("nmr_all", 16)]:
        G[nm] = p.sbuf(f"{nm}{tag}", [128, w], F32, scope)
    G["sq"] = p.sbuf(f"sq{tag}", [128, D], F32, scope)


def alloc_router(p, G, scope, tag):
    G["cwall"] = p.sbuf(f"cwall{tag}", [128, NTT * 8], F32, scope)
    G["cw2all"] = p.sbuf(f"cw2all{tag}", [128, NTT * 8], F32, scope)
    G["comb_w"] = p.sbuf(f"comb_w{tag}", [128, 8], F32, scope)
    G["comb_w2"] = p.sbuf(f"comb_w2{tag}", [128, 8], F32, scope)
    G["c_m"] = p.sbuf(f"c_m{tag}", [128, 4], F32, scope)
    G["comb_half"] = p.sbuf(f"comb_half{tag}", [128, NTT * 9], F32, scope)


def emit_ffn_pass(p, block, G, scope, pass_idx, tag):
    alloc_ln(p, G, scope, tag)
    alloc_router(p, G, scope, tag)
    G["h1q"] = p.sbuf(f"h1q{tag}", [128, 2 * 3072], BF16, scope)
    G["pool_A"] = p.sbuf(f"pool_A{tag}", [128, 2 * 8192], BF16, scope)
    G["pool_B"] = p.sbuf(f"pool_B{tag}", [128, 2 * 8192], BF16, scope)
    with contextlib.ExitStack() as lnscope:
        G["ps_lg"] = [p.psum(f"ps_lg0{tag}", [128, 8], F32, lnscope),
                      p.psum(f"ps_lg1{tag}", [128, 8], F32, lnscope)]
        emit_ln_router(p, block, G, pass_idx, with_router=True)
        barrier(p, block, tensor_done=("tlg", p.cnt["tlg"]))
    with contextlib.ExitStack() as exscope:
        G["ps_y"] = [p.psum(f"ps_y{k}{tag}", [128, 1024], F32, exscope) for k in range(3)]
        G["ps_h1"] = [p.psum(f"ps_h1{k}{tag}", [128, 512], F32, exscope) for k in range(2)]
        emit_ffn_experts(p, block, G, pass_idx)
        barrier(p, block, tensor_done=("tmm2", p.cnt["tmm2"]))


def build(stage=STAGE):
    p = Prog()
    nc = p.nc
    G = {}

    G["in_x"] = nc.dram_tensor("x_sh", [T, D], F32, kind="ExternalInput").ap()
    G["in_w1T"] = nc.dram_tensor("w1T", [NE, D, FF], F32, kind="ExternalInput").ap()
    G["in_w2T"] = nc.dram_tensor("w2T", [NE, FF, D], F32, kind="ExternalInput").ap()
    G["in_b1col"] = nc.dram_tensor("b1col", [128, 32 * 9], F32, kind="ExternalInput").ap()
    G["in_b2r"] = nc.dram_tensor("b2row", [1, 9 * D], F32, kind="ExternalInput").ap()
    G["in_ones"] = nc.dram_tensor("ones_r", [1, 128], F32, kind="ExternalInput").ap()
    G["in_rw"] = nc.dram_tensor("rw_prep", [128, 64], F32, kind="ExternalInput").ap()
    G["in_rb"] = nc.dram_tensor("rb_bc", [128, 8], F32, kind="ExternalInput").ap()
    G["in_out"] = nc.dram_tensor("out", [T, D], F32, kind="ExternalOutput").ap()
    G["w1c"] = nc.dram_tensor("w1c", [NE, NQ, 128, 8192], BF16, kind="Internal")
    G["lnd"] = nc.dram_tensor("lnd", [T, D], BF16, kind="Internal")
    G["w2c"] = nc.dram_tensor("w2c", [NE, NQ, 128, 8192], BF16, kind="Internal")
    G["in_winT"] = nc.dram_tensor("winT", [128, 8 * 2048], F32, kind="ExternalInput").ap()
    G["in_wxdT"] = nc.dram_tensor("wxdT", [128, 8 * 96], F32, kind="ExternalInput").ap()
    G["in_wdtT"] = nc.dram_tensor("wdtT", [64, 1024], F32, kind="ExternalInput").ap()
    G["in_woutT"] = nc.dram_tensor("woutT", [128, 16 * 1024], F32, kind="ExternalInput").ap()
    G["in_sel"] = nc.dram_tensor("sel32", [32, 32 * 128], F32, kind="ExternalInput").ap()
    G["in_bout"] = nc.dram_tensor("bout_r", [1, 1024], F32, kind="ExternalInput").ap()
    G["in_convc"] = nc.dram_tensor("convc", [128, 64], F32, kind="ExternalInput").ap()
    G["in_convb"] = nc.dram_tensor("convb", [128, 16], F32, kind="ExternalInput").ap()
    G["in_invdt"] = nc.dram_tensor("invdtc", [128, 8], F32, kind="ExternalInput").ap()
    G["in_dcol"] = nc.dram_tensor("dcolc", [128, 8], F32, kind="ExternalInput").ap()
    G["in_msel"] = nc.dram_tensor("mselc", [128, 24], F32, kind="ExternalInput").ap()
    G["in_msel2"] = nc.dram_tensor("msel2c", [128, 16], F32, kind="ExternalInput").ap()
    G["xz_dram"] = nc.dram_tensor("xz_d", [128, 16 * 2048], BF16, kind="Internal")
    G["z_dram"] = nc.dram_tensor("z_d", [128, 8 * 2048], BF16, kind="Internal")
    G["dlt_dram"] = nc.dram_tensor("dlt_d", [128, 8 * 2048], BF16, kind="Internal")
    G["y_dram"] = nc.dram_tensor("y_d", [128, 8 * 2048], BF16, kind="Internal")
    G["xres_dram"] = nc.dram_tensor("xres_d", [128, NTT * D], F32, kind="Internal")
    G["cc1_in"] = nc.dram_tensor("cc1_in", [128, 48], BF16, kind="Internal")
    G["cc1_out"] = nc.dram_tensor("cc1_out", [8 * 128, 48], BF16, kind="Internal")
    G["cc2_in"] = nc.dram_tensor("cc2_in", [128, 256], F32, kind="Internal")
    G["cc2_out"] = nc.dram_tensor("cc2_out", [8 * 128, 256], F32, kind="Internal")

    p.setup_consts([1e-5, -0.5, 1.0] + [-(n + 1.0) for n in range(16)])
    G["b1col"] = p.sbuf("b1col_sb", [128, 32 * 9], F32)
    G["b2row_bf"] = p.sbuf("b2rowbf", [1, 9 * D], BF16)
    G["ones_bf"] = p.sbuf("onesbf", [1, 128], BF16)
    G["rw_bf"] = p.sbuf("rwbf", [128, 64], BF16)
    G["rb_bc"] = p.sbuf("rbbc", [128, 8], F32)
    G["haloL"] = p.sbuf("haloL", [128, 48], BF16)
    G["haloR"] = p.sbuf("haloR", [128, 48], BF16)
    G["conv_c_sb"] = p.sbuf("convc_sb", [128, 64], F32)
    G["conv_b_sb"] = p.sbuf("convb_sb", [128, 16], F32)
    G["invdt_sb"] = p.sbuf("invdt_sb", [128, 8], F32)
    G["inv2_sb"] = p.sbuf("inv2_sb", [128, 8], F32)
    G["d_col_sb"] = p.sbuf("dcol_sb", [128, 8], F32)
    G["msel_sb"] = p.sbuf("msel_sb", [128, 24], F32)
    G["msel2_sb"] = p.sbuf("msel2_sb", [128, 16], F32)

    s_ldx = p.sem("dldx")
    s_ldxg = p.sem("dldxg")

    with nc.Block() as block:
        @block.gpsimd
        def _(g):
            g.dma_start(out=G["b2row_bf"][:], in_=G["in_b2r"]).then_inc(s_ldxg, 16)
            g.dma_start(out=G["ones_bf"][:], in_=G["in_ones"]).then_inc(s_ldxg, 16)
            g.dma_start(out=G["rw_bf"][:], in_=G["in_rw"]).then_inc(s_ldxg, 16)
        p.bump("dldxg", 16 * 3)

        s_vc = p.sem("vconst")
        done = False

        with contextlib.ExitStack() as sA:
            G["x_res"] = p.sbuf("x_res_a", [128, NTT * D], F32, sA)

            @block.sync
            def _(s):
                s.dma_start(out=G["x_res"][:].rearrange("p (tt d) -> p tt d", d=D),
                            in_=G["in_x"].rearrange("(tt p) d -> p tt d", p=128)).then_inc(s_ldx, 16)
                for dst, src in [("b1col", "in_b1col"), ("rb_bc", "in_rb"),
                                 ("conv_c_sb", "in_convc"), ("conv_b_sb", "in_convb"),
                                 ("invdt_sb", "in_invdt"), ("d_col_sb", "in_dcol"),
                                 ("msel_sb", "in_msel"), ("msel2_sb", "in_msel2")]:
                    s.dma_start(out=G[dst][:], in_=G[src]).then_inc(s_ldx, 16)
            p.bump("dldx", 16 * 9)

            @block.vector
            def _(v):
                v.wait_ge(s_ldx, p.cnt["dldx"])
                v.wait_ge(s_ldxg, p.cnt["dldxg"])
                p.emit_const_init(v)
                v.engine_nop().then_inc(s_vc, 1)
            p.bump("vconst")

            @block.tensor
            def _(t):
                t.wait_ge(s_ldx, p.cnt["dldx"])
                t.wait_ge(s_ldxg, p.cnt["dldxg"])
                t.wait_ge(s_vc, p.cnt["vconst"])

            @block.scalar
            def _(s):
                s.wait_ge(s_ldx, p.cnt["dldx"])
                s.wait_ge(s_ldxg, p.cnt["dldxg"])
                s.wait_ge(s_vc, p.cnt["vconst"])

            with contextlib.ExitStack() as f1:
                emit_ffn_pass(p, block, G, f1, 0, "_a")
            mbar(p, block, tensor_done=("tmm2", p.cnt["tmm2"]),
                 dma_waits=[("dcache", p.cnt["dcache"])])

            if stage == "ffn1":
                emit_output(p, block, G)
                done = True

            if stage == "mnull":
                s_sp = p.sem("dsp")

                @block.sync
                def _(s):
                    s.wait_ge(p.sems["vev"], p.cnt["vev"])
                    s.dma_start(out=G["xres_dram"].ap(), in_=G["x_res"][:]).then_inc(s_sp, 16)
                p.bump("dsp", 16)
                emit_output_dram(p, block, G)
                done = True

            if not done:
                with contextlib.ExitStack() as mln:
                    alloc_ln(p, G, mln, "_m")
                    G["w_in_sb"] = p.sbuf("w_in_sb", [128, 8 * 2048], BF16, mln)
                    G["xz_stg"] = [p.sbuf(f"xz_stg{k}", [128, 512], BF16, mln) for k in range(2)]
                    G["pay_sb"] = p.sbuf("pay_sb", [128, 48], BF16, mln)
                    G["gath1_sb"] = p.sbuf("gath1_sb", [128, 8 * 48], BF16, mln)
                    G["ps_xz"] = [p.psum(f"ps_xz{k}", [128, 512], F32, mln) for k in range(2)]
                    emit_ln_router(p, block, G, 2, with_router=False)
                    if stage == "mln":
                        s_sp = p.sem("dsp")

                        @block.sync
                        def _(s):
                            s.wait_ge(p.sems["vln"], p.cnt["vln"])
                            s.dma_start(out=G["xres_dram"].ap(),
                                        in_=G["x_res"][:]).then_inc(s_sp, 16)
                        p.bump("dsp", 16)
                        emit_output_dram(p, block, G)
                        done = True
                    if not done:
                        mp_parts = {"mp_w": ("w",), "mp_mm": ("w", "mm")}.get(
                            stage, ("w", "mm", "pay"))
                        emit_m_proj(p, block, G, parts=mp_parts)
                    if stage in ("mproj", "mp_w", "mp_mm"):
                        emit_output_dram(p, block, G)
                        done = True
                    if not done:
                        emit_m_halo(p, block, G)
                        if stage == "mhalo":
                            emit_output_dram(p, block, G)
                            done = True
                    if not done:
                        mbar(p, block, tensor_done=("txz", p.cnt["txz"]),
                             dma_waits=[("dsp", p.cnt["dsp"]), ("dxs", p.cnt["dxs"]),
                                        ("dg1", p.cnt["dg1"]), ("dpy", p.cnt["dpy"])])

        if not done:
            with contextlib.ExitStack() as s2:
                G["u_bf"] = p.sbuf("u_bf", [128, 8 * T], BF16, s2)
                G["y_locbf"] = p.sbuf("y_locbf", [128, 8 * T], BF16, s2)
                G["dlt_bf"] = p.sbuf("dlt_bf", [128, 8 * T], BF16, s2)
                G["bc_bf"] = p.sbuf("bc_bf", [32, T], BF16, s2)
                G["dtlow_bf"] = p.sbuf("dtlow_bf", [64, T], BF16, s2)
                G["sel_sb"] = p.sbuf("sel_sb", [32, 32 * 128], BF16, s2)
                G["zeros_bf"] = p.sbuf("zeros_bf", [128, T], BF16, s2)
                G["w_xd_sb"] = p.sbuf("w_xd_sb", [128, 8 * 96], BF16, s2)
                G["w_dt_sb"] = p.sbuf("w_dt_sb", [64, 1024], BF16, s2)
                G["s_all"] = p.sbuf("s_all", [128, 128], F32, s2)
                G["s_fin"] = p.sbuf("s_fin", [128, 128], F32, s2)
                G["p_all"] = p.sbuf("p_all", [128, 128], F32, s2)
                G["dTot_col"] = p.sbuf("dTot_col", [128, 8], F32, s2)

                s_mw = p.sem("dmw")

                @block.gpsimd
                def _(g):
                    g.dma_start(out=G["w_xd_sb"][:], in_=G["in_wxdT"]).then_inc(s_mw, 16)
                    g.dma_start(out=G["w_dt_sb"][:], in_=G["in_wdtT"]).then_inc(s_mw, 16)
                    g.dma_start(out=G["sel_sb"][:], in_=G["in_sel"]).then_inc(s_mw, 16)
                p.bump("dmw", 16 * 3)

                with contextlib.ExitStack() as s2a:
                    G["xpre"] = [p.sbuf(f"xpre{k}", [128, 2051], BF16, s2a) for k in range(2)]
                    G["cv_acc"] = [p.sbuf(f"cvacc{k}", [128, T], F32, s2a) for k in range(2)]
                    G["z_stg"] = [p.sbuf(f"z_stg{k}", [128, T], BF16, s2a) for k in range(4)]
                    emit_m_conv(p, block, G)
                    if stage == "mconv":
                        emit_output_dram(p, block, G)
                        done = True
                    if not done:
                        mbar(p, block, dma_waits=[("dzs", p.cnt["dzs"]), ("dxl", p.cnt["dxl"])])

                if not done:
                    with contextlib.ExitStack() as s2b:
                        G["sp_ab"] = [p.sbuf(f"sp_ab{k}", [128, 512], F32, s2b) for k in range(2)]
                        G["sp_rl"] = p.sbuf("sp_rl", [128, 512], F32, s2b)
                        G["dl_stg"] = [p.sbuf(f"dl_stg{k}", [128, T], BF16, s2b) for k in range(2)]
                        G["ps_xd"] = [p.psum(f"ps_xd{k}", [128, 512], F32, s2b) for k in range(2)]
                        G["ps_dt"] = [p.psum(f"ps_dt{k}", [128, 512], F32, s2b) for k in range(2)]
                        emit_m_xdbl(p, block, G)
                        if stage == "mxdbl":
                            emit_output_dram(p, block, G)
                            done = True
                        if not done:
                            mbar(p, block, tensor_done=("tdt", p.cnt["tdt"]),
                                 dma_waits=[("ddm", p.cnt["ddm"])])

                if not done:
                    with contextlib.ExitStack() as s2c:
                        G["dA_bf"] = [p.sbuf(f"dA_bf{k}", [128, T], BF16, s2c) for k in range(2)]
                        G["dBu_bf"] = [p.sbuf(f"dBu_bf{k}", [128, T], BF16, s2c) for k in range(2)]
                        G["h_bf"] = [p.sbuf(f"h_bf{k}", [128, T], BF16, s2c) for k in range(2)]
                        G["hC_bf"] = p.sbuf("hC_bf", [128, T], BF16, s2c)
                        G["Cbc_bf"] = [p.sbuf(f"Cbc_bf{k}", [128, T], BF16, s2c) for k in range(2)]
                        G["du_f"] = p.sbuf("du_f", [128, T], F32, s2c)
                        G["y_dt"] = p.sbuf("y_dt", [128, T], F32, s2c)
                        G["ps_B"] = p.psum("ps_B", [128, 2048], F32, s2c)
                        G["ps_C"] = p.psum("ps_C", [128, 2048], F32, s2c)
                        emit_m_scan(p, block, G)
                        if stage == "mscan":
                            emit_output_dram(p, block, G)
                            done = True
                        if not done:
                            mbar(p, block, tensor_done=("tcp", p.cnt["tcp"]))

                if not done:
                    with contextlib.ExitStack() as s2d:
                        G["gath2_sb"] = p.sbuf("gath2_sb", [128, 8 * 256], F32, s2d)
                        G["h0_all"] = p.sbuf("h0_all", [128, 128], F32, s2d)
                        G["psel"] = p.sbuf("psel", [128, 128], F32, s2d)
                        G["ssel"] = p.sbuf("ssel", [128, 128], F32, s2d)
                        G["dl_t"] = [p.sbuf(f"dl_t{k}", [128, T], BF16, s2d) for k in range(2)]
                        G["wn_bf"] = [p.sbuf(f"wn_bf{k}", [128, T], BF16, s2d) for k in range(2)]
                        G["t1_bf"] = p.sbuf("t1_bf", [128, T], BF16, s2d)
                        G["acc_f"] = p.sbuf("acc_f", [128, T], F32, s2d)
                        G["tmp_f"] = p.sbuf("tmp_f", [128, T], F32, s2d)
                        G["ps_c2"] = [p.psum(f"ps_c2{k}", [128, 2048], F32, s2d) for k in range(2)]
                        emit_m_exchange(p, block, G)
                        if stage == "mexch":
                            emit_output_dram(p, block, G)
                            done = True
                        if not done:
                            emit_m_corr(p, block, G)
                            if stage == "mcorr":
                                emit_output_dram(p, block, G)
                                done = True
                        if not done:
                            mbar(p, block, tensor_done=("tc2", p.cnt["tc2"]),
                                 dma_waits=[("dyw", p.cnt["dyw"])])

        if not done:
            G["x_res"] = p.sbuf("x_res_b", [128, NTT * D], F32)
            with contextlib.ExitStack() as s3a:
                G["y_sb"] = p.sbuf("y_sb", [128, 8 * T], BF16, s3a)
                G["z_sb"] = p.sbuf("z_sb", [128, 8 * T], BF16, s3a)
                G["w_out_sb"] = p.sbuf("w_out_sb", [128, 16 * 1024], BF16, s3a)
                G["bout_bf"] = p.sbuf("bout_bf", [1, 1024], BF16, s3a)
                G["ps_o"] = [p.psum(f"ps_o{k}", [128, 1024], F32, s3a) for k in range(2)]
                emit_m_out(p, block, G)
                mbar(p, block, tensor_done=("tom", p.cnt["tom"]))

            if stage == "mamba":
                emit_output(p, block, G)
                done = True

            if not done:
                with contextlib.ExitStack() as f2:
                    emit_ffn_pass(p, block, G, f2, 1, "_b")
                emit_output(p, block, G)

    p.stack.close()
    return p, G


_CACHE = {}


def _host_prep(inputs):
    f32 = np.float32
    L = {}
    w1 = [np.ascontiguousarray(inputs["L0_ffn_shared_w1"].T)] + \
         [np.ascontiguousarray(inputs["L0_ffn_spec_w1"][e].T) for e in range(8)]
    L["w1T"] = np.stack(w1).astype(f32, copy=False)
    w2 = [np.ascontiguousarray(inputs["L0_ffn_shared_w2"].T)] + \
         [np.ascontiguousarray(inputs["L0_ffn_spec_w2"][e].T) for e in range(8)]
    L["w2T"] = np.stack(w2).astype(f32, copy=False)
    b1 = np.stack([inputs["L0_ffn_shared_b1"]] + [inputs["L0_ffn_spec_b1"][e] for e in range(8)])
    L["b1col"] = np.ascontiguousarray(b1.reshape(9, 32, 128).transpose(2, 1, 0).reshape(128, 32 * 9)).astype(f32)
    L["b2row"] = np.ascontiguousarray(
        np.stack([inputs["L0_ffn_shared_b2"]] + [inputs["L0_ffn_spec_b2"][e] for e in range(8)]).reshape(1, -1)).astype(f32)
    L["ones_r"] = np.ones((1, 128), dtype=f32)
    rw = inputs["L0_ffn_router_w"]
    L["rw_prep"] = np.ascontiguousarray(rw.reshape(8, 8, 128).transpose(2, 1, 0).reshape(128, 64)).astype(f32)
    L["rb_bc"] = np.ascontiguousarray(np.tile(inputs["L0_ffn_router_b"][None, :], (128, 1))).astype(f32)

    L["winT"] = np.ascontiguousarray(
        inputs["L0_attn_W_in"].reshape(8, 128, 2048).transpose(1, 0, 2).reshape(128, 8 * 2048)).astype(f32)
    L["wxdT"] = np.ascontiguousarray(
        inputs["L0_attn_W_xdbl"].reshape(8, 128, 96).transpose(1, 0, 2).reshape(128, 8 * 96)).astype(f32)
    L["wdtT"] = np.ascontiguousarray(inputs["L0_attn_W_dt"]).astype(f32)
    L["woutT"] = np.ascontiguousarray(
        inputs["L0_attn_W_out"].reshape(16, 128, 1024).transpose(1, 0, 2).reshape(128, 16 * 1024)).astype(f32)
    L["sel32"] = np.ascontiguousarray(np.kron(np.eye(32, dtype=f32), np.ones((1, 128), f32)))
    L["bout_r"] = np.ascontiguousarray(inputs["L0_attn_b_out"].reshape(1, 1024)).astype(f32)
    cxw = inputs["L0_attn_conv_x_w"].reshape(4, 1024)
    czw = inputs["L0_attn_conv_z_w"].reshape(4, 1024)
    cc = np.empty((128, 64), f32)
    for dt in range(8):
        for j in range(4):
            cc[:, dt * 4 + j] = cxw[j, dt * 128:(dt + 1) * 128]
            cc[:, 32 + dt * 4 + j] = czw[j, dt * 128:(dt + 1) * 128]
    L["convc"] = cc
    cb = np.empty((128, 16), f32)
    for dt in range(8):
        cb[:, dt] = inputs["L0_attn_conv_x_b"][dt * 128:(dt + 1) * 128]
        cb[:, 8 + dt] = inputs["L0_attn_conv_z_b"][dt * 128:(dt + 1) * 128]
    L["convb"] = cb
    L["invdtc"] = np.ascontiguousarray(
        inputs["L0_attn_inv_dt"].reshape(8, 128).T).astype(f32)
    L["dcolc"] = np.ascontiguousarray(inputs["L0_attn_D"].reshape(8, 128).T).astype(f32)
    return L


def _core_masks(core):
    b, c = core // 4, core % 4
    f32 = np.float32
    msel = np.zeros((1, 24), f32)
    for j in range(3):
        cc = c - 1 - j
        if cc >= 0:
            msel[0, j * 8 + (b * 4 + cc)] = 1.0
    msel2 = np.zeros((1, 16), f32)
    if c > 0:
        msel2[0, b * 4 + c - 1] = 1.0
    if c < 3:
        msel2[0, 8 + b * 4 + c + 1] = 1.0
    return (np.ascontiguousarray(np.tile(msel, (128, 1))),
            np.ascontiguousarray(np.tile(msel2, (128, 1))))


def kernel(**inputs):
    key = ("prog", STAGE)
    if key not in _CACHE:
        _CACHE[key] = build(STAGE)
    p, G = _CACHE[key]

    L = _host_prep(inputs)
    x = np.asarray(inputs["L0_attn_x"], dtype=np.float32)

    in_maps = []
    for core in range(8):
        b, c = core // 4, core % 4
        msel, msel2 = _core_masks(core)
        m = {
            "x_sh": np.ascontiguousarray(x[b, c * T:(c + 1) * T, :]),
            "w1T": L["w1T"], "w2T": L["w2T"], "b1col": L["b1col"], "b2row": L["b2row"],
            "ones_r": L["ones_r"],
            "rw_prep": L["rw_prep"], "rb_bc": L["rb_bc"],
            "winT": L["winT"], "wxdT": L["wxdT"], "wdtT": L["wdtT"],
            "woutT": L["woutT"], "sel32": L["sel32"], "bout_r": L["bout_r"],
            "convc": L["convc"], "convb": L["convb"], "invdtc": L["invdtc"],
            "dcolc": L["dcolc"], "mselc": msel, "msel2c": msel2,
        }
        in_maps.append(m)

    res = run_bass_kernel_spmd(p.nc, in_maps, core_ids=list(range(8)))
    out = np.empty((2, 8192, D), dtype=np.float32)
    for core in range(8):
        b, c = core // 4, core % 4
        out[b, c * T:(c + 1) * T, :] = res.results[core]["out"]
    return out


# version 03
# speedup vs baseline: 1.3107x; 1.3107x over previous
import contextlib
import os
import numpy as np
import concourse.bass as bass
import concourse.mybir as mybir
from concourse.bass_utils import run_bass_kernel_spmd

F32 = mybir.dt.float32
BF16 = mybir.dt.bfloat16
AF = mybir.ActivationFunctionType
ALU = mybir.AluOpType
AX = mybir.AxisListType.X

D = 1024
T = 2048
NTT = 16
DT = 8
FF = 4096
NE = 9
NQ = 4
FQ = 1024
FT = 8
WINDOWS = [(0, 384), (384, 384), (768, 384), (1152, 384), (1536, 384), (1920, 128)]
NW = len(WINDOWS)
NQTOT_PASS = NW * NE * NQ

STAGE = "full"


class Prog:
    def __init__(self):
        self.nc = bass.Bass("TRN2", target_bir_lowering=False, debug=False,
                            detect_race_conditions=False)
        self.stack = contextlib.ExitStack()
        self.sems = {}
        self.cnt = {}
        self.nbar = 0

    def sem(self, name):
        if name not in self.sems:
            self.sems[name] = self.stack.enter_context(self.nc.semaphore(name))
            self.cnt[name] = 0
        return self.sems[name]

    def bump(self, name, n=1):
        self.cnt[name] += n
        return self.cnt[name]

    def sbuf(self, name, shape, dt=F32, scope=None):
        st = scope if scope is not None else self.stack
        return st.enter_context(self.nc.sbuf_tensor(name, shape, dt))

    def psum(self, name, shape, dt=F32, scope=None):
        st = scope if scope is not None else self.stack
        return st.enter_context(self.nc.psum_tensor(name, shape, dt))

    def edge(self, inst, eng):
        se = self.sem("selfe")
        n = self.bump("selfe")
        inst.then_inc(se, 1)
        eng.wait_ge(se, n)

    def setup_consts(self, values):
        self.const_vals = list(values)
        self.const_sb = self.sbuf("const_sb", [128, max(1, len(values))], F32)
        for j, v in enumerate(self.const_vals):
            self.nc.const_aps.register(F32, float(v), self.const_sb[:, j:j + 1])

    def emit_const_init(self, v):
        for j, val in enumerate(self.const_vals):
            self.nc.vector.memset(self.const_sb[:, j:j + 1], float(val)) if False else \
                v.memset(self.const_sb[:, j:j + 1], float(val))


def barrier(p, block, tensor_done=None):
    p.nbar += 1
    tgt = 2 * p.nbar
    s = p.sem("bar")

    @block.scalar
    def _(e):
        if tensor_done is not None:
            e.wait_ge(p.sems[tensor_done[0]], tensor_done[1])
        e.nop().then_inc(s, 1)
        e.wait_ge(s, tgt)

    @block.vector
    def _(e):
        if tensor_done is not None:
            e.wait_ge(p.sems[tensor_done[0]], tensor_done[1])
        e.nop().then_inc(s, 1)
        e.wait_ge(s, tgt)

    @block.tensor
    def _(e):
        e.wait_ge(s, tgt)


def emit_ln_router(p, block, G, pass_idx, with_router, sub=99):
    nc = p.nc
    s_vln = p.sem("vln"); s_lnf = p.sem("dlnf"); s_var = p.sem("vvar"); s_rstd = p.sem("srstd")
    b_vln = p.cnt["vln"]; b_lnf = p.cnt["dlnf"]


    xr = G["x_res"]
    ln_tok = G["ln_tok"]
    ln_f = G["ln_f"]
    st = G["stats"]
    mean_all = G["mean_all"]; var_all = G["var_all"]
    rstd_all = G["rstd_all"]; nmr_all = G["nmr_all"]
    sq = G["sq"]
    s1_all = G["s1_all"]; s2_all = G["s2_all"]

    @block.vector
    def _(v):
        for tt in range(NTT):
            xt = xr[:, tt * D:(tt + 1) * D]
            v.tensor_tensor(sq[:], xt, xt, ALU.mult)
            v.reduce_sum(s1_all[:, tt:tt + 1], xt, AX)
            v.reduce_sum(s2_all[:, tt:tt + 1], sq[:], AX)
        v.tensor_scalar(mean_all[:], s1_all[:], 1.0 / D, None, ALU.mult)
        v.tensor_scalar(s2_all[:], s2_all[:], 1.0 / D, None, ALU.mult)
        p.edge(v.tensor_tensor(st[:, 0:16], mean_all[:], mean_all[:], ALU.mult), v)
        v.tensor_tensor(var_all[:], s2_all[:], st[:, 0:16], ALU.subtract).then_inc(s_var, 1)
    p.bump("vvar")

    @block.scalar
    def _(s):
        s.wait_ge(s_var, p.cnt["vvar"])
        p.edge(s.activation(rstd_all[:], var_all[:], AF.Ln, bias=1e-5), s)
        s.activation(rstd_all[:], rstd_all[:], AF.Exp, scale=-0.5).then_inc(s_rstd, 1)
    p.bump("srstd")

    lnd = G["lnd"]
    s_lnst = p.sem("dlnst")
    b_lnst = p.cnt["dlnst"]

    @block.vector
    def _(v):
        v.wait_ge(s_rstd, p.cnt["srstd"])
        p.edge(v.tensor_tensor(nmr_all[:], mean_all[:], rstd_all[:], ALU.mult), v)
        p.edge(v.tensor_scalar(nmr_all[:], nmr_all[:], -1.0, None, ALU.mult), v)
        for tt in range(NTT):
            if tt >= 2:
                v.wait_ge(s_lnst, b_lnst + 16 * (tt - 1))
            xt = xr[:, tt * D:(tt + 1) * D]
            v.tensor_scalar(ln_tok[:, (tt % 2) * D:(tt % 2 + 1) * D], xt,
                            rstd_all[:, tt:tt + 1], nmr_all[:, tt:tt + 1],
                            ALU.mult, ALU.add).then_inc(s_vln, 1)
    p.bump("vln", NTT)

    @block.sync
    def _(s):
        for tt in range(NTT):
            s.wait_ge(s_vln, b_vln + tt + 1)
            s.dma_start(out=lnd[tt * 128:(tt + 1) * 128, :],
                        in_=ln_tok[:, (tt % 2) * D:(tt % 2 + 1) * D]).then_inc(s_lnst, 16)
        for dt in range(DT):
            s.dma_start(out=ln_f[:, dt * T:(dt + 1) * T],
                        in_=lnd[:, dt * 128:(dt + 1) * 128],
                        transpose=True).then_inc(s_lnf, 16)
    p.bump("dlnst", 16 * NTT)
    p.bump("dlnf", 16 * DT)

    if not with_router:
        return

    s_lg = p.sem("tlg"); s_nm = p.sem("vnm"); s_exp = p.sem("sexp"); s_cmb = p.sem("vcmb")
    b_lg = p.cnt["tlg"]; b_nm = p.cnt["vnm"]; b_exp = p.cnt["sexp"]; b_cmb = p.cnt["vcmb"]

    ps_lg = G["ps_lg"]
    rw = G["rw_bf"]; rb = G["rb_bc"]
    cwall = G["cwall"]; cw2all = G["cw2all"]
    pw = G["comb_w"]; pw2 = G["comb_w2"]; c_m = G["c_m"]
    comb_half = G["comb_half"]

    @block.tensor
    def _(t):
        t.wait_ge(s_lnf, b_lnf + 16 * DT)
        for tt in range(NTT):
            if tt >= 2:
                t.wait_ge(s_nm, b_nm + tt - 1)
            ps = ps_lg[tt % 2]
            for kt in range(DT):
                mi = t.matmul(ps[:, :], ln_f[:, kt * T + tt * 128: kt * T + (tt + 1) * 128],
                              rw[:, kt * 8:(kt + 1) * 8], start=(kt == 0), stop=(kt == DT - 1))
            mi.then_inc(s_lg, 1)
    p.bump("tlg", NTT)

    @block.vector
    def _(v):
        for tt in range(NTT):
            v.wait_ge(s_lg, b_lg + tt + 1)
            ps = ps_lg[tt % 2]
            v.tensor_tensor(cwall[:, tt * 8:(tt + 1) * 8], ps[:, :], rb[:],
                            ALU.add).then_inc(s_nm, 1)
    p.bump("vnm", NTT)

    @block.scalar
    def _(s):
        s.wait_ge(s_nm, b_nm + NTT)
        s.activation(cw2all[:], cwall[:], AF.Exp).then_inc(s_exp, 1)
    p.bump("sexp", 1)

    @block.vector
    def _(v):
        v.wait_ge(s_exp, b_exp + 1)
        for tt in range(NTT):
            E = lambda inst: p.edge(inst, v)
            ce = cw2all[:, tt * 8:(tt + 1) * 8]
            E(v.reduce_sum(c_m[:, 3:4], ce, AX))
            E(v.reciprocal(c_m[:, 3:4], c_m[:, 3:4]))
            E(v.tensor_scalar(pw[:], ce, c_m[:, 3:4], None, ALU.mult))
            E(v.reduce_max(c_m[:, 0:1], pw[:], AX))
            E(v.tensor_scalar(pw2[:], pw[:], c_m[:, 0:1], None, ALU.is_equal))
            E(v.tensor_scalar(pw2[:], pw2[:], -1.0, 1.0, ALU.mult, ALU.add))
            E(v.tensor_tensor(pw2[:], pw[:], pw2[:], ALU.mult))
            E(v.reduce_max(c_m[:, 2:3], pw2[:], AX))
            E(v.tensor_scalar(pw2[:], pw[:], c_m[:, 2:3], None, ALU.is_ge))
            E(v.tensor_tensor(pw[:], pw[:], pw2[:], ALU.mult))
            E(v.reduce_sum(c_m[:, 3:4], pw[:], AX))
            E(v.reciprocal(c_m[:, 3:4], c_m[:, 3:4]))
            v.tensor_scalar(comb_half[:, tt * 9 + 1:(tt + 1) * 9], pw[:],
                            c_m[:, 3:4], 0.5, ALU.mult, ALU.mult)
            v.memset(comb_half[:, tt * 9:tt * 9 + 1], 0.5)
            v.engine_nop().then_inc(s_cmb, 1)
    p.bump("vcmb", NTT)


def emit_ffn_experts(p, block, G, pass_idx):
    nc = p.nc
    s_w1g = [p.sem("dw1g0"), p.sem("dw1g1")]; s_w2g = [p.sem("dw2g0"), p.sem("dw2g1")]
    s_w1s = p.sem("dw1s"); s_w2s = p.sem("dw2s")
    s_cache = p.sem("dcache")
    s_mmH = p.sem("tmmH"); s_gel = p.sem("sgel"); s_mm2 = p.sem("tmm2"); s_ev = p.sem("vev")
    s_lnf = p.sem("dlnf")
    b_w1g = [p.cnt["dw1g0"], p.cnt["dw1g1"]]; b_w2g = [p.cnt["dw2g0"], p.cnt["dw2g1"]]
    b_w1s = p.cnt["dw1s"]; b_w2s = p.cnt["dw2s"]
    b_mmH = p.cnt["tmmH"]; b_gel = p.cnt["sgel"]; b_mm2 = p.cnt["tmm2"]; b_ev = p.cnt["vev"]
    b_cache = p.cnt["dcache"]
    b_lnf = p.cnt["dlnf"] - 16 * DT

    xr = G["x_res"]; ln_f = G["ln_f"]; pool_A = G["pool_A"]; pool_B = G["pool_B"]
    h1q = G["h1q"]; b1col = G["b1col"]
    comb_half = G["comb_half"]; ones_bf = G["ones_bf"]; b2row_bf = G["b2row_bf"]
    ps_y = G["ps_y"]; ps_h1 = G["ps_h1"]
    w1T = G["in_w1T"]; w2T = G["in_w2T"]; w1c = G["w1c"]; w2c = G["w2c"]

    QL = [(e, q) for e in range(NE) for q in range(NQ)]
    cached = pass_idx > 0

    def wslc(i):
        return (i % 2) * 8192

    if not cached:
        @block.gpsimd
        def _(g):
            for i, (e, q) in enumerate(QL):
                if i >= 2:
                    g.wait_ge(s_mmH, b_mmH + 16 * (i - 1))
                    g.wait_ge(s_mm2, b_mm2 + 16 * (i - 1))
                    g.wait_ge(s_cache, b_cache + 32 * (i - 1))
                src1 = w1T[e].rearrange("(kt p) f -> p kt f", p=128)[:, :, q * FQ:(q + 1) * FQ]
                dst1 = pool_A[:, wslc(i):wslc(i) + 8192].rearrange("p (kt f) -> p kt f", f=FQ)
                g.dma_start(out=dst1, in_=src1).then_inc(s_w1g[i % 2], 16)
                src2 = w2T[e, q * FQ:(q + 1) * FQ, :].rearrange("(kt p) d -> p kt d", p=128)
                dst2 = pool_B[:, wslc(i):wslc(i) + 8192].rearrange("p (kt d) -> p kt d", d=D)
                g.dma_start(out=dst2, in_=src2).then_inc(s_w2g[i % 2], 16)
        p.bump("dw1g0", 16 * 18); p.bump("dw1g1", 16 * 18)
        p.bump("dw2g0", 16 * 18); p.bump("dw2g1", 16 * 18)

        @block.sync
        def _(s):
            for i, (e, q) in enumerate(QL):
                s.wait_ge(s_w1g[i % 2], b_w1g[i % 2] + 16 * (i // 2 + 1))
                s.wait_ge(s_w2g[i % 2], b_w2g[i % 2] + 16 * (i // 2 + 1))
                s.dma_start(out=w1c[e, q], in_=pool_A[:, wslc(i):wslc(i) + 8192]).then_inc(s_cache, 16)
                s.dma_start(out=w2c[e, q], in_=pool_B[:, wslc(i):wslc(i) + 8192]).then_inc(s_cache, 16)
        p.bump("dcache", 32 * 36)
    else:
        @block.sync
        def _(s):
            for i, (e, q) in enumerate(QL):
                if i >= 2:
                    s.wait_ge(s_mmH, b_mmH + 16 * (i - 1))
                    s.wait_ge(s_mm2, b_mm2 + 16 * (i - 1))
                s.dma_start(out=pool_A[:, wslc(i):wslc(i) + 8192], in_=w1c[e, q]).then_inc(s_w1s, 16)
                s.dma_start(out=pool_B[:, wslc(i):wslc(i) + 8192], in_=w2c[e, q]).then_inc(s_w2s, 16)
        p.bump("dw1s", 16 * 36)
        p.bump("dw2s", 16 * 36)

    def w1_ready(t, i):
        if not cached:
            t.wait_ge(s_w1g[i % 2], b_w1g[i % 2] + 16 * (i // 2 + 1))
        else:
            t.wait_ge(s_w1s, b_w1s + 16 * (i + 1))

    def w2_ready(t, i):
        if not cached:
            t.wait_ge(s_w2g[i % 2], b_w2g[i % 2] + 16 * (i // 2 + 1))
        else:
            t.wait_ge(s_w2s, b_w2s + 16 * (i + 1))

    def emit_mm1(t, i, e, q, w):
        base = wslc(i)
        for ft in range(FT):
            fidx = (i * 2 + w) * 8 + ft
            if fidx >= 2:
                t.wait_ge(s_gel, b_gel + fidx - 1)
            ps = ps_h1[fidx % 2]
            for kt in range(DT):
                for h in range(2):
                    mi = t.matmul(ps[:, h * 512:(h + 1) * 512],
                                  pool_A[:, base + kt * FQ + ft * 128: base + kt * FQ + (ft + 1) * 128],
                                  ln_f[:, kt * T + w * 1024 + h * 512: kt * T + w * 1024 + (h + 1) * 512],
                                  start=(kt == 0), stop=(kt == DT - 1))
            mi.then_inc(s_mmH, 1)

    def emit_mm2(t, i, e, q, w):
        base = wslc(i)
        for ttl in range(8):
            yi = (i * 2 + w) * 8 + ttl
            if yi >= 2:
                t.wait_ge(s_ev, b_ev + yi - 1)
            psy = ps_y[yi % 2]
            first = True
            if q == 0:
                for dh in range(2):
                    t.matmul(psy[:, dh * 512:(dh + 1) * 512],
                             ones_bf[0:1, 0:128],
                             b2row_bf[0:1, e * D + dh * 512: e * D + (dh + 1) * 512],
                             start=True, stop=False)
                first = False
            for kf in range(FT):
                if ttl == 0:
                    t.wait_ge(s_gel, b_gel + (i * 2 + w) * 8 + kf + 1)
                for dh in range(2):
                    mi = t.matmul(psy[:, dh * 512:(dh + 1) * 512],
                                  h1q[:, kf * 1024 + ttl * 128: kf * 1024 + (ttl + 1) * 128],
                                  pool_B[:, base + kf * D + dh * 512: base + kf * D + (dh + 1) * 512],
                                  start=(first and kf == 0), stop=(kf == FT - 1))
            mi.then_inc(s_mm2, 1)

    @block.tensor
    def _(t):
        t.wait_ge(s_lnf, b_lnf + 16 * DT)
        for i, (e, q) in enumerate(QL):
            w1_ready(t, i)
            emit_mm1(t, i, e, q, 0)
            emit_mm2_pending = True
            w2_ready(t, i)
            emit_mm2(t, i, e, q, 0)
            emit_mm1(t, i, e, q, 1)
            emit_mm2(t, i, e, q, 1)
    p.bump("tmmH", 16 * 36)
    p.bump("tmm2", 16 * 36)

    @block.scalar
    def _(s):
        for i, (e, q) in enumerate(QL):
            for w in range(2):
                for ft in range(FT):
                    fidx = (i * 2 + w) * 8 + ft
                    s.wait_ge(s_mmH, b_mmH + fidx + 1)
                    s.wait_ge(s_mm2, b_mm2 + 16 * i + 8 * w)
                    ftg = q * FT + ft
                    s.activation(h1q[:, ft * 1024: (ft + 1) * 1024],
                                 ps_h1[fidx % 2][:, :], AF.Gelu_apprx_tanh,
                                 bias=b1col[:, ftg * 9 + e: ftg * 9 + e + 1]).then_inc(s_gel, 1)
    p.bump("sgel", 16 * 36)

    @block.vector
    def _(v):
        for i, (e, q) in enumerate(QL):
            for w in range(2):
                for ttl in range(8):
                    yi = (i * 2 + w) * 8 + ttl
                    v.wait_ge(s_mm2, b_mm2 + yi + 1)
                    tt = w * 8 + ttl
                    xt = xr[:, tt * D:(tt + 1) * D]
                    v.scalar_tensor_tensor(xt, ps_y[yi % 2][:, :],
                                           comb_half[:, tt * 9 + e: tt * 9 + e + 1],
                                           xt, ALU.mult, ALU.add).then_inc(s_ev, 1)
    p.bump("vev", 16 * 36)


def mbar(p, block, tensor_done=None, dma_waits=()):
    if not hasattr(p, "mcnt"):
        p.mcnt = 0
    p.mcnt += 3
    tgt = p.mcnt
    s = p.sem("mbar")

    def mk(e):
        if tensor_done is not None:
            e.wait_ge(p.sems[tensor_done[0]], tensor_done[1])
        for nm, c in dma_waits:
            e.wait_ge(p.sems[nm], c)
        e.nop().then_inc(s, 1)
        e.wait_ge(s, tgt)

    @block.scalar
    def _(e):
        mk(e)

    @block.vector
    def _(e):
        mk(e)

    @block.gpsimd
    def _(e):
        mk(e)

    @block.tensor
    def _(e):
        e.wait_ge(s, tgt)

    @block.sync
    def _(e):
        e.wait_ge(s, tgt)


NDT = 8
NST = 16


def emit_m_proj(p, block, G, parts=("w", "mm", "pay")):
    nc = p.nc
    s_lnf = p.sem("dlnf")
    s_mw = p.sem("dmw")
    s_mm = p.sem("txz"); s_ev = p.sem("sxe"); s_st = p.sem("dxs"); s_py = p.sem("dpy")
    s_sp = p.sem("dsp"); s_vln = p.sem("vln")
    b_lnf = p.cnt["dlnf"] - 16 * DT
    b_mm = p.cnt["txz"]; b_ev = p.cnt["sxe"]; b_st = p.cnt["dxs"]

    win = G["w_in_sb"]; lnf = G["ln_f"]; stg = G["xz_stg"]; pay = G["pay_sb"]
    ps = G["ps_xz"]

    @block.gpsimd
    def _(g):
        g.wait_ge(p.sems["dcache"], p.cnt["dcache"])
        g.dma_start(out=win[:], in_=G["in_winT"]).then_inc(s_mw, 16)
    p.bump("dmw", 16)


    @block.sync
    def _(s):
        s.wait_ge(s_vln, p.cnt["vln"])
        s.dma_start(out=G["xres_dram"].ap(), in_=G["x_res"][:]).then_inc(s_sp, 16)
    p.bump("dsp", 16)

    QL = [(it, w) for it in range(16) for w in range(4)]
    if "mm" not in parts:
        return

    @block.tensor
    def _(t):
        t.wait_ge(s_lnf, b_lnf + 16 * DT)
        t.wait_ge(s_mw, p.cnt["dmw"])
        for i, (it, w) in enumerate(QL):
            if i >= 2:
                t.wait_ge(s_ev, b_ev + i - 1)
            pp = ps[i % 2]
            for kt in range(DT):
                mi = t.matmul(pp[:, :512],
                              win[:, kt * 2048 + it * 128: kt * 2048 + (it + 1) * 128],
                              lnf[:, kt * T + w * 512: kt * T + (w + 1) * 512],
                              start=(kt == 0), stop=(kt == DT - 1))
            mi.then_inc(s_mm, 1)
    p.bump("txz", 64)

    s_pc = p.sem("spy")

    @block.scalar
    def _(s):
        for i, (it, w) in enumerate(QL):
            s.wait_ge(s_mm, b_mm + i + 1)
            if i >= 2:
                s.wait_ge(s_st, b_st + 16 * (i - 1))
            if "pay" in parts and w == 0:
                s.copy(out=pay[:, it * 3: it * 3 + 2],
                       in_=ps[i % 2][:, 0:2]).then_inc(s_pc, 1)
            if "pay" in parts and w == 3:
                s.copy(out=pay[:, it * 3 + 2: it * 3 + 3],
                       in_=ps[i % 2][:, 511:512]).then_inc(s_pc, 1)
            s.copy(out=stg[i % 2][:], in_=ps[i % 2][:, :512]).then_inc(s_ev, 1)
    p.bump("sxe", 64)
    if "pay" in parts:
        p.bump("spy", 32)

    @block.sync
    def _(s):
        for i, (it, w) in enumerate(QL):
            s.wait_ge(s_ev, b_ev + i + 1)
            s.dma_start(out=G["xz_dram"].ap()[:, it * 2048 + w * 512: it * 2048 + (w + 1) * 512],
                        in_=stg[i % 2][:]).then_inc(s_st, 16)
    p.bump("dxs", 16 * 64)

    if "pay" not in parts:
        return

    @block.sync
    def _(s):
        s.wait_ge(s_pc, p.cnt["spy"])
        s.dma_start(out=G["cc1_in"].ap(), in_=pay[:]).then_inc(s_py, 16)
    p.bump("dpy", 16)


def emit_m_halo(p, block, G):
    s_py = p.sem("dpy"); s_cc = p.sem("cc1"); s_g1 = p.sem("dg1"); s_hl = p.sem("vhl")

    @block.gpsimd
    def _(g):
        g.wait_ge(s_py, p.cnt["dpy"])
        g.collective_compute(
            "AllGather", mybir.AluOpType.bypass,
            replica_groups=[list(range(8))],
            ins=[G["cc1_in"].ap().opt()],
            outs=[G["cc1_out"].ap().opt()],
        ).then_inc(s_cc, 1)
        g.wait_ge(s_cc, p.cnt["cc1"] + 1)
        g.dma_start(out=G["gath1_sb"][:].rearrange("p (r c) -> p r c", c=48),
                    in_=G["cc1_out"].ap().rearrange("(r p) c -> p r c", p=128)).then_inc(s_g1, 16)
    p.bump("cc1", 1)
    p.bump("dg1", 16)

    @block.vector
    def _(v):
        v.wait_ge(s_g1, p.cnt["dg1"])
        hL = G["haloL"]; hR = G["haloR"]; gg = G["gath1_sb"]; m2 = G["msel2_sb"]
        p.edge(v.memset(hL[:], 0.0), v)
        p.edge(v.memset(hR[:], 0.0), v)
        for r in range(8):
            p.edge(v.scalar_tensor_tensor(hL[:], gg[:, r * 48:(r + 1) * 48],
                                          m2[:, r:r + 1], hL[:],
                                          ALU.mult, ALU.add), v)
            p.edge(v.scalar_tensor_tensor(hR[:], gg[:, r * 48:(r + 1) * 48],
                                          m2[:, 8 + r:9 + r], hR[:],
                                          ALU.mult, ALU.add), v)
        v.engine_nop().then_inc(s_hl, 1)
    p.bump("vhl", 1)


def emit_m_conv(p, block, G):
    s_xl = p.sem("dxl"); s_hc = p.sem("ghl"); s_cv = p.sem("vcv")
    s_sl = p.sem("ssl"); s_zs = p.sem("dzs"); s_hl = p.sem("vhl")
    b_xl = p.cnt["dxl"]; b_hc = p.cnt["ghl"]; b_cv = p.cnt["vcv"]
    b_sl = p.cnt["ssl"]; b_zs = p.cnt["dzs"]
    xpre = G["xpre"]; cva = G["cv_acc"]; zst = G["z_stg"]
    cc = G["conv_c_sb"]; cb = G["conv_b_sb"]

    @block.sync
    def _(s):
        for it in range(16):
            if it >= 2:
                s.wait_ge(s_cv, b_cv + it - 1)
            s.dma_start(out=xpre[it % 2][:, 1:2049],
                        in_=G["xz_dram"].ap()[:, it * 2048:(it + 1) * 2048]).then_inc(s_xl, 16)
    p.bump("dxl", 16 * 16)

    @block.gpsimd
    def _(g):
        g.wait_ge(s_hl, p.cnt["vhl"])
        for it in range(16):
            g.wait_ge(s_xl, b_xl + 16 * (it + 1))
            if it >= 2:
                g.wait_ge(s_cv, b_cv + it - 1)
            g.tensor_copy(out=xpre[it % 2][:, 0:1], in_=G["haloL"][:, it * 3 + 2: it * 3 + 3])
            g.tensor_copy(out=xpre[it % 2][:, 2049:2051],
                          in_=G["haloR"][:, it * 3: it * 3 + 2]).then_inc(s_hc, 1)
    p.bump("ghl", 16)

    @block.vector
    def _(v):
        v.memset(G["zeros_bf"][:], 0.0)
        for it in range(16):
            v.wait_ge(s_xl, b_xl + 16 * (it + 1))
            v.wait_ge(s_hc, b_hc + it + 1)
            if it >= 2:
                v.wait_ge(s_sl, b_sl + it - 1)
            x = xpre[it % 2]
            a = cva[it % 2]
            v.tensor_scalar(a[:], x[:, 0:2048], cc[:, it * 4: it * 4 + 1], None, ALU.mult)
            for j in range(1, 4):
                v.scalar_tensor_tensor(a[:], x[:, j:j + 2048], cc[:, it * 4 + j: it * 4 + j + 1],
                                       a[:], ALU.mult, ALU.add)
            v.engine_nop().then_inc(s_cv, 1)
    p.bump("vcv", 16)

    @block.scalar
    def _(s):
        for it in range(16):
            s.wait_ge(s_cv, b_cv + it + 1)
            if it < 8:
                s.activation(out=G["u_bf"][:, it * T:(it + 1) * T], in_=cva[it % 2][:],
                             func=AF.Silu, bias=cb[:, it:it + 1]).then_inc(s_sl, 1)
            else:
                if it >= 12:
                    s.wait_ge(s_zs, b_zs + 16 * (it - 11))
                s.activation(out=zst[it % 4][:], in_=cva[it % 2][:],
                             func=AF.Silu, bias=cb[:, it:it + 1]).then_inc(s_sl, 1)
    p.bump("ssl", 16)

    @block.sync
    def _(s):
        for dt in range(8):
            it = 8 + dt
            s.wait_ge(s_sl, b_sl + it + 1)
            s.dma_start(out=G["z_dram"].ap()[:, dt * T:(dt + 1) * T],
                        in_=zst[it % 4][:]).then_inc(s_zs, 16)
    p.bump("dzs", 16 * 8)


def emit_m_xdbl(p, block, G):
    nc = p.nc
    s_sl = p.sem("ssl")
    s_x2 = p.sem("tx2"); s_xe = p.sem("sxd")
    s_dt = p.sem("tdt"); s_sab = p.sem("ssp"); s_vsp = p.sem("vsp")
    s_dd = p.sem("vdd"); s_dm = p.sem("ddm"); s_gt = p.sem("gdt"); s_pp = p.sem("spp")
    b_sl = p.cnt["ssl"] - 16
    b_x2 = p.cnt["tx2"]; b_xe = p.cnt["sxd"]; b_dt = p.cnt["tdt"]
    b_sab = p.cnt["ssp"]; b_vsp = p.cnt["vsp"]; b_dd = p.cnt["vdd"]
    b_dm = p.cnt["ddm"]; b_gt = p.cnt["gdt"]

    psx = G["ps_xd"]; psd = G["ps_dt"]
    dtl = G["dtlow_bf"]; bc = G["bc_bf"]
    u = G["u_bf"]; dlt = G["dlt_bf"]
    inv = G["invdt_sb"]; inv2 = G["inv2_sb"]

    s_iv = p.sem("viv")
    @block.vector
    def _(v):
        p.edge(v.tensor_tensor(inv2[:], inv[:], inv[:], ALU.add), v)
        v.engine_nop().then_inc(s_iv, 1)
    p.bump("viv", 1)

    @block.tensor
    def _(t):
        t.wait_ge(s_sl, b_sl + 8)
        t.wait_ge(p.sems["dmw"], p.cnt["dmw"])
        for w in range(4):
            if w >= 2:
                t.wait_ge(s_xe, b_xe + 2 * (w - 1))
            pp = psx[w % 2]
            for kt in range(DT):
                mi = t.matmul(pp[0:96, :512],
                              G["w_xd_sb"][:, kt * 96:(kt + 1) * 96],
                              u[:, kt * T + w * 512: kt * T + (w + 1) * 512],
                              start=(kt == 0), stop=(kt == DT - 1))
            mi.then_inc(s_x2, 1)
    p.bump("tx2", 4)

    @block.scalar
    def _(s):
        for w in range(4):
            s.wait_ge(s_x2, b_x2 + w + 1)
            s.copy(out=dtl[0:64, w * 512:(w + 1) * 512],
                   in_=psx[w % 2][0:64, :512]).then_inc(s_xe, 1)
            s.copy(out=bc[0:32, w * 512:(w + 1) * 512],
                   in_=psx[w % 2][64:96, :512]).then_inc(s_xe, 1)
    p.bump("sxd", 8)

    QL = [(mt, w) for mt in range(8) for w in range(4)]

    @block.tensor
    def _(t):
        t.wait_ge(s_xe, b_xe + 8)
        for i, (mt, w) in enumerate(QL):
            if i >= 2:
                t.wait_ge(s_sab, b_sab + i - 1)
                t.wait_ge(s_vsp, b_vsp + i - 1)
            t.matmul(psd[i % 2][:, :512],
                     G["w_dt_sb"][:, mt * 128:(mt + 1) * 128],
                     dtl[0:64, w * 512:(w + 1) * 512],
                     start=True, stop=True).then_inc(s_dt, 1)
    p.bump("tdt", 32)

    @block.scalar
    def _(s):
        s.wait_ge(s_iv, p.cnt["viv"])
        for i, (mt, w) in enumerate(QL):
            s.wait_ge(s_dt, b_dt + i + 1)
            if i >= 2:
                s.wait_ge(s_vsp, b_vsp + i - 1)
            ab = G["sp_ab"]
            p.edge(s.activation(out=ab[i % 2][:], in_=psd[i % 2][:, :512], func=AF.Abs,
                                bias=inv2[:, mt:mt + 1]), s)
            p.edge(s.activation(out=ab[i % 2][:], in_=ab[i % 2][:], func=AF.Exp, scale=-1.0), s)
            s.activation(out=ab[i % 2][:], in_=ab[i % 2][:], func=AF.Ln,
                         bias=1.0).then_inc(s_sab, 1)
    p.bump("ssp", 32)

    @block.vector
    def _(v):
        for i, (mt, w) in enumerate(QL):
            v.wait_ge(s_dt, b_dt + i + 1)
            rl = G["sp_rl"]
            v.tensor_scalar(rl[:], psd[i % 2][:, :512], inv2[:, mt:mt + 1], 0.0,
                            ALU.add, ALU.max)
            v.wait_ge(s_sab, b_sab + i + 1)
            v.tensor_tensor(dlt[:, mt * T + w * 512: mt * T + (w + 1) * 512],
                            rl[:], G["sp_ab"][i % 2][:], ALU.add).then_inc(s_vsp, 1)
        for mt in range(8):
            if mt >= 2:
                v.wait_ge(s_dm, b_dm + 16 * (mt - 1))
                v.wait_ge(s_gt, b_gt + mt - 1)
            v.tensor_tensor_scan(out=G["dl_stg"][mt % 2][:],
                                 data0=G["zeros_bf"][:],
                                 data1=dlt[:, mt * T:(mt + 1) * T],
                                 initial=0.0, op0=ALU.add, op1=ALU.add).then_inc(s_dd, 1)
    p.bump("vsp", 32)
    p.bump("vdd", 8)

    @block.gpsimd
    def _(g):
        for mt in range(8):
            g.wait_ge(s_dd, b_dd + mt + 1)
            g.tensor_copy(out=G["dTot_col"][:, mt:mt + 1],
                          in_=G["dl_stg"][mt % 2][:, 2047:2048]).then_inc(s_gt, 1)
    p.bump("gdt", 8)

    @block.sync
    def _(s):
        for mt in range(8):
            s.wait_ge(s_dd, b_dd + mt + 1)
            s.dma_start(out=G["dlt_dram"].ap()[:, mt * T:(mt + 1) * T],
                        in_=G["dl_stg"][mt % 2][:]).then_inc(s_dm, 16)
    p.bump("ddm", 16 * 8)

    @block.scalar
    def _(s):
        for mt in range(8):
            s.wait_ge(s_gt, b_gt + mt + 1)
            for n in range(1, NST + 1):
                mi = s.activation(out=G["p_all"][:, mt * 16 + n - 1: mt * 16 + n],
                                  in_=G["dTot_col"][:, mt:mt + 1], func=AF.Exp,
                                  scale=-float(n))
        mi.then_inc(s_pp, 1)
    p.bump("spp", 1)


def emit_m_scan(p, block, G):
    nc = p.nc
    s_bp = p.sem("tbp"); s_cp = p.sem("tcp"); s_da = p.sem("sda"); s_ce = p.sem("sce")
    s_du = p.sem("vdu"); s_sc = p.sem("vsc"); s_hc = p.sem("ghc"); s_ye = p.sem("sye")
    b_bp = p.cnt["tbp"]; b_cp = p.cnt["tcp"]; b_da = p.cnt["sda"]; b_ce = p.cnt["sce"]
    b_du = p.cnt["vdu"]; b_sc = p.cnt["vsc"]; b_hc = p.cnt["ghc"]; b_ye = p.cnt["sye"]

    psB = G["ps_B"]; psC = G["ps_C"]
    sel = G["sel_sb"]; bc = G["bc_bf"]; dlt = G["dlt_bf"]; u = G["u_bf"]
    dA = G["dA_bf"]; dBu = G["dBu_bf"]; hh = G["h_bf"]; hC = G["hC_bf"]
    du = G["du_f"]; yd = G["y_dt"]; yl = G["y_locbf"]
    QL = [(dt, n) for dt in range(8) for n in range(1, NST + 1)]

    @block.tensor
    def _(t):
        for i, (dt, n) in enumerate(QL):
            if i >= 1:
                t.wait_ge(s_du, b_du + i)
            for c in range(4):
                mi = t.matmul(psB[:, c * 512:(c + 1) * 512],
                              sel[:, (n - 1) * 128: n * 128],
                              bc[0:32, c * 512:(c + 1) * 512],
                              start=True, stop=True)
            mi.then_inc(s_bp, 1)
            if i >= 1:
                t.wait_ge(s_hc, b_hc + i)
            for c in range(4):
                mi = t.matmul(psC[:, c * 512:(c + 1) * 512],
                              sel[:, (16 + n - 1) * 128: (16 + n) * 128],
                              bc[0:32, c * 512:(c + 1) * 512],
                              start=True, stop=True)
            mi.then_inc(s_cp, 1)
    p.bump("tbp", 128)
    p.bump("tcp", 128)

    @block.scalar
    def _(s):
        for i, (dt, n) in enumerate(QL):
            if i >= 2:
                s.wait_ge(s_sc, b_sc + i - 1)
            s.activation(out=dA[i % 2][:], in_=dlt[:, dt * T:(dt + 1) * T],
                         func=AF.Exp, scale=-float(n)).then_inc(s_da, 1)
            s.wait_ge(s_cp, b_cp + i + 1)
            if i >= 2:
                s.wait_ge(s_hc, b_hc + i - 1)
            s.copy(out=G["Cbc_bf"][i % 2][:], in_=psC[:]).then_inc(s_ce, 1)
            if n == NST:
                s.wait_ge(s_hc, b_hc + i + 1)
                s.copy(out=yl[:, dt * T:(dt + 1) * T], in_=yd[:]).then_inc(s_ye, 1)
    p.bump("sda", 128)
    p.bump("sce", 128)
    p.bump("sye", 8)

    @block.vector
    def _(v):
        for i, (dt, n) in enumerate(QL):
            if n == 1:
                v.tensor_tensor(du[:], dlt[:, dt * T:(dt + 1) * T],
                                u[:, dt * T:(dt + 1) * T], ALU.mult)
            v.wait_ge(s_bp, b_bp + i + 1)
            v.tensor_tensor(dBu[i % 2][:], psB[:], du[:], ALU.mult).then_inc(s_du, 1)
            v.wait_ge(s_da, b_da + i + 1)
            if i >= 2:
                v.wait_ge(s_hc, b_hc + i - 1)
            v.tensor_tensor_scan(out=hh[i % 2][:], data0=dA[i % 2][:], data1=dBu[i % 2][:],
                                 initial=0.0, op0=ALU.mult, op1=ALU.add).then_inc(s_sc, 1)
    p.bump("vdu", 128)
    p.bump("vsc", 128)

    @block.gpsimd
    def _(g):
        for i, (dt, n) in enumerate(QL):
            g.wait_ge(s_sc, b_sc + i + 1)
            g.wait_ge(s_ce, b_ce + i + 1)
            g.tensor_copy(out=G["s_all"][:, i:i + 1], in_=hh[i % 2][:, 2047:2048])
            g.tensor_tensor(hC[:], hh[i % 2][:], G["Cbc_bf"][i % 2][:], ALU.mult)
            if n == 1:
                g.wait_ge(s_ye, b_ye + dt)
                g.tensor_copy(out=yd[:], in_=hC[:]).then_inc(s_hc, 1)
            else:
                g.tensor_tensor(yd[:], yd[:], hC[:], ALU.add).then_inc(s_hc, 1)
    p.bump("ghc", 128)


def emit_m_exchange(p, block, G):
    s_hc = p.sem("ghc"); s_pp = p.sem("spp"); s_sf = p.sem("gsf")
    s_ps = p.sem("dps"); s_cc = p.sem("cc2"); s_g2 = p.sem("dg2"); s_h0 = p.sem("vh0")

    @block.gpsimd
    def _(g):
        g.wait_ge(s_hc, p.cnt["ghc"])
        g.tensor_copy(out=G["s_fin"][:], in_=G["s_all"][:]).then_inc(s_sf, 1)
    p.bump("gsf", 1)

    @block.sync
    def _(s):
        s.wait_ge(s_sf, p.cnt["gsf"])
        s.wait_ge(s_pp, p.cnt["spp"])
        s.dma_start(out=G["cc2_in"].ap()[:, 0:128], in_=G["s_fin"][:]).then_inc(s_ps, 16)
        s.dma_start(out=G["cc2_in"].ap()[:, 128:256], in_=G["p_all"][:]).then_inc(s_ps, 16)
    p.bump("dps", 32)

    @block.gpsimd
    def _(g):
        g.wait_ge(s_ps, p.cnt["dps"])
        g.collective_compute(
            "AllGather", mybir.AluOpType.bypass,
            replica_groups=[list(range(8))],
            ins=[G["cc2_in"].ap().opt()],
            outs=[G["cc2_out"].ap().opt()],
        ).then_inc(s_cc, 1)
        g.wait_ge(s_cc, p.cnt["cc2"] + 1)
        g.dma_start(out=G["gath2_sb"][:].rearrange("p (r c) -> p r c", c=256),
                    in_=G["cc2_out"].ap().rearrange("(r p) c -> p r c", p=128)).then_inc(s_g2, 16)
    p.bump("cc2", 1)
    p.bump("dg2", 16)

    @block.vector
    def _(v):
        v.wait_ge(s_g2, p.cnt["dg2"])
        h0 = G["h0_all"]; gg = G["gath2_sb"]; ms = G["msel_sb"]
        Ps = G["psel"]; Ss = G["ssel"]
        p.edge(v.memset(h0[:], 0.0), v)
        for j in (2, 1, 0):
            p.edge(v.memset(Ps[:], 0.0), v)
            p.edge(v.memset(Ss[:], 0.0), v)
            for r in range(8):
                p.edge(v.scalar_tensor_tensor(Ps[:], gg[:, r * 256 + 128: r * 256 + 256],
                                              ms[:, j * 8 + r: j * 8 + r + 1], Ps[:],
                                              ALU.mult, ALU.add), v)
                p.edge(v.scalar_tensor_tensor(Ss[:], gg[:, r * 256: r * 256 + 128],
                                              ms[:, j * 8 + r: j * 8 + r + 1], Ss[:],
                                              ALU.mult, ALU.add), v)
            p.edge(v.tensor_tensor(h0[:], h0[:], Ps[:], ALU.mult), v)
            p.edge(v.tensor_tensor(h0[:], h0[:], Ss[:], ALU.add), v)
        v.engine_nop().then_inc(s_h0, 1)
    p.bump("vh0", 1)


def emit_m_corr(p, block, G):
    nc = p.nc
    s_h0 = p.sem("vh0")
    s_dl = p.sem("ddl"); s_wn = p.sem("swn"); s_c2 = p.sem("tc2")
    s_ac = p.sem("vac"); s_fn = p.sem("gfn"); s_yw = p.sem("dyw")
    b_dl = p.cnt["ddl"]; b_wn = p.cnt["swn"]; b_c2 = p.cnt["tc2"]
    b_ac = p.cnt["vac"]; b_fn = p.cnt["gfn"]

    psC = G["ps_c2"]; sel = G["sel_sb"]; bc = G["bc_bf"]
    dl = G["dl_t"]; wn = G["wn_bf"]; acc = G["acc_f"]; t1 = G["t1_bf"]
    h0 = G["h0_all"]; u = G["u_bf"]; yl = G["y_locbf"]
    NSEQ = [(dt, n) for dt in range(8) for n in range(1, NST + 1)]

    @block.sync
    def _(s):
        for dt in range(8):
            if dt >= 2:
                s.wait_ge(s_wn, b_wn + 16 * (dt - 1))
            s.dma_start(out=dl[dt % 2][:],
                        in_=G["dlt_dram"].ap()[:, dt * T:(dt + 1) * T]).then_inc(s_dl, 16)
    p.bump("ddl", 16 * 8)

    @block.scalar
    def _(s):
        for i, (dt, n) in enumerate(NSEQ):
            if n == 1:
                s.wait_ge(s_dl, b_dl + 16 * (dt + 1))
            if i >= 2:
                s.wait_ge(s_ac, b_ac + i - 1)
            s.activation(out=wn[i % 2][:], in_=dl[dt % 2][:], func=AF.Exp,
                         scale=-float(n)).then_inc(s_wn, 1)
    p.bump("swn", 128)

    @block.tensor
    def _(t):
        for i, (dt, n) in enumerate(NSEQ):
            if i >= 2:
                t.wait_ge(s_ac, b_ac + i - 1)
            for c in range(4):
                mi = t.matmul(psC[i % 2][:, c * 512:(c + 1) * 512],
                              sel[:, (16 + n - 1) * 128: (16 + n) * 128],
                              bc[0:32, c * 512:(c + 1) * 512],
                              start=True, stop=True)
            mi.then_inc(s_c2, 1)
    p.bump("tc2", 128)

    @block.vector
    def _(v):
        v.wait_ge(s_h0, p.cnt["vh0"])
        for i, (dt, n) in enumerate(NSEQ):
            v.wait_ge(s_c2, b_c2 + i + 1)
            v.wait_ge(s_wn, b_wn + i + 1)
            v.tensor_tensor(t1[:], psC[i % 2][:], wn[i % 2][:], ALU.mult)
            col = h0[:, dt * 16 + n - 1: dt * 16 + n]
            if n == 1:
                v.scalar_tensor_tensor(acc[:], t1[:], col, G["zeros_bf"][:],
                                       ALU.mult, ALU.add).then_inc(s_ac, 1)
            else:
                v.scalar_tensor_tensor(acc[:], t1[:], col, acc[:],
                                       ALU.mult, ALU.add).then_inc(s_ac, 1)
            if n == NST:
                v.scalar_tensor_tensor(G["tmp_f"][:], u[:, dt * T:(dt + 1) * T],
                                       G["d_col_sb"][:, dt:dt + 1], acc[:],
                                       ALU.mult, ALU.add)
                v.tensor_tensor(yl[:, dt * T:(dt + 1) * T], yl[:, dt * T:(dt + 1) * T],
                                G["tmp_f"][:], ALU.add).then_inc(s_fn, 1)
    p.bump("vac", 128)
    p.bump("gfn", 8)

    @block.sync
    def _(s):
        for dt in range(8):
            s.wait_ge(s_fn, b_fn + dt + 1)
            s.dma_start(out=G["y_dram"].ap()[:, dt * T:(dt + 1) * T],
                        in_=yl[:, dt * T:(dt + 1) * T]).then_inc(s_yw, 16)
    p.bump("dyw", 16 * 8)


def emit_m_out(p, block, G):
    nc = p.nc
    s_ld = p.sem("dm8"); s_mw = p.sem("dmw")
    s_om = p.sem("tom"); s_oe = p.sem("voe")
    b_om = p.cnt["tom"]; b_oe = p.cnt["voe"]

    @block.sync
    def _(s):
        s.dma_start(out=G["x_res"][:], in_=G["xres_dram"].ap()).then_inc(s_ld, 16)
        s.dma_start(out=G["y_sb"][:], in_=G["y_dram"].ap()).then_inc(s_ld, 16)
        s.dma_start(out=G["z_sb"][:], in_=G["z_dram"].ap()).then_inc(s_ld, 16)
    p.bump("dm8", 16 * 3)

    @block.gpsimd
    def _(g):
        g.dma_start(out=G["w_out_sb"][:], in_=G["in_woutT"]).then_inc(s_mw, 16)
        g.dma_start(out=G["bout_bf"][:], in_=G["in_bout"]).then_inc(s_mw, 16)
    p.bump("dmw", 16 * 2)

    ps = G["ps_o"]

    @block.tensor
    def _(t):
        t.wait_ge(s_ld, p.cnt["dm8"])
        t.wait_ge(s_mw, p.cnt["dmw"])
        for tt in range(NTT):
            if tt >= 2:
                t.wait_ge(s_oe, b_oe + tt - 1)
            pp = ps[tt % 2]
            for dh in range(2):
                t.matmul(pp[:, dh * 512:(dh + 1) * 512],
                         G["ones_bf"][0:1, 0:128],
                         G["bout_bf"][0:1, dh * 512:(dh + 1) * 512],
                         start=True, stop=False)
                for kt in range(16):
                    src = G["y_sb"] if kt < 8 else G["z_sb"]
                    mi = t.matmul(pp[:, dh * 512:(dh + 1) * 512],
                                  src[:, (kt % 8) * T + tt * 128: (kt % 8) * T + (tt + 1) * 128],
                                  G["w_out_sb"][:, kt * 1024 + dh * 512: kt * 1024 + (dh + 1) * 512],
                                  start=False, stop=(kt == 15))
            mi.then_inc(s_om, 1)
    p.bump("tom", NTT)

    @block.vector
    def _(v):
        v.wait_ge(s_ld, p.cnt["dm8"])
        for tt in range(NTT):
            v.wait_ge(s_om, b_om + tt + 1)
            xt = G["x_res"][:, tt * D:(tt + 1) * D]
            v.tensor_tensor(xt, ps[tt % 2][:, :], xt, ALU.add).then_inc(s_oe, 1)
    p.bump("voe", NTT)


def emit_output_dram(p, block, G):
    s_st = p.sem("dst")

    @block.sync
    def _(s):
        for nm, c in sorted(p.cnt.items()):
            if c > 0 and nm != "dst":
                s.wait_ge(p.sems[nm], c)
        src = G["xres_dram"].ap().rearrange("p (tt d) -> p tt d", d=D)
        dst = G["in_out"].rearrange("(tt p) d -> p tt d", p=128)
        s.dma_start(out=dst, in_=src).then_inc(s_st, 16)
        s.wait_ge(s_st, 16)
    p.bump("dst", 16)


def emit_output(p, block, G):
    s_st = p.sem("dst")
    s_ev = p.sem("vev")
    dma_final = [(nm, c) for nm, c in sorted(p.cnt.items())
                 if nm.startswith("d") and nm != "dst" and c > 0]

    @block.sync
    def _(s):
        s.wait_ge(s_ev, p.cnt["vev"])
        for nm, c in dma_final:
            s.wait_ge(p.sems[nm], c)
        src = G["x_res"][:].rearrange("p (tt d) -> p tt d", d=D)
        dst = G["in_out"].rearrange("(tt p) d -> p tt d", p=128)
        s.dma_start(out=dst, in_=src).then_inc(s_st, 16)
        s.wait_ge(s_st, 16)
    p.bump("dst", 16)


def alloc_ln(p, G, scope, tag):
    G["ln_tok"] = p.sbuf(f"ln_tok{tag}", [128, 2 * D], BF16, scope)
    G["ln_f"] = p.sbuf(f"ln_f{tag}", [128, DT * T], BF16, scope)
    for nm, w in [("stats", 16), ("s1_all", 16), ("s2_all", 16), ("mean_all", 16),
                  ("var_all", 16), ("rstd_all", 16), ("nmr_all", 16)]:
        G[nm] = p.sbuf(f"{nm}{tag}", [128, w], F32, scope)
    G["sq"] = p.sbuf(f"sq{tag}", [128, D], F32, scope)


def alloc_router(p, G, scope, tag):
    G["cwall"] = p.sbuf(f"cwall{tag}", [128, NTT * 8], F32, scope)
    G["cw2all"] = p.sbuf(f"cw2all{tag}", [128, NTT * 8], F32, scope)
    G["comb_w"] = p.sbuf(f"comb_w{tag}", [128, 8], F32, scope)
    G["comb_w2"] = p.sbuf(f"comb_w2{tag}", [128, 8], F32, scope)
    G["c_m"] = p.sbuf(f"c_m{tag}", [128, 4], F32, scope)
    G["comb_half"] = p.sbuf(f"comb_half{tag}", [128, NTT * 9], F32, scope)


def emit_ffn_pass(p, block, G, scope, pass_idx, tag):
    alloc_ln(p, G, scope, tag)
    alloc_router(p, G, scope, tag)
    G["h1q"] = p.sbuf(f"h1q{tag}", [128, 8 * 1024], BF16, scope)
    G["pool_A"] = p.sbuf(f"pool_A{tag}", [128, 2 * 8192], BF16, scope)
    G["pool_B"] = p.sbuf(f"pool_B{tag}", [128, 2 * 8192], BF16, scope)
    with contextlib.ExitStack() as lnscope:
        G["ps_lg"] = [p.psum(f"ps_lg0{tag}", [128, 8], F32, lnscope),
                      p.psum(f"ps_lg1{tag}", [128, 8], F32, lnscope)]
        emit_ln_router(p, block, G, pass_idx, with_router=True)
        barrier(p, block, tensor_done=("tlg", p.cnt["tlg"]))
    with contextlib.ExitStack() as exscope:
        G["ps_y"] = [p.psum(f"ps_y{k}{tag}", [128, 1024], F32, exscope) for k in range(2)]
        G["ps_h1"] = [p.psum(f"ps_h1{k}{tag}", [128, 1024], F32, exscope) for k in range(2)]
        emit_ffn_experts(p, block, G, pass_idx)
        barrier(p, block, tensor_done=("tmm2", p.cnt["tmm2"]))


def build(stage=STAGE):
    p = Prog()
    nc = p.nc
    G = {}

    G["in_x"] = nc.dram_tensor("x_sh", [T, D], F32, kind="ExternalInput").ap()
    G["in_w1T"] = nc.dram_tensor("w1T", [NE, D, FF], F32, kind="ExternalInput").ap()
    G["in_w2T"] = nc.dram_tensor("w2T", [NE, FF, D], F32, kind="ExternalInput").ap()
    G["in_b1col"] = nc.dram_tensor("b1col", [128, 32 * 9], F32, kind="ExternalInput").ap()
    G["in_b2r"] = nc.dram_tensor("b2row", [1, 9 * D], F32, kind="ExternalInput").ap()
    G["in_ones"] = nc.dram_tensor("ones_r", [1, 128], F32, kind="ExternalInput").ap()
    G["in_rw"] = nc.dram_tensor("rw_prep", [128, 64], F32, kind="ExternalInput").ap()
    G["in_rb"] = nc.dram_tensor("rb_bc", [128, 8], F32, kind="ExternalInput").ap()
    G["in_out"] = nc.dram_tensor("out", [T, D], F32, kind="ExternalOutput").ap()
    G["w1c"] = nc.dram_tensor("w1c", [NE, NQ, 128, 8192], BF16, kind="Internal")
    G["lnd"] = nc.dram_tensor("lnd", [T, D], BF16, kind="Internal")
    G["w2c"] = nc.dram_tensor("w2c", [NE, NQ, 128, 8192], BF16, kind="Internal")
    G["in_winT"] = nc.dram_tensor("winT", [128, 8 * 2048], F32, kind="ExternalInput").ap()
    G["in_wxdT"] = nc.dram_tensor("wxdT", [128, 8 * 96], F32, kind="ExternalInput").ap()
    G["in_wdtT"] = nc.dram_tensor("wdtT", [64, 1024], F32, kind="ExternalInput").ap()
    G["in_woutT"] = nc.dram_tensor("woutT", [128, 16 * 1024], F32, kind="ExternalInput").ap()
    G["in_sel"] = nc.dram_tensor("sel32", [32, 32 * 128], F32, kind="ExternalInput").ap()
    G["in_bout"] = nc.dram_tensor("bout_r", [1, 1024], F32, kind="ExternalInput").ap()
    G["in_convc"] = nc.dram_tensor("convc", [128, 64], F32, kind="ExternalInput").ap()
    G["in_convb"] = nc.dram_tensor("convb", [128, 16], F32, kind="ExternalInput").ap()
    G["in_invdt"] = nc.dram_tensor("invdtc", [128, 8], F32, kind="ExternalInput").ap()
    G["in_dcol"] = nc.dram_tensor("dcolc", [128, 8], F32, kind="ExternalInput").ap()
    G["in_msel"] = nc.dram_tensor("mselc", [128, 24], F32, kind="ExternalInput").ap()
    G["in_msel2"] = nc.dram_tensor("msel2c", [128, 16], F32, kind="ExternalInput").ap()
    G["xz_dram"] = nc.dram_tensor("xz_d", [128, 16 * 2048], BF16, kind="Internal")
    G["z_dram"] = nc.dram_tensor("z_d", [128, 8 * 2048], BF16, kind="Internal")
    G["dlt_dram"] = nc.dram_tensor("dlt_d", [128, 8 * 2048], BF16, kind="Internal")
    G["y_dram"] = nc.dram_tensor("y_d", [128, 8 * 2048], BF16, kind="Internal")
    G["xres_dram"] = nc.dram_tensor("xres_d", [128, NTT * D], F32, kind="Internal")
    G["cc1_in"] = nc.dram_tensor("cc1_in", [128, 48], BF16, kind="Internal")
    G["cc1_out"] = nc.dram_tensor("cc1_out", [8 * 128, 48], BF16, kind="Internal")
    G["cc2_in"] = nc.dram_tensor("cc2_in", [128, 256], F32, kind="Internal")
    G["cc2_out"] = nc.dram_tensor("cc2_out", [8 * 128, 256], F32, kind="Internal")

    p.setup_consts([1e-5, -0.5, 1.0] + [-(n + 1.0) for n in range(16)])
    G["b1col"] = p.sbuf("b1col_sb", [128, 32 * 9], F32)
    G["b2row_bf"] = p.sbuf("b2rowbf", [1, 9 * D], BF16)
    G["ones_bf"] = p.sbuf("onesbf", [1, 128], BF16)
    G["rw_bf"] = p.sbuf("rwbf", [128, 64], BF16)
    G["rb_bc"] = p.sbuf("rbbc", [128, 8], F32)
    G["haloL"] = p.sbuf("haloL", [128, 48], BF16)
    G["haloR"] = p.sbuf("haloR", [128, 48], BF16)
    G["conv_c_sb"] = p.sbuf("convc_sb", [128, 64], F32)
    G["conv_b_sb"] = p.sbuf("convb_sb", [128, 16], F32)
    G["invdt_sb"] = p.sbuf("invdt_sb", [128, 8], F32)
    G["inv2_sb"] = p.sbuf("inv2_sb", [128, 8], F32)
    G["d_col_sb"] = p.sbuf("dcol_sb", [128, 8], F32)
    G["msel_sb"] = p.sbuf("msel_sb", [128, 24], F32)
    G["msel2_sb"] = p.sbuf("msel2_sb", [128, 16], F32)

    s_ldx = p.sem("dldx")
    s_ldxg = p.sem("dldxg")

    with nc.Block() as block:
        @block.gpsimd
        def _(g):
            g.dma_start(out=G["b2row_bf"][:], in_=G["in_b2r"]).then_inc(s_ldxg, 16)
            g.dma_start(out=G["ones_bf"][:], in_=G["in_ones"]).then_inc(s_ldxg, 16)
            g.dma_start(out=G["rw_bf"][:], in_=G["in_rw"]).then_inc(s_ldxg, 16)
        p.bump("dldxg", 16 * 3)

        s_vc = p.sem("vconst")
        done = False

        with contextlib.ExitStack() as sA:
            G["x_res"] = p.sbuf("x_res_a", [128, NTT * D], F32, sA)

            @block.sync
            def _(s):
                s.dma_start(out=G["x_res"][:].rearrange("p (tt d) -> p tt d", d=D),
                            in_=G["in_x"].rearrange("(tt p) d -> p tt d", p=128)).then_inc(s_ldx, 16)
                for dst, src in [("b1col", "in_b1col"), ("rb_bc", "in_rb"),
                                 ("conv_c_sb", "in_convc"), ("conv_b_sb", "in_convb"),
                                 ("invdt_sb", "in_invdt"), ("d_col_sb", "in_dcol"),
                                 ("msel_sb", "in_msel"), ("msel2_sb", "in_msel2")]:
                    s.dma_start(out=G[dst][:], in_=G[src]).then_inc(s_ldx, 16)
            p.bump("dldx", 16 * 9)

            @block.vector
            def _(v):
                v.wait_ge(s_ldx, p.cnt["dldx"])
                v.wait_ge(s_ldxg, p.cnt["dldxg"])
                p.emit_const_init(v)
                v.engine_nop().then_inc(s_vc, 1)
            p.bump("vconst")

            @block.tensor
            def _(t):
                t.wait_ge(s_ldx, p.cnt["dldx"])
                t.wait_ge(s_ldxg, p.cnt["dldxg"])
                t.wait_ge(s_vc, p.cnt["vconst"])

            @block.scalar
            def _(s):
                s.wait_ge(s_ldx, p.cnt["dldx"])
                s.wait_ge(s_ldxg, p.cnt["dldxg"])
                s.wait_ge(s_vc, p.cnt["vconst"])

            with contextlib.ExitStack() as f1:
                emit_ffn_pass(p, block, G, f1, 0, "_a")
            mbar(p, block, tensor_done=("tmm2", p.cnt["tmm2"]),
                 dma_waits=[("dcache", p.cnt["dcache"])])

            if stage == "ffn1":
                emit_output(p, block, G)
                done = True

            if stage == "mnull":
                s_sp = p.sem("dsp")

                @block.sync
                def _(s):
                    s.wait_ge(p.sems["vev"], p.cnt["vev"])
                    s.dma_start(out=G["xres_dram"].ap(), in_=G["x_res"][:]).then_inc(s_sp, 16)
                p.bump("dsp", 16)
                emit_output_dram(p, block, G)
                done = True

            if not done:
                with contextlib.ExitStack() as mln:
                    alloc_ln(p, G, mln, "_m")
                    G["w_in_sb"] = p.sbuf("w_in_sb", [128, 8 * 2048], BF16, mln)
                    G["xz_stg"] = [p.sbuf(f"xz_stg{k}", [128, 512], BF16, mln) for k in range(2)]
                    G["pay_sb"] = p.sbuf("pay_sb", [128, 48], BF16, mln)
                    G["gath1_sb"] = p.sbuf("gath1_sb", [128, 8 * 48], BF16, mln)
                    G["ps_xz"] = [p.psum(f"ps_xz{k}", [128, 512], F32, mln) for k in range(2)]
                    emit_ln_router(p, block, G, 2, with_router=False)
                    if stage == "mln":
                        s_sp = p.sem("dsp")

                        @block.sync
                        def _(s):
                            s.wait_ge(p.sems["vln"], p.cnt["vln"])
                            s.dma_start(out=G["xres_dram"].ap(),
                                        in_=G["x_res"][:]).then_inc(s_sp, 16)
                        p.bump("dsp", 16)
                        emit_output_dram(p, block, G)
                        done = True
                    if not done:
                        mp_parts = {"mp_w": ("w",), "mp_mm": ("w", "mm")}.get(
                            stage, ("w", "mm", "pay"))
                        emit_m_proj(p, block, G, parts=mp_parts)
                    if stage in ("mproj", "mp_w", "mp_mm"):
                        emit_output_dram(p, block, G)
                        done = True
                    if not done:
                        emit_m_halo(p, block, G)
                        if stage == "mhalo":
                            emit_output_dram(p, block, G)
                            done = True
                    if not done:
                        mbar(p, block, tensor_done=("txz", p.cnt["txz"]),
                             dma_waits=[("dsp", p.cnt["dsp"]), ("dxs", p.cnt["dxs"]),
                                        ("dg1", p.cnt["dg1"]), ("dpy", p.cnt["dpy"])])

        if not done:
            with contextlib.ExitStack() as s2:
                G["u_bf"] = p.sbuf("u_bf", [128, 8 * T], BF16, s2)
                G["y_locbf"] = p.sbuf("y_locbf", [128, 8 * T], BF16, s2)
                G["dlt_bf"] = p.sbuf("dlt_bf", [128, 8 * T], BF16, s2)
                G["bc_bf"] = p.sbuf("bc_bf", [32, T], BF16, s2)
                G["dtlow_bf"] = p.sbuf("dtlow_bf", [64, T], BF16, s2)
                G["sel_sb"] = p.sbuf("sel_sb", [32, 32 * 128], BF16, s2)
                G["zeros_bf"] = p.sbuf("zeros_bf", [128, T], BF16, s2)
                G["w_xd_sb"] = p.sbuf("w_xd_sb", [128, 8 * 96], BF16, s2)
                G["w_dt_sb"] = p.sbuf("w_dt_sb", [64, 1024], BF16, s2)
                G["s_all"] = p.sbuf("s_all", [128, 128], F32, s2)
                G["s_fin"] = p.sbuf("s_fin", [128, 128], F32, s2)
                G["p_all"] = p.sbuf("p_all", [128, 128], F32, s2)
                G["dTot_col"] = p.sbuf("dTot_col", [128, 8], F32, s2)

                s_mw = p.sem("dmw")

                @block.gpsimd
                def _(g):
                    g.dma_start(out=G["w_xd_sb"][:], in_=G["in_wxdT"]).then_inc(s_mw, 16)
                    g.dma_start(out=G["w_dt_sb"][:], in_=G["in_wdtT"]).then_inc(s_mw, 16)
                    g.dma_start(out=G["sel_sb"][:], in_=G["in_sel"]).then_inc(s_mw, 16)
                p.bump("dmw", 16 * 3)

                with contextlib.ExitStack() as s2a:
                    G["xpre"] = [p.sbuf(f"xpre{k}", [128, 2051], BF16, s2a) for k in range(2)]
                    G["cv_acc"] = [p.sbuf(f"cvacc{k}", [128, T], F32, s2a) for k in range(2)]
                    G["z_stg"] = [p.sbuf(f"z_stg{k}", [128, T], BF16, s2a) for k in range(4)]
                    emit_m_conv(p, block, G)
                    if stage == "mconv":
                        emit_output_dram(p, block, G)
                        done = True
                    if not done:
                        mbar(p, block, dma_waits=[("dzs", p.cnt["dzs"]), ("dxl", p.cnt["dxl"])])

                if not done:
                    with contextlib.ExitStack() as s2b:
                        G["sp_ab"] = [p.sbuf(f"sp_ab{k}", [128, 512], F32, s2b) for k in range(2)]
                        G["sp_rl"] = p.sbuf("sp_rl", [128, 512], F32, s2b)
                        G["dl_stg"] = [p.sbuf(f"dl_stg{k}", [128, T], BF16, s2b) for k in range(2)]
                        G["ps_xd"] = [p.psum(f"ps_xd{k}", [128, 512], F32, s2b) for k in range(2)]
                        G["ps_dt"] = [p.psum(f"ps_dt{k}", [128, 512], F32, s2b) for k in range(2)]
                        emit_m_xdbl(p, block, G)
                        if stage == "mxdbl":
                            emit_output_dram(p, block, G)
                            done = True
                        if not done:
                            mbar(p, block, tensor_done=("tdt", p.cnt["tdt"]),
                                 dma_waits=[("ddm", p.cnt["ddm"])])

                if not done:
                    with contextlib.ExitStack() as s2c:
                        G["dA_bf"] = [p.sbuf(f"dA_bf{k}", [128, T], BF16, s2c) for k in range(2)]
                        G["dBu_bf"] = [p.sbuf(f"dBu_bf{k}", [128, T], BF16, s2c) for k in range(2)]
                        G["h_bf"] = [p.sbuf(f"h_bf{k}", [128, T], BF16, s2c) for k in range(2)]
                        G["hC_bf"] = p.sbuf("hC_bf", [128, T], BF16, s2c)
                        G["Cbc_bf"] = [p.sbuf(f"Cbc_bf{k}", [128, T], BF16, s2c) for k in range(2)]
                        G["du_f"] = p.sbuf("du_f", [128, T], F32, s2c)
                        G["y_dt"] = p.sbuf("y_dt", [128, T], F32, s2c)
                        G["ps_B"] = p.psum("ps_B", [128, 2048], F32, s2c)
                        G["ps_C"] = p.psum("ps_C", [128, 2048], F32, s2c)
                        emit_m_scan(p, block, G)
                        if stage == "mscan":
                            emit_output_dram(p, block, G)
                            done = True
                        if not done:
                            mbar(p, block, tensor_done=("tcp", p.cnt["tcp"]))

                if not done:
                    with contextlib.ExitStack() as s2d:
                        G["gath2_sb"] = p.sbuf("gath2_sb", [128, 8 * 256], F32, s2d)
                        G["h0_all"] = p.sbuf("h0_all", [128, 128], F32, s2d)
                        G["psel"] = p.sbuf("psel", [128, 128], F32, s2d)
                        G["ssel"] = p.sbuf("ssel", [128, 128], F32, s2d)
                        G["dl_t"] = [p.sbuf(f"dl_t{k}", [128, T], BF16, s2d) for k in range(2)]
                        G["wn_bf"] = [p.sbuf(f"wn_bf{k}", [128, T], BF16, s2d) for k in range(2)]
                        G["t1_bf"] = p.sbuf("t1_bf", [128, T], BF16, s2d)
                        G["acc_f"] = p.sbuf("acc_f", [128, T], F32, s2d)
                        G["tmp_f"] = p.sbuf("tmp_f", [128, T], F32, s2d)
                        G["ps_c2"] = [p.psum(f"ps_c2{k}", [128, 2048], F32, s2d) for k in range(2)]
                        emit_m_exchange(p, block, G)
                        if stage == "mexch":
                            emit_output_dram(p, block, G)
                            done = True
                        if not done:
                            emit_m_corr(p, block, G)
                            if stage == "mcorr":
                                emit_output_dram(p, block, G)
                                done = True
                        if not done:
                            mbar(p, block, tensor_done=("tc2", p.cnt["tc2"]),
                                 dma_waits=[("dyw", p.cnt["dyw"])])

        if not done:
            G["x_res"] = p.sbuf("x_res_b", [128, NTT * D], F32)
            with contextlib.ExitStack() as s3a:
                G["y_sb"] = p.sbuf("y_sb", [128, 8 * T], BF16, s3a)
                G["z_sb"] = p.sbuf("z_sb", [128, 8 * T], BF16, s3a)
                G["w_out_sb"] = p.sbuf("w_out_sb", [128, 16 * 1024], BF16, s3a)
                G["bout_bf"] = p.sbuf("bout_bf", [1, 1024], BF16, s3a)
                G["ps_o"] = [p.psum(f"ps_o{k}", [128, 1024], F32, s3a) for k in range(2)]
                emit_m_out(p, block, G)
                mbar(p, block, tensor_done=("tom", p.cnt["tom"]))

            if stage == "mamba":
                emit_output(p, block, G)
                done = True

            if not done:
                with contextlib.ExitStack() as f2:
                    emit_ffn_pass(p, block, G, f2, 1, "_b")
                emit_output(p, block, G)

    p.stack.close()
    return p, G


_CACHE = {}


def _host_prep(inputs):
    f32 = np.float32
    L = {}
    w1 = [np.ascontiguousarray(inputs["L0_ffn_shared_w1"].T)] + \
         [np.ascontiguousarray(inputs["L0_ffn_spec_w1"][e].T) for e in range(8)]
    L["w1T"] = np.stack(w1).astype(f32, copy=False)
    w2 = [np.ascontiguousarray(inputs["L0_ffn_shared_w2"].T)] + \
         [np.ascontiguousarray(inputs["L0_ffn_spec_w2"][e].T) for e in range(8)]
    L["w2T"] = np.stack(w2).astype(f32, copy=False)
    b1 = np.stack([inputs["L0_ffn_shared_b1"]] + [inputs["L0_ffn_spec_b1"][e] for e in range(8)])
    L["b1col"] = np.ascontiguousarray(b1.reshape(9, 32, 128).transpose(2, 1, 0).reshape(128, 32 * 9)).astype(f32)
    L["b2row"] = np.ascontiguousarray(
        np.stack([inputs["L0_ffn_shared_b2"]] + [inputs["L0_ffn_spec_b2"][e] for e in range(8)]).reshape(1, -1)).astype(f32)
    L["ones_r"] = np.ones((1, 128), dtype=f32)
    rw = inputs["L0_ffn_router_w"]
    L["rw_prep"] = np.ascontiguousarray(rw.reshape(8, 8, 128).transpose(2, 1, 0).reshape(128, 64)).astype(f32)
    L["rb_bc"] = np.ascontiguousarray(np.tile(inputs["L0_ffn_router_b"][None, :], (128, 1))).astype(f32)

    L["winT"] = np.ascontiguousarray(
        inputs["L0_attn_W_in"].reshape(8, 128, 2048).transpose(1, 0, 2).reshape(128, 8 * 2048)).astype(f32)
    L["wxdT"] = np.ascontiguousarray(
        inputs["L0_attn_W_xdbl"].reshape(8, 128, 96).transpose(1, 0, 2).reshape(128, 8 * 96)).astype(f32)
    L["wdtT"] = np.ascontiguousarray(inputs["L0_attn_W_dt"]).astype(f32)
    L["woutT"] = np.ascontiguousarray(
        inputs["L0_attn_W_out"].reshape(16, 128, 1024).transpose(1, 0, 2).reshape(128, 16 * 1024)).astype(f32)
    L["sel32"] = np.ascontiguousarray(np.kron(np.eye(32, dtype=f32), np.ones((1, 128), f32)))
    L["bout_r"] = np.ascontiguousarray(inputs["L0_attn_b_out"].reshape(1, 1024)).astype(f32)
    cxw = inputs["L0_attn_conv_x_w"].reshape(4, 1024)
    czw = inputs["L0_attn_conv_z_w"].reshape(4, 1024)
    cc = np.empty((128, 64), f32)
    for dt in range(8):
        for j in range(4):
            cc[:, dt * 4 + j] = cxw[j, dt * 128:(dt + 1) * 128]
            cc[:, 32 + dt * 4 + j] = czw[j, dt * 128:(dt + 1) * 128]
    L["convc"] = cc
    cb = np.empty((128, 16), f32)
    for dt in range(8):
        cb[:, dt] = inputs["L0_attn_conv_x_b"][dt * 128:(dt + 1) * 128]
        cb[:, 8 + dt] = inputs["L0_attn_conv_z_b"][dt * 128:(dt + 1) * 128]
    L["convb"] = cb
    L["invdtc"] = np.ascontiguousarray(
        inputs["L0_attn_inv_dt"].reshape(8, 128).T).astype(f32)
    L["dcolc"] = np.ascontiguousarray(inputs["L0_attn_D"].reshape(8, 128).T).astype(f32)
    return L


def _core_masks(core):
    b, c = core // 4, core % 4
    f32 = np.float32
    msel = np.zeros((1, 24), f32)
    for j in range(3):
        cc = c - 1 - j
        if cc >= 0:
            msel[0, j * 8 + (b * 4 + cc)] = 1.0
    msel2 = np.zeros((1, 16), f32)
    if c > 0:
        msel2[0, b * 4 + c - 1] = 1.0
    if c < 3:
        msel2[0, 8 + b * 4 + c + 1] = 1.0
    return (np.ascontiguousarray(np.tile(msel, (128, 1))),
            np.ascontiguousarray(np.tile(msel2, (128, 1))))


def kernel(**inputs):
    key = ("prog", STAGE)
    if key not in _CACHE:
        _CACHE[key] = build(STAGE)
    p, G = _CACHE[key]

    L = _host_prep(inputs)
    x = np.asarray(inputs["L0_attn_x"], dtype=np.float32)

    in_maps = []
    for core in range(8):
        b, c = core // 4, core % 4
        msel, msel2 = _core_masks(core)
        m = {
            "x_sh": np.ascontiguousarray(x[b, c * T:(c + 1) * T, :]),
            "w1T": L["w1T"], "w2T": L["w2T"], "b1col": L["b1col"], "b2row": L["b2row"],
            "ones_r": L["ones_r"],
            "rw_prep": L["rw_prep"], "rb_bc": L["rb_bc"],
            "winT": L["winT"], "wxdT": L["wxdT"], "wdtT": L["wdtT"],
            "woutT": L["woutT"], "sel32": L["sel32"], "bout_r": L["bout_r"],
            "convc": L["convc"], "convb": L["convb"], "invdtc": L["invdtc"],
            "dcolc": L["dcolc"], "mselc": msel, "msel2c": msel2,
        }
        in_maps.append(m)

    res = run_bass_kernel_spmd(p.nc, in_maps, core_ids=list(range(8)))
    out = np.empty((2, 8192, D), dtype=np.float32)
    for core in range(8):
        b, c = core // 4, core % 4
        out[b, c * T:(c + 1) * T, :] = res.results[core]["out"]
    return out


# version 05
# speedup vs baseline: 1.3773x; 1.0508x over previous
import contextlib
import os
import numpy as np
import concourse.bass as bass
import concourse.mybir as mybir
from concourse.bass_utils import run_bass_kernel_spmd

F32 = mybir.dt.float32
BF16 = mybir.dt.bfloat16
AF = mybir.ActivationFunctionType
ALU = mybir.AluOpType
AX = mybir.AxisListType.X

D = 1024
T = 2048
NTT = 16
DT = 8
FF = 4096
NE = 9
NQ = 4
FQ = 1024
FT = 8
WINDOWS = [(0, 384), (384, 384), (768, 384), (1152, 384), (1536, 384), (1920, 128)]
NW = len(WINDOWS)
NQTOT_PASS = NW * NE * NQ

STAGE = "full"


class Prog:
    def __init__(self):
        self.nc = bass.Bass("TRN2", target_bir_lowering=False, debug=False,
                            detect_race_conditions=False)
        self.stack = contextlib.ExitStack()
        self.sems = {}
        self.cnt = {}
        self.nbar = 0

    def sem(self, name):
        if name not in self.sems:
            self.sems[name] = self.stack.enter_context(self.nc.semaphore(name))
            self.cnt[name] = 0
        return self.sems[name]

    def bump(self, name, n=1):
        self.cnt[name] += n
        return self.cnt[name]

    def sbuf(self, name, shape, dt=F32, scope=None):
        st = scope if scope is not None else self.stack
        return st.enter_context(self.nc.sbuf_tensor(name, shape, dt))

    def psum(self, name, shape, dt=F32, scope=None):
        st = scope if scope is not None else self.stack
        return st.enter_context(self.nc.psum_tensor(name, shape, dt))

    def edge(self, inst, eng):
        se = self.sem("selfe")
        n = self.bump("selfe")
        inst.then_inc(se, 1)
        eng.wait_ge(se, n)

    def setup_consts(self, values):
        self.const_vals = list(values)
        self.const_sb = self.sbuf("const_sb", [128, max(1, len(values))], F32)
        for j, v in enumerate(self.const_vals):
            self.nc.const_aps.register(F32, float(v), self.const_sb[:, j:j + 1])

    def emit_const_init(self, v):
        for j, val in enumerate(self.const_vals):
            self.nc.vector.memset(self.const_sb[:, j:j + 1], float(val)) if False else \
                v.memset(self.const_sb[:, j:j + 1], float(val))


def barrier(p, block, tensor_done=None):
    p.nbar += 1
    tgt = 2 * p.nbar
    s = p.sem("bar")

    @block.scalar
    def _(e):
        if tensor_done is not None:
            e.wait_ge(p.sems[tensor_done[0]], tensor_done[1])
        e.nop().then_inc(s, 1)
        e.wait_ge(s, tgt)

    @block.vector
    def _(e):
        if tensor_done is not None:
            e.wait_ge(p.sems[tensor_done[0]], tensor_done[1])
        e.nop().then_inc(s, 1)
        e.wait_ge(s, tgt)

    @block.tensor
    def _(e):
        e.wait_ge(s, tgt)


def emit_ln_router(p, block, G, pass_idx, with_router, sub=99):
    nc = p.nc
    s_vln = p.sem("vln"); s_lnf = p.sem("dlnf"); s_var = p.sem("vvar"); s_rstd = p.sem("srstd")
    b_vln = p.cnt["vln"]; b_lnf = p.cnt["dlnf"]


    xr = G["x_res"]
    ln_tok = G["ln_tok"]
    ln_f = G["ln_f"]
    st = G["stats"]
    mean_all = G["mean_all"]; var_all = G["var_all"]
    rstd_all = G["rstd_all"]; nmr_all = G["nmr_all"]
    sq = G["sq"]
    s1_all = G["s1_all"]; s2_all = G["s2_all"]

    @block.vector
    def _(v):
        for tt in range(NTT):
            xt = xr[:, tt * D:(tt + 1) * D]
            v.tensor_tensor(sq[:], xt, xt, ALU.mult)
            v.reduce_sum(s1_all[:, tt:tt + 1], xt, AX)
            v.reduce_sum(s2_all[:, tt:tt + 1], sq[:], AX)
        v.tensor_scalar(mean_all[:], s1_all[:], 1.0 / D, None, ALU.mult)
        v.tensor_scalar(s2_all[:], s2_all[:], 1.0 / D, None, ALU.mult)
        p.edge(v.tensor_tensor(st[:, 0:16], mean_all[:], mean_all[:], ALU.mult), v)
        v.tensor_tensor(var_all[:], s2_all[:], st[:, 0:16], ALU.subtract).then_inc(s_var, 1)
    p.bump("vvar")

    @block.scalar
    def _(s):
        s.wait_ge(s_var, p.cnt["vvar"])
        p.edge(s.activation(rstd_all[:], var_all[:], AF.Ln, bias=1e-5), s)
        s.activation(rstd_all[:], rstd_all[:], AF.Exp, scale=-0.5).then_inc(s_rstd, 1)
    p.bump("srstd")

    lnd = G["lnd"]
    s_lnst = p.sem("dlnst")
    b_lnst = p.cnt["dlnst"]

    @block.vector
    def _(v):
        v.wait_ge(s_rstd, p.cnt["srstd"])
        p.edge(v.tensor_tensor(nmr_all[:], mean_all[:], rstd_all[:], ALU.mult), v)
        p.edge(v.tensor_scalar(nmr_all[:], nmr_all[:], -1.0, None, ALU.mult), v)
        for tt in range(NTT):
            if tt >= 2:
                v.wait_ge(s_lnst, b_lnst + 16 * (tt - 1))
            xt = xr[:, tt * D:(tt + 1) * D]
            v.tensor_scalar(ln_tok[:, (tt % 2) * D:(tt % 2 + 1) * D], xt,
                            rstd_all[:, tt:tt + 1], nmr_all[:, tt:tt + 1],
                            ALU.mult, ALU.add).then_inc(s_vln, 1)
    p.bump("vln", NTT)

    @block.sync
    def _(s):
        for tt in range(NTT):
            s.wait_ge(s_vln, b_vln + tt + 1)
            s.dma_start(out=lnd[tt * 128:(tt + 1) * 128, :],
                        in_=ln_tok[:, (tt % 2) * D:(tt % 2 + 1) * D]).then_inc(s_lnst, 16)
        for dt in range(DT):
            s.dma_start(out=ln_f[:, dt * T:(dt + 1) * T],
                        in_=lnd[:, dt * 128:(dt + 1) * 128],
                        transpose=True).then_inc(s_lnf, 16)
    p.bump("dlnst", 16 * NTT)
    p.bump("dlnf", 16 * DT)

    if not with_router:
        return

    s_lg = p.sem("tlg"); s_nm = p.sem("vnm"); s_exp = p.sem("sexp"); s_cmb = p.sem("vcmb")
    b_lg = p.cnt["tlg"]; b_nm = p.cnt["vnm"]; b_exp = p.cnt["sexp"]; b_cmb = p.cnt["vcmb"]

    ps_lg = G["ps_lg"]
    rw = G["rw_bf"]; rb = G["rb_bc"]
    cwall = G["cwall"]; cw2all = G["cw2all"]
    pw = G["comb_w"]; pw2 = G["comb_w2"]; c_m = G["c_m"]
    comb_half = G["comb_half"]

    @block.tensor
    def _(t):
        t.wait_ge(s_lnf, b_lnf + 16 * DT)
        for tt in range(NTT):
            if tt >= 2:
                t.wait_ge(s_nm, b_nm + tt - 1)
            ps = ps_lg[tt % 2]
            for kt in range(DT):
                mi = t.matmul(ps[:, :], ln_f[:, kt * T + tt * 128: kt * T + (tt + 1) * 128],
                              rw[:, kt * 8:(kt + 1) * 8], start=(kt == 0), stop=(kt == DT - 1))
            mi.then_inc(s_lg, 1)
    p.bump("tlg", NTT)

    @block.vector
    def _(v):
        for tt in range(NTT):
            v.wait_ge(s_lg, b_lg + tt + 1)
            ps = ps_lg[tt % 2]
            v.tensor_tensor(cwall[:, tt * 8:(tt + 1) * 8], ps[:, :], rb[:],
                            ALU.add).then_inc(s_nm, 1)
    p.bump("vnm", NTT)

    @block.scalar
    def _(s):
        s.wait_ge(s_nm, b_nm + NTT)
        s.activation(cw2all[:], cwall[:], AF.Exp).then_inc(s_exp, 1)
    p.bump("sexp", 1)

    @block.vector
    def _(v):
        v.wait_ge(s_exp, b_exp + 1)
        for tt in range(NTT):
            E = lambda inst: p.edge(inst, v)
            ce = cw2all[:, tt * 8:(tt + 1) * 8]
            E(v.reduce_sum(c_m[:, 3:4], ce, AX))
            E(v.reciprocal(c_m[:, 3:4], c_m[:, 3:4]))
            E(v.tensor_scalar(pw[:], ce, c_m[:, 3:4], None, ALU.mult))
            E(v.reduce_max(c_m[:, 0:1], pw[:], AX))
            E(v.tensor_scalar(pw2[:], pw[:], c_m[:, 0:1], None, ALU.is_equal))
            E(v.tensor_scalar(pw2[:], pw2[:], -1.0, 1.0, ALU.mult, ALU.add))
            E(v.tensor_tensor(pw2[:], pw[:], pw2[:], ALU.mult))
            E(v.reduce_max(c_m[:, 2:3], pw2[:], AX))
            E(v.tensor_scalar(pw2[:], pw[:], c_m[:, 2:3], None, ALU.is_ge))
            E(v.tensor_tensor(pw[:], pw[:], pw2[:], ALU.mult))
            E(v.reduce_sum(c_m[:, 3:4], pw[:], AX))
            E(v.reciprocal(c_m[:, 3:4], c_m[:, 3:4]))
            v.tensor_scalar(comb_half[:, tt * 9 + 1:(tt + 1) * 9], pw[:],
                            c_m[:, 3:4], 0.5, ALU.mult, ALU.mult)
            v.memset(comb_half[:, tt * 9:tt * 9 + 1], 0.5)
            v.engine_nop().then_inc(s_cmb, 1)
    p.bump("vcmb", NTT)


def emit_ffn_experts(p, block, G, pass_idx):
    nc = p.nc
    s_w1g = [p.sem("dw1g0"), p.sem("dw1g1")]; s_w2g = [p.sem("dw2g0"), p.sem("dw2g1")]
    s_w1s = p.sem("dw1s"); s_w2s = p.sem("dw2s")
    s_cache = p.sem("dcache")
    s_mmH = p.sem("tmmH"); s_gel = p.sem("sgel"); s_mm2 = p.sem("tmm2"); s_ev = p.sem("vev")
    s_lnf = p.sem("dlnf")
    b_w1g = [p.cnt["dw1g0"], p.cnt["dw1g1"]]; b_w2g = [p.cnt["dw2g0"], p.cnt["dw2g1"]]
    b_w1s = p.cnt["dw1s"]; b_w2s = p.cnt["dw2s"]
    b_mmH = p.cnt["tmmH"]; b_gel = p.cnt["sgel"]; b_mm2 = p.cnt["tmm2"]; b_ev = p.cnt["vev"]
    b_cache = p.cnt["dcache"]
    b_lnf = p.cnt["dlnf"] - 16 * DT

    xr = G["x_res"]; ln_f = G["ln_f"]; pool_A = G["pool_A"]; pool_B = G["pool_B"]
    h1q = G["h1q"]; b1col = G["b1col"]
    comb_half = G["comb_half"]; ones_bf = G["ones_bf"]; b2row_bf = G["b2row_bf"]
    ps_y = G["ps_y"]; ps_h1 = G["ps_h1"]
    w1T = G["in_w1T"]; w2T = G["in_w2T"]; w1c = G["w1c"]; w2c = G["w2c"]

    QL = [(e, q) for e in range(NE) for q in range(NQ)]
    cached = pass_idx > 0

    def wslc(i):
        return (i % 2) * 8192

    if not cached:
        @block.gpsimd
        def _(g):
            for i, (e, q) in enumerate(QL):
                if i >= 2:
                    g.wait_ge(s_mmH, b_mmH + 16 * (i - 1))
                    g.wait_ge(s_mm2, b_mm2 + 16 * (i - 1))
                    g.wait_ge(s_cache, b_cache + 32 * (i - 1))
                src1 = w1T[e].rearrange("(kt p) f -> p kt f", p=128)[:, :, q * FQ:(q + 1) * FQ]
                dst1 = pool_A[:, wslc(i):wslc(i) + 8192].rearrange("p (kt f) -> p kt f", f=FQ)
                g.dma_start(out=dst1, in_=src1).then_inc(s_w1g[i % 2], 16)
                src2 = w2T[e, q * FQ:(q + 1) * FQ, :].rearrange("(kt p) d -> p kt d", p=128)
                dst2 = pool_B[:, wslc(i):wslc(i) + 8192].rearrange("p (kt d) -> p kt d", d=D)
                g.dma_start(out=dst2, in_=src2).then_inc(s_w2g[i % 2], 16)
        p.bump("dw1g0", 16 * 18); p.bump("dw1g1", 16 * 18)
        p.bump("dw2g0", 16 * 18); p.bump("dw2g1", 16 * 18)

        @block.sync
        def _(s):
            for i, (e, q) in enumerate(QL):
                s.wait_ge(s_w1g[i % 2], b_w1g[i % 2] + 16 * (i // 2 + 1))
                s.wait_ge(s_w2g[i % 2], b_w2g[i % 2] + 16 * (i // 2 + 1))
                s.dma_start(out=w1c[e, q], in_=pool_A[:, wslc(i):wslc(i) + 8192]).then_inc(s_cache, 16)
                s.dma_start(out=w2c[e, q], in_=pool_B[:, wslc(i):wslc(i) + 8192]).then_inc(s_cache, 16)
        p.bump("dcache", 32 * 36)
    else:
        @block.sync
        def _(s):
            for i, (e, q) in enumerate(QL):
                if i >= 2:
                    s.wait_ge(s_mmH, b_mmH + 16 * (i - 1))
                    s.wait_ge(s_mm2, b_mm2 + 16 * (i - 1))
                s.dma_start(out=pool_A[:, wslc(i):wslc(i) + 8192], in_=w1c[e, q]).then_inc(s_w1s, 16)
                s.dma_start(out=pool_B[:, wslc(i):wslc(i) + 8192], in_=w2c[e, q]).then_inc(s_w2s, 16)
        p.bump("dw1s", 16 * 36)
        p.bump("dw2s", 16 * 36)

    def w1_ready(t, i):
        if not cached:
            t.wait_ge(s_w1g[i % 2], b_w1g[i % 2] + 16 * (i // 2 + 1))
        else:
            t.wait_ge(s_w1s, b_w1s + 16 * (i + 1))

    def w2_ready(t, i):
        if not cached:
            t.wait_ge(s_w2g[i % 2], b_w2g[i % 2] + 16 * (i // 2 + 1))
        else:
            t.wait_ge(s_w2s, b_w2s + 16 * (i + 1))

    def emit_mm1(t, i, e, q, w):
        base = wslc(i)
        for ft in range(FT):
            fidx = (i * 2 + w) * 8 + ft
            if fidx >= 2:
                t.wait_ge(s_gel, b_gel + fidx - 1)
            ps = ps_h1[fidx % 2]
            for kt in range(DT):
                for h in range(2):
                    mi = t.matmul(ps[:, h * 512:(h + 1) * 512],
                                  pool_A[:, base + kt * FQ + ft * 128: base + kt * FQ + (ft + 1) * 128],
                                  ln_f[:, kt * T + w * 1024 + h * 512: kt * T + w * 1024 + (h + 1) * 512],
                                  start=(kt == 0), stop=(kt == DT - 1))
            mi.then_inc(s_mmH, 1)

    def emit_mm2(t, i, e, q, w):
        base = wslc(i)
        for ttl in range(8):
            yi = (i * 2 + w) * 8 + ttl
            if yi >= 2:
                t.wait_ge(s_ev, b_ev + yi - 1)
            psy = ps_y[yi % 2]
            first = True
            if q == 0:
                for dh in range(2):
                    t.matmul(psy[:, dh * 512:(dh + 1) * 512],
                             ones_bf[0:1, 0:128],
                             b2row_bf[0:1, e * D + dh * 512: e * D + (dh + 1) * 512],
                             start=True, stop=False)
                first = False
            for kf in range(FT):
                if ttl == 0:
                    t.wait_ge(s_gel, b_gel + (i * 2 + w) * 8 + kf + 1)
                for dh in range(2):
                    mi = t.matmul(psy[:, dh * 512:(dh + 1) * 512],
                                  h1q[:, kf * 1024 + ttl * 128: kf * 1024 + (ttl + 1) * 128],
                                  pool_B[:, base + kf * D + dh * 512: base + kf * D + (dh + 1) * 512],
                                  start=(first and kf == 0), stop=(kf == FT - 1))
            mi.then_inc(s_mm2, 1)

    @block.tensor
    def _(t):
        t.wait_ge(s_lnf, b_lnf + 16 * DT)
        for i, (e, q) in enumerate(QL):
            w1_ready(t, i)
            emit_mm1(t, i, e, q, 0)
            emit_mm2_pending = True
            w2_ready(t, i)
            emit_mm2(t, i, e, q, 0)
            emit_mm1(t, i, e, q, 1)
            emit_mm2(t, i, e, q, 1)
    p.bump("tmmH", 16 * 36)
    p.bump("tmm2", 16 * 36)

    @block.scalar
    def _(s):
        for i, (e, q) in enumerate(QL):
            for w in range(2):
                for ft in range(FT):
                    fidx = (i * 2 + w) * 8 + ft
                    s.wait_ge(s_mmH, b_mmH + fidx + 1)
                    s.wait_ge(s_mm2, b_mm2 + 16 * i + 8 * w)
                    ftg = q * FT + ft
                    s.activation(h1q[:, ft * 1024: (ft + 1) * 1024],
                                 ps_h1[fidx % 2][:, :], AF.Gelu_apprx_tanh,
                                 bias=b1col[:, ftg * 9 + e: ftg * 9 + e + 1]).then_inc(s_gel, 1)
    p.bump("sgel", 16 * 36)

    @block.vector
    def _(v):
        for i, (e, q) in enumerate(QL):
            for w in range(2):
                for ttl in range(8):
                    yi = (i * 2 + w) * 8 + ttl
                    v.wait_ge(s_mm2, b_mm2 + yi + 1)
                    tt = w * 8 + ttl
                    xt = xr[:, tt * D:(tt + 1) * D]
                    v.scalar_tensor_tensor(xt, ps_y[yi % 2][:, :],
                                           comb_half[:, tt * 9 + e: tt * 9 + e + 1],
                                           xt, ALU.mult, ALU.add).then_inc(s_ev, 1)
    p.bump("vev", 16 * 36)


def mbar(p, block, tensor_done=None, dma_waits=()):
    if not hasattr(p, "mcnt"):
        p.mcnt = 0
    p.mcnt += 3
    tgt = p.mcnt
    s = p.sem("mbar")

    def mk(e):
        if tensor_done is not None:
            e.wait_ge(p.sems[tensor_done[0]], tensor_done[1])
        for nm, c in dma_waits:
            e.wait_ge(p.sems[nm], c)
        e.nop().then_inc(s, 1)
        e.wait_ge(s, tgt)

    @block.scalar
    def _(e):
        mk(e)

    @block.vector
    def _(e):
        mk(e)

    @block.gpsimd
    def _(e):
        mk(e)

    @block.tensor
    def _(e):
        e.wait_ge(s, tgt)

    @block.sync
    def _(e):
        e.wait_ge(s, tgt)


NDT = 8
NST = 16


def emit_m_proj(p, block, G, parts=("w", "mm", "pay")):
    nc = p.nc
    s_lnf = p.sem("dlnf")
    s_mw = p.sem("dmw")
    s_mm = p.sem("txz"); s_ev = p.sem("sxe"); s_st = p.sem("dxs"); s_py = p.sem("dpy")
    s_sp = p.sem("dsp"); s_vln = p.sem("vln")
    b_lnf = p.cnt["dlnf"] - 16 * DT
    b_mm = p.cnt["txz"]; b_ev = p.cnt["sxe"]; b_st = p.cnt["dxs"]

    win = G["w_in_sb"]; lnf = G["ln_f"]; stg = G["xz_stg"]; pay = G["pay_sb"]
    ps = G["ps_xz"]

    @block.gpsimd
    def _(g):
        g.wait_ge(p.sems["dcache"], p.cnt["dcache"])
        g.dma_start(out=win[:], in_=G["in_winT"]).then_inc(s_mw, 16)
    p.bump("dmw", 16)


    @block.sync
    def _(s):
        s.wait_ge(s_vln, p.cnt["vln"])
        s.dma_start(out=G["xres_dram"].ap(), in_=G["x_res"][:]).then_inc(s_sp, 16)
    p.bump("dsp", 16)

    QL = [(it, w) for it in range(16) for w in range(4)]
    if "mm" not in parts:
        return

    @block.tensor
    def _(t):
        t.wait_ge(s_lnf, b_lnf + 16 * DT)
        t.wait_ge(s_mw, p.cnt["dmw"])
        for i, (it, w) in enumerate(QL):
            if i >= 2:
                t.wait_ge(s_ev, b_ev + i - 1)
            pp = ps[i % 2]
            for kt in range(DT):
                mi = t.matmul(pp[:, :512],
                              win[:, kt * 2048 + it * 128: kt * 2048 + (it + 1) * 128],
                              lnf[:, kt * T + w * 512: kt * T + (w + 1) * 512],
                              start=(kt == 0), stop=(kt == DT - 1))
            mi.then_inc(s_mm, 1)
    p.bump("txz", 64)

    s_pc = p.sem("spy")

    @block.scalar
    def _(s):
        for i, (it, w) in enumerate(QL):
            s.wait_ge(s_mm, b_mm + i + 1)
            if i >= 2:
                s.wait_ge(s_st, b_st + 16 * (i - 1))
            if "pay" in parts and w == 0:
                s.copy(out=pay[:, it * 3: it * 3 + 2],
                       in_=ps[i % 2][:, 0:2]).then_inc(s_pc, 1)
            if "pay" in parts and w == 3:
                s.copy(out=pay[:, it * 3 + 2: it * 3 + 3],
                       in_=ps[i % 2][:, 511:512]).then_inc(s_pc, 1)
            s.copy(out=stg[i % 2][:], in_=ps[i % 2][:, :512]).then_inc(s_ev, 1)
    p.bump("sxe", 64)
    if "pay" in parts:
        p.bump("spy", 32)

    @block.sync
    def _(s):
        for i, (it, w) in enumerate(QL):
            s.wait_ge(s_ev, b_ev + i + 1)
            s.dma_start(out=G["xz_dram"].ap()[:, it * 2048 + w * 512: it * 2048 + (w + 1) * 512],
                        in_=stg[i % 2][:]).then_inc(s_st, 16)
    p.bump("dxs", 16 * 64)

    if "pay" not in parts:
        return

    @block.sync
    def _(s):
        s.wait_ge(s_pc, p.cnt["spy"])
        s.dma_start(out=G["cc1_in"].ap(), in_=pay[:]).then_inc(s_py, 16)
    p.bump("dpy", 16)


def emit_m_halo(p, block, G):
    s_py = p.sem("dpy"); s_cc = p.sem("cc1"); s_g1 = p.sem("dg1"); s_hl = p.sem("vhl")

    @block.gpsimd
    def _(g):
        g.wait_ge(s_py, p.cnt["dpy"])
        g.collective_compute(
            "AllGather", mybir.AluOpType.bypass,
            replica_groups=[list(range(8))],
            ins=[G["cc1_in"].ap().opt()],
            outs=[G["cc1_out"].ap().opt()],
        ).then_inc(s_cc, 1)
        g.wait_ge(s_cc, p.cnt["cc1"] + 1)
        g.dma_start(out=G["gath1_sb"][:].rearrange("p (r c) -> p r c", c=48),
                    in_=G["cc1_out"].ap().rearrange("(r p) c -> p r c", p=128)).then_inc(s_g1, 16)
    p.bump("cc1", 1)
    p.bump("dg1", 16)

    @block.vector
    def _(v):
        v.wait_ge(s_g1, p.cnt["dg1"])
        hL = G["haloL"]; hR = G["haloR"]; gg = G["gath1_sb"]; m2 = G["msel2_sb"]
        p.edge(v.memset(hL[:], 0.0), v)
        p.edge(v.memset(hR[:], 0.0), v)
        for r in range(8):
            p.edge(v.scalar_tensor_tensor(hL[:], gg[:, r * 48:(r + 1) * 48],
                                          m2[:, r:r + 1], hL[:],
                                          ALU.mult, ALU.add), v)
            p.edge(v.scalar_tensor_tensor(hR[:], gg[:, r * 48:(r + 1) * 48],
                                          m2[:, 8 + r:9 + r], hR[:],
                                          ALU.mult, ALU.add), v)
        v.engine_nop().then_inc(s_hl, 1)
    p.bump("vhl", 1)


def emit_m_conv(p, block, G):
    s_xl = p.sem("dxl"); s_hc = p.sem("ghl"); s_cv = p.sem("vcv")
    s_sl = p.sem("ssl"); s_zs = p.sem("dzs"); s_hl = p.sem("vhl")
    b_xl = p.cnt["dxl"]; b_hc = p.cnt["ghl"]; b_cv = p.cnt["vcv"]
    b_sl = p.cnt["ssl"]; b_zs = p.cnt["dzs"]
    xpre = G["xpre"]; cva = G["cv_acc"]; zst = G["z_stg"]
    cc = G["conv_c_sb"]; cb = G["conv_b_sb"]

    @block.sync
    def _(s):
        for it in range(16):
            if it >= 2:
                s.wait_ge(s_cv, b_cv + it - 1)
            s.dma_start(out=xpre[it % 2][:, 1:2049],
                        in_=G["xz_dram"].ap()[:, it * 2048:(it + 1) * 2048]).then_inc(s_xl, 16)
    p.bump("dxl", 16 * 16)

    @block.gpsimd
    def _(g):
        g.wait_ge(s_hl, p.cnt["vhl"])
        for it in range(16):
            g.wait_ge(s_xl, b_xl + 16 * (it + 1))
            if it >= 2:
                g.wait_ge(s_cv, b_cv + it - 1)
            g.tensor_copy(out=xpre[it % 2][:, 0:1], in_=G["haloL"][:, it * 3 + 2: it * 3 + 3])
            g.tensor_copy(out=xpre[it % 2][:, 2049:2051],
                          in_=G["haloR"][:, it * 3: it * 3 + 2]).then_inc(s_hc, 1)
    p.bump("ghl", 16)

    @block.vector
    def _(v):
        v.memset(G["zeros_bf"][:], 0.0)
        for it in range(16):
            v.wait_ge(s_xl, b_xl + 16 * (it + 1))
            v.wait_ge(s_hc, b_hc + it + 1)
            if it >= 2:
                v.wait_ge(s_sl, b_sl + it - 1)
            x = xpre[it % 2]
            a = cva[it % 2]
            v.tensor_scalar(a[:], x[:, 0:2048], cc[:, it * 4: it * 4 + 1], None, ALU.mult)
            for j in range(1, 4):
                v.scalar_tensor_tensor(a[:], x[:, j:j + 2048], cc[:, it * 4 + j: it * 4 + j + 1],
                                       a[:], ALU.mult, ALU.add)
            v.engine_nop().then_inc(s_cv, 1)
    p.bump("vcv", 16)

    @block.scalar
    def _(s):
        for it in range(16):
            s.wait_ge(s_cv, b_cv + it + 1)
            if it < 8:
                s.activation(out=G["u_bf"][:, it * T:(it + 1) * T], in_=cva[it % 2][:],
                             func=AF.Silu, bias=cb[:, it:it + 1]).then_inc(s_sl, 1)
            else:
                if it >= 12:
                    s.wait_ge(s_zs, b_zs + 16 * (it - 11))
                s.activation(out=zst[it % 4][:], in_=cva[it % 2][:],
                             func=AF.Silu, bias=cb[:, it:it + 1]).then_inc(s_sl, 1)
    p.bump("ssl", 16)

    @block.sync
    def _(s):
        for dt in range(8):
            it = 8 + dt
            s.wait_ge(s_sl, b_sl + it + 1)
            s.dma_start(out=G["z_dram"].ap()[:, dt * T:(dt + 1) * T],
                        in_=zst[it % 4][:]).then_inc(s_zs, 16)
    p.bump("dzs", 16 * 8)


def emit_m_xdbl(p, block, G):
    nc = p.nc
    s_sl = p.sem("ssl")
    s_x2 = p.sem("tx2"); s_xe = p.sem("sxd")
    s_dt = p.sem("tdt"); s_sab = p.sem("ssp"); s_vsp = p.sem("vsp")
    s_dd = p.sem("vdd"); s_dm = p.sem("ddm"); s_gt = p.sem("gdt"); s_pp = p.sem("spp")
    b_sl = p.cnt["ssl"] - 16
    b_x2 = p.cnt["tx2"]; b_xe = p.cnt["sxd"]; b_dt = p.cnt["tdt"]
    b_sab = p.cnt["ssp"]; b_vsp = p.cnt["vsp"]; b_dd = p.cnt["vdd"]
    b_dm = p.cnt["ddm"]; b_gt = p.cnt["gdt"]

    psx = G["ps_xd"]; psd = G["ps_dt"]
    dtl = G["dtlow_bf"]; bc = G["bc_bf"]
    u = G["u_bf"]; dlt = G["dlt_bf"]
    inv = G["invdt_sb"]; inv2 = G["inv2_sb"]

    s_iv = p.sem("viv")
    @block.vector
    def _(v):
        p.edge(v.tensor_tensor(inv2[:], inv[:], inv[:], ALU.add), v)
        v.engine_nop().then_inc(s_iv, 1)
    p.bump("viv", 1)

    @block.tensor
    def _(t):
        t.wait_ge(s_sl, b_sl + 8)
        t.wait_ge(p.sems["dmw"], p.cnt["dmw"])
        for w in range(4):
            if w >= 2:
                t.wait_ge(s_xe, b_xe + 2 * (w - 1))
            pp = psx[w % 2]
            for kt in range(DT):
                mi = t.matmul(pp[0:96, :512],
                              G["w_xd_sb"][:, kt * 96:(kt + 1) * 96],
                              u[:, kt * T + w * 512: kt * T + (w + 1) * 512],
                              start=(kt == 0), stop=(kt == DT - 1))
            mi.then_inc(s_x2, 1)
    p.bump("tx2", 4)

    @block.scalar
    def _(s):
        for w in range(4):
            s.wait_ge(s_x2, b_x2 + w + 1)
            s.copy(out=dtl[0:64, w * 512:(w + 1) * 512],
                   in_=psx[w % 2][0:64, :512]).then_inc(s_xe, 1)
            s.copy(out=bc[0:32, w * 512:(w + 1) * 512],
                   in_=psx[w % 2][64:96, :512]).then_inc(s_xe, 1)
    p.bump("sxd", 8)

    QL = [(mt, w) for mt in range(8) for w in range(4)]

    @block.tensor
    def _(t):
        t.wait_ge(s_xe, b_xe + 8)
        for i, (mt, w) in enumerate(QL):
            if i >= 2:
                t.wait_ge(s_sab, b_sab + i - 1)
                t.wait_ge(s_vsp, b_vsp + i - 1)
            t.matmul(psd[i % 2][:, :512],
                     G["w_dt_sb"][:, mt * 128:(mt + 1) * 128],
                     dtl[0:64, w * 512:(w + 1) * 512],
                     start=True, stop=True).then_inc(s_dt, 1)
    p.bump("tdt", 32)

    @block.scalar
    def _(s):
        s.wait_ge(s_iv, p.cnt["viv"])
        for i, (mt, w) in enumerate(QL):
            s.wait_ge(s_dt, b_dt + i + 1)
            if i >= 2:
                s.wait_ge(s_vsp, b_vsp + i - 1)
            ab = G["sp_ab"]
            p.edge(s.activation(out=ab[i % 2][:], in_=psd[i % 2][:, :512], func=AF.Abs,
                                bias=inv2[:, mt:mt + 1]), s)
            p.edge(s.activation(out=ab[i % 2][:], in_=ab[i % 2][:], func=AF.Exp, scale=-1.0), s)
            s.activation(out=ab[i % 2][:], in_=ab[i % 2][:], func=AF.Ln,
                         bias=1.0).then_inc(s_sab, 1)
    p.bump("ssp", 32)

    @block.vector
    def _(v):
        for i, (mt, w) in enumerate(QL):
            v.wait_ge(s_dt, b_dt + i + 1)
            rl = G["sp_rl"]
            v.tensor_scalar(rl[:], psd[i % 2][:, :512], inv2[:, mt:mt + 1], 0.0,
                            ALU.add, ALU.max)
            v.wait_ge(s_sab, b_sab + i + 1)
            v.tensor_tensor(dlt[:, mt * T + w * 512: mt * T + (w + 1) * 512],
                            rl[:], G["sp_ab"][i % 2][:], ALU.add).then_inc(s_vsp, 1)
        for mt in range(8):
            if mt >= 2:
                v.wait_ge(s_dm, b_dm + 16 * (mt - 1))
                v.wait_ge(s_gt, b_gt + mt - 1)
            v.tensor_tensor_scan(out=G["dl_stg"][mt % 2][:],
                                 data0=G["zeros_bf"][:],
                                 data1=dlt[:, mt * T:(mt + 1) * T],
                                 initial=0.0, op0=ALU.add, op1=ALU.add).then_inc(s_dd, 1)
    p.bump("vsp", 32)
    p.bump("vdd", 8)

    @block.gpsimd
    def _(g):
        for mt in range(8):
            g.wait_ge(s_dd, b_dd + mt + 1)
            g.tensor_copy(out=G["dTot_col"][:, mt:mt + 1],
                          in_=G["dl_stg"][mt % 2][:, 2047:2048]).then_inc(s_gt, 1)
    p.bump("gdt", 8)

    @block.sync
    def _(s):
        for mt in range(8):
            s.wait_ge(s_dd, b_dd + mt + 1)
            s.dma_start(out=G["dlt_dram"].ap()[:, mt * T:(mt + 1) * T],
                        in_=G["dl_stg"][mt % 2][:]).then_inc(s_dm, 16)
    p.bump("ddm", 16 * 8)

    @block.scalar
    def _(s):
        for mt in range(8):
            s.wait_ge(s_gt, b_gt + mt + 1)
            for n in range(1, NST + 1):
                mi = s.activation(out=G["p_all"][:, mt * 16 + n - 1: mt * 16 + n],
                                  in_=G["dTot_col"][:, mt:mt + 1], func=AF.Exp,
                                  scale=-float(n))
        mi.then_inc(s_pp, 1)
    p.bump("spp", 1)


def emit_m_scan(p, block, G):
    nc = p.nc
    s_bp = p.sem("tbp"); s_cp = p.sem("tcp"); s_da = p.sem("sda"); s_ce = p.sem("sce")
    s_du = p.sem("vdu"); s_sc = p.sem("vsc"); s_hc = p.sem("ghc"); s_ye = p.sem("sye")
    s_ya = p.sem("vya")
    b_bp = p.cnt["tbp"]; b_cp = p.cnt["tcp"]; b_da = p.cnt["sda"]; b_ce = p.cnt["sce"]
    b_du = p.cnt["vdu"]; b_sc = p.cnt["vsc"]; b_hc = p.cnt["ghc"]; b_ye = p.cnt["sye"]
    b_ya = p.cnt["vya"]

    psB = G["ps_B"]; psC = G["ps_C"]
    sel = G["sel_sb"]; bc = G["bc_bf"]; dlt = G["dlt_bf"]; u = G["u_bf"]
    dA = G["dA_bf"]; dBu = G["dBu_bf"]; hh = G["h_bf"]; hC = G["hC_bf"]
    du = G["du_f"]; yd = G["y_dt"]; yl = G["y_locbf"]
    QL = [(dt, n) for dt in range(8) for n in range(1, NST + 1)]

    @block.tensor
    def _(t):
        for i, (dt, n) in enumerate(QL):
            if i >= 1:
                t.wait_ge(s_du, b_du + i)
            for c in range(4):
                mi = t.matmul(psB[:, c * 512:(c + 1) * 512],
                              sel[:, (n - 1) * 128: n * 128],
                              bc[0:32, c * 512:(c + 1) * 512],
                              start=True, stop=True)
            mi.then_inc(s_bp, 1)
            if i >= 1:
                t.wait_ge(s_hc, b_hc + i)
            for c in range(4):
                mi = t.matmul(psC[:, c * 512:(c + 1) * 512],
                              sel[:, (16 + n - 1) * 128: (16 + n) * 128],
                              bc[0:32, c * 512:(c + 1) * 512],
                              start=True, stop=True)
            mi.then_inc(s_cp, 1)
    p.bump("tbp", 128)
    p.bump("tcp", 128)

    @block.scalar
    def _(s):
        for i, (dt, n) in enumerate(QL):
            if i >= 2:
                s.wait_ge(s_sc, b_sc + i - 1)
            s.activation(out=dA[i % 2][:], in_=dlt[:, dt * T:(dt + 1) * T],
                         func=AF.Exp, scale=-float(n)).then_inc(s_da, 1)
            s.wait_ge(s_cp, b_cp + i + 1)
            if i >= 2:
                s.wait_ge(s_hc, b_hc + i - 1)
            s.copy(out=G["Cbc_bf"][i % 2][:], in_=psC[:]).then_inc(s_ce, 1)
            if n == NST:
                s.wait_ge(s_ya, b_ya + 16 * (dt + 1))
                s.copy(out=yl[:, dt * T:(dt + 1) * T], in_=yd[:]).then_inc(s_ye, 1)
    p.bump("sda", 128)
    p.bump("sce", 128)
    p.bump("sye", 8)

    @block.vector
    def _(v):
        for i, (dt, n) in enumerate(QL):
            if n == 1:
                v.tensor_tensor(du[:], dlt[:, dt * T:(dt + 1) * T],
                                u[:, dt * T:(dt + 1) * T], ALU.mult)
            v.wait_ge(s_bp, b_bp + i + 1)
            v.tensor_tensor(dBu[i % 2][:], psB[:], du[:], ALU.mult).then_inc(s_du, 1)
            if i >= 1:
                dtp, np_ = QL[i - 1]
                v.wait_ge(s_hc, b_hc + i)
                if np_ == 1:
                    v.wait_ge(s_ye, b_ye + dtp)
                    v.tensor_copy(out=yd[:], in_=hC[(i - 1) % 2][:]).then_inc(s_ya, 1)
                else:
                    v.tensor_tensor(yd[:], yd[:], hC[(i - 1) % 2][:],
                                    ALU.add).then_inc(s_ya, 1)
            v.wait_ge(s_da, b_da + i + 1)
            if i >= 2:
                v.wait_ge(s_hc, b_hc + i - 1)
            v.tensor_tensor_scan(out=hh[i % 2][:], data0=dA[i % 2][:], data1=dBu[i % 2][:],
                                 initial=0.0, op0=ALU.mult, op1=ALU.add).then_inc(s_sc, 1)
        v.wait_ge(s_hc, b_hc + 128)
        v.tensor_tensor(yd[:], yd[:], hC[127 % 2][:], ALU.add).then_inc(s_ya, 1)
    p.bump("vdu", 128)
    p.bump("vsc", 128)
    p.bump("vya", 128)

    @block.gpsimd
    def _(g):
        for i, (dt, n) in enumerate(QL):
            g.wait_ge(s_sc, b_sc + i + 1)
            g.wait_ge(s_ce, b_ce + i + 1)
            if i >= 2:
                g.wait_ge(s_ya, b_ya + i - 1)
            g.tensor_copy(out=G["s_all"][:, i:i + 1], in_=hh[i % 2][:, 2047:2048])
            g.tensor_tensor(hC[i % 2][:], hh[i % 2][:], G["Cbc_bf"][i % 2][:],
                            ALU.mult).then_inc(s_hc, 1)
    p.bump("ghc", 128)


def emit_m_exchange(p, block, G):
    s_hc = p.sem("ghc"); s_pp = p.sem("spp"); s_sf = p.sem("gsf")
    s_ps = p.sem("dps"); s_cc = p.sem("cc2"); s_g2 = p.sem("dg2"); s_h0 = p.sem("vh0")

    @block.gpsimd
    def _(g):
        g.wait_ge(s_hc, p.cnt["ghc"])
        g.tensor_copy(out=G["s_fin"][:], in_=G["s_all"][:]).then_inc(s_sf, 1)
    p.bump("gsf", 1)

    @block.sync
    def _(s):
        s.wait_ge(s_sf, p.cnt["gsf"])
        s.wait_ge(s_pp, p.cnt["spp"])
        s.dma_start(out=G["cc2_in"].ap()[:, 0:128], in_=G["s_fin"][:]).then_inc(s_ps, 16)
        s.dma_start(out=G["cc2_in"].ap()[:, 128:256], in_=G["p_all"][:]).then_inc(s_ps, 16)
    p.bump("dps", 32)

    @block.gpsimd
    def _(g):
        g.wait_ge(s_ps, p.cnt["dps"])
        g.collective_compute(
            "AllGather", mybir.AluOpType.bypass,
            replica_groups=[list(range(8))],
            ins=[G["cc2_in"].ap().opt()],
            outs=[G["cc2_out"].ap().opt()],
        ).then_inc(s_cc, 1)
        g.wait_ge(s_cc, p.cnt["cc2"] + 1)
        g.dma_start(out=G["gath2_sb"][:].rearrange("p (r c) -> p r c", c=256),
                    in_=G["cc2_out"].ap().rearrange("(r p) c -> p r c", p=128)).then_inc(s_g2, 16)
    p.bump("cc2", 1)
    p.bump("dg2", 16)

    @block.vector
    def _(v):
        v.wait_ge(s_g2, p.cnt["dg2"])
        h0 = G["h0_all"]; gg = G["gath2_sb"]; ms = G["msel_sb"]
        Ps = G["psel"]; Ss = G["ssel"]
        p.edge(v.memset(h0[:], 0.0), v)
        for j in (2, 1, 0):
            p.edge(v.memset(Ps[:], 0.0), v)
            p.edge(v.memset(Ss[:], 0.0), v)
            for r in range(8):
                p.edge(v.scalar_tensor_tensor(Ps[:], gg[:, r * 256 + 128: r * 256 + 256],
                                              ms[:, j * 8 + r: j * 8 + r + 1], Ps[:],
                                              ALU.mult, ALU.add), v)
                p.edge(v.scalar_tensor_tensor(Ss[:], gg[:, r * 256: r * 256 + 128],
                                              ms[:, j * 8 + r: j * 8 + r + 1], Ss[:],
                                              ALU.mult, ALU.add), v)
            p.edge(v.tensor_tensor(h0[:], h0[:], Ps[:], ALU.mult), v)
            p.edge(v.tensor_tensor(h0[:], h0[:], Ss[:], ALU.add), v)
        v.engine_nop().then_inc(s_h0, 1)
    p.bump("vh0", 1)


def emit_m_corr(p, block, G):
    nc = p.nc
    s_h0 = p.sem("vh0")
    s_dl = p.sem("ddl"); s_wn = p.sem("swn"); s_c2 = p.sem("tc2")
    s_ac = p.sem("vac"); s_fn = p.sem("gfn"); s_yw = p.sem("dyw")
    b_dl = p.cnt["ddl"]; b_wn = p.cnt["swn"]; b_c2 = p.cnt["tc2"]
    b_ac = p.cnt["vac"]; b_fn = p.cnt["gfn"]

    psC = G["ps_c2"]; sel = G["sel_sb"]; bc = G["bc_bf"]
    dl = G["dl_t"]; wn = G["wn_bf"]; acc = G["acc_f"]; t1 = G["t1_bf"]
    h0 = G["h0_all"]; u = G["u_bf"]; yl = G["y_locbf"]
    NSEQ = [(dt, n) for dt in range(8) for n in range(1, NST + 1)]

    @block.sync
    def _(s):
        for dt in range(8):
            if dt >= 2:
                s.wait_ge(s_wn, b_wn + 16 * (dt - 1))
            s.dma_start(out=dl[dt % 2][:],
                        in_=G["dlt_dram"].ap()[:, dt * T:(dt + 1) * T]).then_inc(s_dl, 16)
    p.bump("ddl", 16 * 8)

    @block.scalar
    def _(s):
        for i, (dt, n) in enumerate(NSEQ):
            if n == 1:
                s.wait_ge(s_dl, b_dl + 16 * (dt + 1))
            if i >= 2:
                s.wait_ge(s_ac, b_ac + i - 1)
            s.activation(out=wn[i % 2][:], in_=dl[dt % 2][:], func=AF.Exp,
                         scale=-float(n)).then_inc(s_wn, 1)
    p.bump("swn", 128)

    @block.tensor
    def _(t):
        for i, (dt, n) in enumerate(NSEQ):
            if i >= 2:
                t.wait_ge(s_ac, b_ac + i - 1)
            for c in range(4):
                mi = t.matmul(psC[i % 2][:, c * 512:(c + 1) * 512],
                              sel[:, (16 + n - 1) * 128: (16 + n) * 128],
                              bc[0:32, c * 512:(c + 1) * 512],
                              start=True, stop=True)
            mi.then_inc(s_c2, 1)
    p.bump("tc2", 128)

    @block.vector
    def _(v):
        v.wait_ge(s_h0, p.cnt["vh0"])
        for i, (dt, n) in enumerate(NSEQ):
            v.wait_ge(s_c2, b_c2 + i + 1)
            v.wait_ge(s_wn, b_wn + i + 1)
            v.tensor_tensor(t1[:], psC[i % 2][:], wn[i % 2][:], ALU.mult)
            col = h0[:, dt * 16 + n - 1: dt * 16 + n]
            if n == 1:
                v.scalar_tensor_tensor(acc[:], t1[:], col, G["zeros_bf"][:],
                                       ALU.mult, ALU.add).then_inc(s_ac, 1)
            else:
                v.scalar_tensor_tensor(acc[:], t1[:], col, acc[:],
                                       ALU.mult, ALU.add).then_inc(s_ac, 1)
            if n == NST:
                v.scalar_tensor_tensor(G["tmp_f"][:], u[:, dt * T:(dt + 1) * T],
                                       G["d_col_sb"][:, dt:dt + 1], acc[:],
                                       ALU.mult, ALU.add)
                v.tensor_tensor(yl[:, dt * T:(dt + 1) * T], yl[:, dt * T:(dt + 1) * T],
                                G["tmp_f"][:], ALU.add).then_inc(s_fn, 1)
    p.bump("vac", 128)
    p.bump("gfn", 8)

    @block.sync
    def _(s):
        for dt in range(8):
            s.wait_ge(s_fn, b_fn + dt + 1)
            s.dma_start(out=G["y_dram"].ap()[:, dt * T:(dt + 1) * T],
                        in_=yl[:, dt * T:(dt + 1) * T]).then_inc(s_yw, 16)
    p.bump("dyw", 16 * 8)


def emit_m_out(p, block, G):
    nc = p.nc
    s_ld = p.sem("dm8"); s_mw = p.sem("dmw")
    s_om = p.sem("tom"); s_oe = p.sem("voe")
    b_om = p.cnt["tom"]; b_oe = p.cnt["voe"]

    @block.sync
    def _(s):
        s.dma_start(out=G["x_res"][:], in_=G["xres_dram"].ap()).then_inc(s_ld, 16)
        s.dma_start(out=G["y_sb"][:], in_=G["y_dram"].ap()).then_inc(s_ld, 16)
        s.dma_start(out=G["z_sb"][:], in_=G["z_dram"].ap()).then_inc(s_ld, 16)
    p.bump("dm8", 16 * 3)

    @block.gpsimd
    def _(g):
        g.dma_start(out=G["w_out_sb"][:], in_=G["in_woutT"]).then_inc(s_mw, 16)
        g.dma_start(out=G["bout_bf"][:], in_=G["in_bout"]).then_inc(s_mw, 16)
    p.bump("dmw", 16 * 2)

    ps = G["ps_o"]

    @block.tensor
    def _(t):
        t.wait_ge(s_ld, p.cnt["dm8"])
        t.wait_ge(s_mw, p.cnt["dmw"])
        for tt in range(NTT):
            if tt >= 2:
                t.wait_ge(s_oe, b_oe + tt - 1)
            pp = ps[tt % 2]
            for dh in range(2):
                t.matmul(pp[:, dh * 512:(dh + 1) * 512],
                         G["ones_bf"][0:1, 0:128],
                         G["bout_bf"][0:1, dh * 512:(dh + 1) * 512],
                         start=True, stop=False)
                for kt in range(16):
                    src = G["y_sb"] if kt < 8 else G["z_sb"]
                    mi = t.matmul(pp[:, dh * 512:(dh + 1) * 512],
                                  src[:, (kt % 8) * T + tt * 128: (kt % 8) * T + (tt + 1) * 128],
                                  G["w_out_sb"][:, kt * 1024 + dh * 512: kt * 1024 + (dh + 1) * 512],
                                  start=False, stop=(kt == 15))
            mi.then_inc(s_om, 1)
    p.bump("tom", NTT)

    @block.vector
    def _(v):
        v.wait_ge(s_ld, p.cnt["dm8"])
        for tt in range(NTT):
            v.wait_ge(s_om, b_om + tt + 1)
            xt = G["x_res"][:, tt * D:(tt + 1) * D]
            v.tensor_tensor(xt, ps[tt % 2][:, :], xt, ALU.add).then_inc(s_oe, 1)
    p.bump("voe", NTT)


def emit_output_dram(p, block, G):
    s_st = p.sem("dst")

    @block.sync
    def _(s):
        for nm, c in sorted(p.cnt.items()):
            if c > 0 and nm != "dst":
                s.wait_ge(p.sems[nm], c)
        src = G["xres_dram"].ap().rearrange("p (tt d) -> p tt d", d=D)
        dst = G["in_out"].rearrange("(tt p) d -> p tt d", p=128)
        s.dma_start(out=dst, in_=src).then_inc(s_st, 16)
        s.wait_ge(s_st, 16)
    p.bump("dst", 16)


def emit_output(p, block, G):
    s_st = p.sem("dst")
    s_ev = p.sem("vev")
    dma_final = [(nm, c) for nm, c in sorted(p.cnt.items())
                 if nm.startswith("d") and nm != "dst" and c > 0]

    @block.sync
    def _(s):
        s.wait_ge(s_ev, p.cnt["vev"])
        for nm, c in dma_final:
            s.wait_ge(p.sems[nm], c)
        src = G["x_res"][:].rearrange("p (tt d) -> p tt d", d=D)
        dst = G["in_out"].rearrange("(tt p) d -> p tt d", p=128)
        s.dma_start(out=dst, in_=src).then_inc(s_st, 16)
        s.wait_ge(s_st, 16)
    p.bump("dst", 16)


def alloc_ln(p, G, scope, tag):
    G["ln_tok"] = p.sbuf(f"ln_tok{tag}", [128, 2 * D], BF16, scope)
    G["ln_f"] = p.sbuf(f"ln_f{tag}", [128, DT * T], BF16, scope)
    for nm, w in [("stats", 16), ("s1_all", 16), ("s2_all", 16), ("mean_all", 16),
                  ("var_all", 16), ("rstd_all", 16), ("nmr_all", 16)]:
        G[nm] = p.sbuf(f"{nm}{tag}", [128, w], F32, scope)
    G["sq"] = p.sbuf(f"sq{tag}", [128, D], F32, scope)


def alloc_router(p, G, scope, tag):
    G["cwall"] = p.sbuf(f"cwall{tag}", [128, NTT * 8], F32, scope)
    G["cw2all"] = p.sbuf(f"cw2all{tag}", [128, NTT * 8], F32, scope)
    G["comb_w"] = p.sbuf(f"comb_w{tag}", [128, 8], F32, scope)
    G["comb_w2"] = p.sbuf(f"comb_w2{tag}", [128, 8], F32, scope)
    G["c_m"] = p.sbuf(f"c_m{tag}", [128, 4], F32, scope)
    G["comb_half"] = p.sbuf(f"comb_half{tag}", [128, NTT * 9], F32, scope)


def emit_ffn_pass(p, block, G, scope, pass_idx, tag):
    alloc_ln(p, G, scope, tag)
    alloc_router(p, G, scope, tag)
    G["h1q"] = p.sbuf(f"h1q{tag}", [128, 8 * 1024], BF16, scope)
    G["pool_A"] = p.sbuf(f"pool_A{tag}", [128, 2 * 8192], BF16, scope)
    G["pool_B"] = p.sbuf(f"pool_B{tag}", [128, 2 * 8192], BF16, scope)
    with contextlib.ExitStack() as lnscope:
        G["ps_lg"] = [p.psum(f"ps_lg0{tag}", [128, 8], F32, lnscope),
                      p.psum(f"ps_lg1{tag}", [128, 8], F32, lnscope)]
        emit_ln_router(p, block, G, pass_idx, with_router=True)
        barrier(p, block, tensor_done=("tlg", p.cnt["tlg"]))
    with contextlib.ExitStack() as exscope:
        G["ps_y"] = [p.psum(f"ps_y{k}{tag}", [128, 1024], F32, exscope) for k in range(2)]
        G["ps_h1"] = [p.psum(f"ps_h1{k}{tag}", [128, 1024], F32, exscope) for k in range(2)]
        emit_ffn_experts(p, block, G, pass_idx)
        barrier(p, block, tensor_done=("tmm2", p.cnt["tmm2"]))


def build(stage=STAGE):
    p = Prog()
    nc = p.nc
    G = {}

    G["in_x"] = nc.dram_tensor("x_sh", [T, D], F32, kind="ExternalInput").ap()
    G["in_w1T"] = nc.dram_tensor("w1T", [NE, D, FF], F32, kind="ExternalInput").ap()
    G["in_w2T"] = nc.dram_tensor("w2T", [NE, FF, D], F32, kind="ExternalInput").ap()
    G["in_b1col"] = nc.dram_tensor("b1col", [128, 32 * 9], F32, kind="ExternalInput").ap()
    G["in_b2r"] = nc.dram_tensor("b2row", [1, 9 * D], F32, kind="ExternalInput").ap()
    G["in_ones"] = nc.dram_tensor("ones_r", [1, 128], F32, kind="ExternalInput").ap()
    G["in_rw"] = nc.dram_tensor("rw_prep", [128, 64], F32, kind="ExternalInput").ap()
    G["in_rb"] = nc.dram_tensor("rb_bc", [128, 8], F32, kind="ExternalInput").ap()
    G["in_out"] = nc.dram_tensor("out", [T, D], F32, kind="ExternalOutput").ap()
    G["w1c"] = nc.dram_tensor("w1c", [NE, NQ, 128, 8192], BF16, kind="Internal")
    G["lnd"] = nc.dram_tensor("lnd", [T, D], BF16, kind="Internal")
    G["w2c"] = nc.dram_tensor("w2c", [NE, NQ, 128, 8192], BF16, kind="Internal")
    G["in_winT"] = nc.dram_tensor("winT", [128, 8 * 2048], F32, kind="ExternalInput").ap()
    G["in_wxdT"] = nc.dram_tensor("wxdT", [128, 8 * 96], F32, kind="ExternalInput").ap()
    G["in_wdtT"] = nc.dram_tensor("wdtT", [64, 1024], F32, kind="ExternalInput").ap()
    G["in_woutT"] = nc.dram_tensor("woutT", [128, 16 * 1024], F32, kind="ExternalInput").ap()
    G["in_sel"] = nc.dram_tensor("sel32", [32, 32 * 128], F32, kind="ExternalInput").ap()
    G["in_bout"] = nc.dram_tensor("bout_r", [1, 1024], F32, kind="ExternalInput").ap()
    G["in_convc"] = nc.dram_tensor("convc", [128, 64], F32, kind="ExternalInput").ap()
    G["in_convb"] = nc.dram_tensor("convb", [128, 16], F32, kind="ExternalInput").ap()
    G["in_invdt"] = nc.dram_tensor("invdtc", [128, 8], F32, kind="ExternalInput").ap()
    G["in_dcol"] = nc.dram_tensor("dcolc", [128, 8], F32, kind="ExternalInput").ap()
    G["in_msel"] = nc.dram_tensor("mselc", [128, 24], F32, kind="ExternalInput").ap()
    G["in_msel2"] = nc.dram_tensor("msel2c", [128, 16], F32, kind="ExternalInput").ap()
    G["xz_dram"] = nc.dram_tensor("xz_d", [128, 16 * 2048], BF16, kind="Internal")
    G["z_dram"] = nc.dram_tensor("z_d", [128, 8 * 2048], BF16, kind="Internal")
    G["dlt_dram"] = nc.dram_tensor("dlt_d", [128, 8 * 2048], BF16, kind="Internal")
    G["y_dram"] = nc.dram_tensor("y_d", [128, 8 * 2048], BF16, kind="Internal")
    G["xres_dram"] = nc.dram_tensor("xres_d", [128, NTT * D], F32, kind="Internal")
    G["cc1_in"] = nc.dram_tensor("cc1_in", [128, 48], BF16, kind="Internal")
    G["cc1_out"] = nc.dram_tensor("cc1_out", [8 * 128, 48], BF16, kind="Internal")
    G["cc2_in"] = nc.dram_tensor("cc2_in", [128, 256], F32, kind="Internal")
    G["cc2_out"] = nc.dram_tensor("cc2_out", [8 * 128, 256], F32, kind="Internal")

    p.setup_consts([1e-5, -0.5, 1.0] + [-(n + 1.0) for n in range(16)])
    G["b1col"] = p.sbuf("b1col_sb", [128, 32 * 9], F32)
    G["b2row_bf"] = p.sbuf("b2rowbf", [1, 9 * D], BF16)
    G["ones_bf"] = p.sbuf("onesbf", [1, 128], BF16)
    G["rw_bf"] = p.sbuf("rwbf", [128, 64], BF16)
    G["rb_bc"] = p.sbuf("rbbc", [128, 8], F32)
    G["haloL"] = p.sbuf("haloL", [128, 48], BF16)
    G["haloR"] = p.sbuf("haloR", [128, 48], BF16)
    G["conv_c_sb"] = p.sbuf("convc_sb", [128, 64], F32)
    G["conv_b_sb"] = p.sbuf("convb_sb", [128, 16], F32)
    G["invdt_sb"] = p.sbuf("invdt_sb", [128, 8], F32)
    G["inv2_sb"] = p.sbuf("inv2_sb", [128, 8], F32)
    G["d_col_sb"] = p.sbuf("dcol_sb", [128, 8], F32)
    G["msel_sb"] = p.sbuf("msel_sb", [128, 24], F32)
    G["msel2_sb"] = p.sbuf("msel2_sb", [128, 16], F32)

    s_ldx = p.sem("dldx")
    s_ldxg = p.sem("dldxg")

    with nc.Block() as block:
        @block.gpsimd
        def _(g):
            g.dma_start(out=G["b2row_bf"][:], in_=G["in_b2r"]).then_inc(s_ldxg, 16)
            g.dma_start(out=G["ones_bf"][:], in_=G["in_ones"]).then_inc(s_ldxg, 16)
            g.dma_start(out=G["rw_bf"][:], in_=G["in_rw"]).then_inc(s_ldxg, 16)
        p.bump("dldxg", 16 * 3)

        s_vc = p.sem("vconst")
        done = False

        with contextlib.ExitStack() as sA:
            G["x_res"] = p.sbuf("x_res_a", [128, NTT * D], F32, sA)

            @block.sync
            def _(s):
                s.dma_start(out=G["x_res"][:].rearrange("p (tt d) -> p tt d", d=D),
                            in_=G["in_x"].rearrange("(tt p) d -> p tt d", p=128)).then_inc(s_ldx, 16)
                for dst, src in [("b1col", "in_b1col"), ("rb_bc", "in_rb"),
                                 ("conv_c_sb", "in_convc"), ("conv_b_sb", "in_convb"),
                                 ("invdt_sb", "in_invdt"), ("d_col_sb", "in_dcol"),
                                 ("msel_sb", "in_msel"), ("msel2_sb", "in_msel2")]:
                    s.dma_start(out=G[dst][:], in_=G[src]).then_inc(s_ldx, 16)
            p.bump("dldx", 16 * 9)

            @block.vector
            def _(v):
                v.wait_ge(s_ldx, p.cnt["dldx"])
                v.wait_ge(s_ldxg, p.cnt["dldxg"])
                p.emit_const_init(v)
                v.engine_nop().then_inc(s_vc, 1)
            p.bump("vconst")

            @block.tensor
            def _(t):
                t.wait_ge(s_ldx, p.cnt["dldx"])
                t.wait_ge(s_ldxg, p.cnt["dldxg"])
                t.wait_ge(s_vc, p.cnt["vconst"])

            @block.scalar
            def _(s):
                s.wait_ge(s_ldx, p.cnt["dldx"])
                s.wait_ge(s_ldxg, p.cnt["dldxg"])
                s.wait_ge(s_vc, p.cnt["vconst"])

            with contextlib.ExitStack() as f1:
                emit_ffn_pass(p, block, G, f1, 0, "_a")
            mbar(p, block, tensor_done=("tmm2", p.cnt["tmm2"]),
                 dma_waits=[("dcache", p.cnt["dcache"])])

            if stage == "ffn1":
                emit_output(p, block, G)
                done = True

            if stage == "mnull":
                s_sp = p.sem("dsp")

                @block.sync
                def _(s):
                    s.wait_ge(p.sems["vev"], p.cnt["vev"])
                    s.dma_start(out=G["xres_dram"].ap(), in_=G["x_res"][:]).then_inc(s_sp, 16)
                p.bump("dsp", 16)
                emit_output_dram(p, block, G)
                done = True

            if not done:
                with contextlib.ExitStack() as mln:
                    alloc_ln(p, G, mln, "_m")
                    G["w_in_sb"] = p.sbuf("w_in_sb", [128, 8 * 2048], BF16, mln)
                    G["xz_stg"] = [p.sbuf(f"xz_stg{k}", [128, 512], BF16, mln) for k in range(2)]
                    G["pay_sb"] = p.sbuf("pay_sb", [128, 48], BF16, mln)
                    G["gath1_sb"] = p.sbuf("gath1_sb", [128, 8 * 48], BF16, mln)
                    G["ps_xz"] = [p.psum(f"ps_xz{k}", [128, 512], F32, mln) for k in range(2)]
                    emit_ln_router(p, block, G, 2, with_router=False)
                    if stage == "mln":
                        s_sp = p.sem("dsp")

                        @block.sync
                        def _(s):
                            s.wait_ge(p.sems["vln"], p.cnt["vln"])
                            s.dma_start(out=G["xres_dram"].ap(),
                                        in_=G["x_res"][:]).then_inc(s_sp, 16)
                        p.bump("dsp", 16)
                        emit_output_dram(p, block, G)
                        done = True
                    if not done:
                        mp_parts = {"mp_w": ("w",), "mp_mm": ("w", "mm")}.get(
                            stage, ("w", "mm", "pay"))
                        emit_m_proj(p, block, G, parts=mp_parts)
                    if stage in ("mproj", "mp_w", "mp_mm"):
                        emit_output_dram(p, block, G)
                        done = True
                    if not done:
                        emit_m_halo(p, block, G)
                        if stage == "mhalo":
                            emit_output_dram(p, block, G)
                            done = True
                    if not done:
                        mbar(p, block, tensor_done=("txz", p.cnt["txz"]),
                             dma_waits=[("dsp", p.cnt["dsp"]), ("dxs", p.cnt["dxs"]),
                                        ("dg1", p.cnt["dg1"]), ("dpy", p.cnt["dpy"])])

        if not done:
            with contextlib.ExitStack() as s2:
                G["u_bf"] = p.sbuf("u_bf", [128, 8 * T], BF16, s2)
                G["y_locbf"] = p.sbuf("y_locbf", [128, 8 * T], BF16, s2)
                G["dlt_bf"] = p.sbuf("dlt_bf", [128, 8 * T], BF16, s2)
                G["bc_bf"] = p.sbuf("bc_bf", [32, T], BF16, s2)
                G["dtlow_bf"] = p.sbuf("dtlow_bf", [64, T], BF16, s2)
                G["sel_sb"] = p.sbuf("sel_sb", [32, 32 * 128], BF16, s2)
                G["zeros_bf"] = p.sbuf("zeros_bf", [128, T], BF16, s2)
                G["w_xd_sb"] = p.sbuf("w_xd_sb", [128, 8 * 96], BF16, s2)
                G["w_dt_sb"] = p.sbuf("w_dt_sb", [64, 1024], BF16, s2)
                G["s_all"] = p.sbuf("s_all", [128, 128], F32, s2)
                G["s_fin"] = p.sbuf("s_fin", [128, 128], F32, s2)
                G["p_all"] = p.sbuf("p_all", [128, 128], F32, s2)
                G["dTot_col"] = p.sbuf("dTot_col", [128, 8], F32, s2)

                s_mw = p.sem("dmw")

                @block.gpsimd
                def _(g):
                    g.dma_start(out=G["w_xd_sb"][:], in_=G["in_wxdT"]).then_inc(s_mw, 16)
                    g.dma_start(out=G["w_dt_sb"][:], in_=G["in_wdtT"]).then_inc(s_mw, 16)
                    g.dma_start(out=G["sel_sb"][:], in_=G["in_sel"]).then_inc(s_mw, 16)
                p.bump("dmw", 16 * 3)

                with contextlib.ExitStack() as s2a:
                    G["xpre"] = [p.sbuf(f"xpre{k}", [128, 2051], BF16, s2a) for k in range(2)]
                    G["cv_acc"] = [p.sbuf(f"cvacc{k}", [128, T], F32, s2a) for k in range(2)]
                    G["z_stg"] = [p.sbuf(f"z_stg{k}", [128, T], BF16, s2a) for k in range(4)]
                    emit_m_conv(p, block, G)
                    if stage == "mconv":
                        emit_output_dram(p, block, G)
                        done = True
                    if not done:
                        mbar(p, block, dma_waits=[("dzs", p.cnt["dzs"]), ("dxl", p.cnt["dxl"])])

                if not done:
                    with contextlib.ExitStack() as s2b:
                        G["sp_ab"] = [p.sbuf(f"sp_ab{k}", [128, 512], F32, s2b) for k in range(2)]
                        G["sp_rl"] = p.sbuf("sp_rl", [128, 512], F32, s2b)
                        G["dl_stg"] = [p.sbuf(f"dl_stg{k}", [128, T], BF16, s2b) for k in range(2)]
                        G["ps_xd"] = [p.psum(f"ps_xd{k}", [128, 512], F32, s2b) for k in range(2)]
                        G["ps_dt"] = [p.psum(f"ps_dt{k}", [128, 512], F32, s2b) for k in range(2)]
                        emit_m_xdbl(p, block, G)
                        if stage == "mxdbl":
                            emit_output_dram(p, block, G)
                            done = True
                        if not done:
                            mbar(p, block, tensor_done=("tdt", p.cnt["tdt"]),
                                 dma_waits=[("ddm", p.cnt["ddm"])])

                if not done:
                    with contextlib.ExitStack() as s2c:
                        G["dA_bf"] = [p.sbuf(f"dA_bf{k}", [128, T], BF16, s2c) for k in range(2)]
                        G["dBu_bf"] = [p.sbuf(f"dBu_bf{k}", [128, T], BF16, s2c) for k in range(2)]
                        G["h_bf"] = [p.sbuf(f"h_bf{k}", [128, T], BF16, s2c) for k in range(2)]
                        G["hC_bf"] = [p.sbuf(f"hC_bf{k}", [128, T], BF16, s2c) for k in range(2)]
                        G["Cbc_bf"] = [p.sbuf(f"Cbc_bf{k}", [128, T], BF16, s2c) for k in range(2)]
                        G["du_f"] = p.sbuf("du_f", [128, T], F32, s2c)
                        G["y_dt"] = p.sbuf("y_dt", [128, T], F32, s2c)
                        G["ps_B"] = p.psum("ps_B", [128, 2048], F32, s2c)
                        G["ps_C"] = p.psum("ps_C", [128, 2048], F32, s2c)
                        emit_m_scan(p, block, G)
                        if stage == "mscan":
                            emit_output_dram(p, block, G)
                            done = True
                        if not done:
                            mbar(p, block, tensor_done=("tcp", p.cnt["tcp"]))

                if not done:
                    with contextlib.ExitStack() as s2d:
                        G["gath2_sb"] = p.sbuf("gath2_sb", [128, 8 * 256], F32, s2d)
                        G["h0_all"] = p.sbuf("h0_all", [128, 128], F32, s2d)
                        G["psel"] = p.sbuf("psel", [128, 128], F32, s2d)
                        G["ssel"] = p.sbuf("ssel", [128, 128], F32, s2d)
                        G["dl_t"] = [p.sbuf(f"dl_t{k}", [128, T], BF16, s2d) for k in range(2)]
                        G["wn_bf"] = [p.sbuf(f"wn_bf{k}", [128, T], BF16, s2d) for k in range(2)]
                        G["t1_bf"] = p.sbuf("t1_bf", [128, T], BF16, s2d)
                        G["acc_f"] = p.sbuf("acc_f", [128, T], F32, s2d)
                        G["tmp_f"] = p.sbuf("tmp_f", [128, T], F32, s2d)
                        G["ps_c2"] = [p.psum(f"ps_c2{k}", [128, 2048], F32, s2d) for k in range(2)]
                        emit_m_exchange(p, block, G)
                        if stage == "mexch":
                            emit_output_dram(p, block, G)
                            done = True
                        if not done:
                            emit_m_corr(p, block, G)
                            if stage == "mcorr":
                                emit_output_dram(p, block, G)
                                done = True
                        if not done:
                            mbar(p, block, tensor_done=("tc2", p.cnt["tc2"]),
                                 dma_waits=[("dyw", p.cnt["dyw"])])

        if not done:
            G["x_res"] = p.sbuf("x_res_b", [128, NTT * D], F32)
            with contextlib.ExitStack() as s3a:
                G["y_sb"] = p.sbuf("y_sb", [128, 8 * T], BF16, s3a)
                G["z_sb"] = p.sbuf("z_sb", [128, 8 * T], BF16, s3a)
                G["w_out_sb"] = p.sbuf("w_out_sb", [128, 16 * 1024], BF16, s3a)
                G["bout_bf"] = p.sbuf("bout_bf", [1, 1024], BF16, s3a)
                G["ps_o"] = [p.psum(f"ps_o{k}", [128, 1024], F32, s3a) for k in range(2)]
                emit_m_out(p, block, G)
                mbar(p, block, tensor_done=("tom", p.cnt["tom"]))

            if stage == "mamba":
                emit_output(p, block, G)
                done = True

            if not done:
                with contextlib.ExitStack() as f2:
                    emit_ffn_pass(p, block, G, f2, 1, "_b")
                emit_output(p, block, G)

    p.stack.close()
    return p, G


_CACHE = {}


def _host_prep(inputs):
    f32 = np.float32
    L = {}
    w1 = [np.ascontiguousarray(inputs["L0_ffn_shared_w1"].T)] + \
         [np.ascontiguousarray(inputs["L0_ffn_spec_w1"][e].T) for e in range(8)]
    L["w1T"] = np.stack(w1).astype(f32, copy=False)
    w2 = [np.ascontiguousarray(inputs["L0_ffn_shared_w2"].T)] + \
         [np.ascontiguousarray(inputs["L0_ffn_spec_w2"][e].T) for e in range(8)]
    L["w2T"] = np.stack(w2).astype(f32, copy=False)
    b1 = np.stack([inputs["L0_ffn_shared_b1"]] + [inputs["L0_ffn_spec_b1"][e] for e in range(8)])
    L["b1col"] = np.ascontiguousarray(b1.reshape(9, 32, 128).transpose(2, 1, 0).reshape(128, 32 * 9)).astype(f32)
    L["b2row"] = np.ascontiguousarray(
        np.stack([inputs["L0_ffn_shared_b2"]] + [inputs["L0_ffn_spec_b2"][e] for e in range(8)]).reshape(1, -1)).astype(f32)
    L["ones_r"] = np.ones((1, 128), dtype=f32)
    rw = inputs["L0_ffn_router_w"]
    L["rw_prep"] = np.ascontiguousarray(rw.reshape(8, 8, 128).transpose(2, 1, 0).reshape(128, 64)).astype(f32)
    L["rb_bc"] = np.ascontiguousarray(np.tile(inputs["L0_ffn_router_b"][None, :], (128, 1))).astype(f32)

    L["winT"] = np.ascontiguousarray(
        inputs["L0_attn_W_in"].reshape(8, 128, 2048).transpose(1, 0, 2).reshape(128, 8 * 2048)).astype(f32)
    L["wxdT"] = np.ascontiguousarray(
        inputs["L0_attn_W_xdbl"].reshape(8, 128, 96).transpose(1, 0, 2).reshape(128, 8 * 96)).astype(f32)
    L["wdtT"] = np.ascontiguousarray(inputs["L0_attn_W_dt"]).astype(f32)
    L["woutT"] = np.ascontiguousarray(
        inputs["L0_attn_W_out"].reshape(16, 128, 1024).transpose(1, 0, 2).reshape(128, 16 * 1024)).astype(f32)
    L["sel32"] = np.ascontiguousarray(np.kron(np.eye(32, dtype=f32), np.ones((1, 128), f32)))
    L["bout_r"] = np.ascontiguousarray(inputs["L0_attn_b_out"].reshape(1, 1024)).astype(f32)
    cxw = inputs["L0_attn_conv_x_w"].reshape(4, 1024)
    czw = inputs["L0_attn_conv_z_w"].reshape(4, 1024)
    cc = np.empty((128, 64), f32)
    for dt in range(8):
        for j in range(4):
            cc[:, dt * 4 + j] = cxw[j, dt * 128:(dt + 1) * 128]
            cc[:, 32 + dt * 4 + j] = czw[j, dt * 128:(dt + 1) * 128]
    L["convc"] = cc
    cb = np.empty((128, 16), f32)
    for dt in range(8):
        cb[:, dt] = inputs["L0_attn_conv_x_b"][dt * 128:(dt + 1) * 128]
        cb[:, 8 + dt] = inputs["L0_attn_conv_z_b"][dt * 128:(dt + 1) * 128]
    L["convb"] = cb
    L["invdtc"] = np.ascontiguousarray(
        inputs["L0_attn_inv_dt"].reshape(8, 128).T).astype(f32)
    L["dcolc"] = np.ascontiguousarray(inputs["L0_attn_D"].reshape(8, 128).T).astype(f32)
    return L


def _core_masks(core):
    b, c = core // 4, core % 4
    f32 = np.float32
    msel = np.zeros((1, 24), f32)
    for j in range(3):
        cc = c - 1 - j
        if cc >= 0:
            msel[0, j * 8 + (b * 4 + cc)] = 1.0
    msel2 = np.zeros((1, 16), f32)
    if c > 0:
        msel2[0, b * 4 + c - 1] = 1.0
    if c < 3:
        msel2[0, 8 + b * 4 + c + 1] = 1.0
    return (np.ascontiguousarray(np.tile(msel, (128, 1))),
            np.ascontiguousarray(np.tile(msel2, (128, 1))))


def kernel(**inputs):
    key = ("prog", STAGE)
    if key not in _CACHE:
        _CACHE[key] = build(STAGE)
    p, G = _CACHE[key]

    L = _host_prep(inputs)
    x = np.asarray(inputs["L0_attn_x"], dtype=np.float32)

    in_maps = []
    for core in range(8):
        b, c = core // 4, core % 4
        msel, msel2 = _core_masks(core)
        m = {
            "x_sh": np.ascontiguousarray(x[b, c * T:(c + 1) * T, :]),
            "w1T": L["w1T"], "w2T": L["w2T"], "b1col": L["b1col"], "b2row": L["b2row"],
            "ones_r": L["ones_r"],
            "rw_prep": L["rw_prep"], "rb_bc": L["rb_bc"],
            "winT": L["winT"], "wxdT": L["wxdT"], "wdtT": L["wdtT"],
            "woutT": L["woutT"], "sel32": L["sel32"], "bout_r": L["bout_r"],
            "convc": L["convc"], "convb": L["convb"], "invdtc": L["invdtc"],
            "dcolc": L["dcolc"], "mselc": msel, "msel2c": msel2,
        }
        in_maps.append(m)

    res = run_bass_kernel_spmd(p.nc, in_maps, core_ids=list(range(8)))
    out = np.empty((2, 8192, D), dtype=np.float32)
    for core in range(8):
        b, c = core // 4, core % 4
        out[b, c * T:(c + 1) * T, :] = res.results[core]["out"]
    return out


# version 06
# speedup vs baseline: 1.3883x; 1.0080x over previous
import contextlib
import os
import numpy as np
import concourse.bass as bass
import concourse.mybir as mybir
from concourse.bass_utils import run_bass_kernel_spmd

F32 = mybir.dt.float32
BF16 = mybir.dt.bfloat16
AF = mybir.ActivationFunctionType
ALU = mybir.AluOpType
AX = mybir.AxisListType.X

D = 1024
T = 2048
NTT = 16
DT = 8
FF = 4096
NE = 9
NQ = 4
FQ = 1024
FT = 8
WINDOWS = [(0, 384), (384, 384), (768, 384), (1152, 384), (1536, 384), (1920, 128)]
NW = len(WINDOWS)
NQTOT_PASS = NW * NE * NQ

STAGE = "full"


class Prog:
    def __init__(self):
        self.nc = bass.Bass("TRN2", target_bir_lowering=False, debug=False,
                            detect_race_conditions=False)
        self.stack = contextlib.ExitStack()
        self.sems = {}
        self.cnt = {}
        self.nbar = 0

    def sem(self, name):
        if name not in self.sems:
            self.sems[name] = self.stack.enter_context(self.nc.semaphore(name))
            self.cnt[name] = 0
        return self.sems[name]

    def bump(self, name, n=1):
        self.cnt[name] += n
        return self.cnt[name]

    def sbuf(self, name, shape, dt=F32, scope=None):
        st = scope if scope is not None else self.stack
        return st.enter_context(self.nc.sbuf_tensor(name, shape, dt))

    def psum(self, name, shape, dt=F32, scope=None):
        st = scope if scope is not None else self.stack
        return st.enter_context(self.nc.psum_tensor(name, shape, dt))

    def edge(self, inst, eng):
        se = self.sem("selfe")
        n = self.bump("selfe")
        inst.then_inc(se, 1)
        eng.wait_ge(se, n)

    def setup_consts(self, values):
        self.const_vals = list(values)
        self.const_sb = self.sbuf("const_sb", [128, max(1, len(values))], F32)
        for j, v in enumerate(self.const_vals):
            self.nc.const_aps.register(F32, float(v), self.const_sb[:, j:j + 1])

    def emit_const_init(self, v):
        for j, val in enumerate(self.const_vals):
            self.nc.vector.memset(self.const_sb[:, j:j + 1], float(val)) if False else \
                v.memset(self.const_sb[:, j:j + 1], float(val))


def barrier(p, block, tensor_done=None):
    p.nbar += 1
    tgt = 2 * p.nbar
    s = p.sem("bar")

    @block.scalar
    def _(e):
        if tensor_done is not None:
            e.wait_ge(p.sems[tensor_done[0]], tensor_done[1])
        e.nop().then_inc(s, 1)
        e.wait_ge(s, tgt)

    @block.vector
    def _(e):
        if tensor_done is not None:
            e.wait_ge(p.sems[tensor_done[0]], tensor_done[1])
        e.nop().then_inc(s, 1)
        e.wait_ge(s, tgt)

    @block.tensor
    def _(e):
        e.wait_ge(s, tgt)


def emit_ln_router(p, block, G, pass_idx, with_router, sub=99):
    nc = p.nc
    s_vln = p.sem("vln"); s_lnf = p.sem("dlnf"); s_var = p.sem("vvar"); s_rstd = p.sem("srstd")
    b_vln = p.cnt["vln"]; b_lnf = p.cnt["dlnf"]


    xr = G["x_res"]
    ln_tok = G["ln_tok"]
    ln_f = G["ln_f"]
    st = G["stats"]
    mean_all = G["mean_all"]; var_all = G["var_all"]
    rstd_all = G["rstd_all"]; nmr_all = G["nmr_all"]
    sq = G["sq"]
    s1_all = G["s1_all"]; s2_all = G["s2_all"]

    @block.vector
    def _(v):
        for tt in range(NTT):
            xt = xr[:, tt * D:(tt + 1) * D]
            v.tensor_tensor(sq[:], xt, xt, ALU.mult)
            v.reduce_sum(s1_all[:, tt:tt + 1], xt, AX)
            v.reduce_sum(s2_all[:, tt:tt + 1], sq[:], AX)
        v.tensor_scalar(mean_all[:], s1_all[:], 1.0 / D, None, ALU.mult)
        v.tensor_scalar(s2_all[:], s2_all[:], 1.0 / D, None, ALU.mult)
        p.edge(v.tensor_tensor(st[:, 0:16], mean_all[:], mean_all[:], ALU.mult), v)
        v.tensor_tensor(var_all[:], s2_all[:], st[:, 0:16], ALU.subtract).then_inc(s_var, 1)
    p.bump("vvar")

    @block.scalar
    def _(s):
        s.wait_ge(s_var, p.cnt["vvar"])
        p.edge(s.activation(rstd_all[:], var_all[:], AF.Ln, bias=1e-5), s)
        s.activation(rstd_all[:], rstd_all[:], AF.Exp, scale=-0.5).then_inc(s_rstd, 1)
    p.bump("srstd")

    lnd = G["lnd"]
    s_lnst = p.sem("dlnst")
    b_lnst = p.cnt["dlnst"]

    @block.vector
    def _(v):
        v.wait_ge(s_rstd, p.cnt["srstd"])
        p.edge(v.tensor_tensor(nmr_all[:], mean_all[:], rstd_all[:], ALU.mult), v)
        p.edge(v.tensor_scalar(nmr_all[:], nmr_all[:], -1.0, None, ALU.mult), v)
        for tt in range(NTT):
            if tt >= 2:
                v.wait_ge(s_lnst, b_lnst + 16 * (tt - 1))
            xt = xr[:, tt * D:(tt + 1) * D]
            v.tensor_scalar(ln_tok[:, (tt % 2) * D:(tt % 2 + 1) * D], xt,
                            rstd_all[:, tt:tt + 1], nmr_all[:, tt:tt + 1],
                            ALU.mult, ALU.add).then_inc(s_vln, 1)
    p.bump("vln", NTT)

    @block.sync
    def _(s):
        for tt in range(NTT):
            s.wait_ge(s_vln, b_vln + tt + 1)
            s.dma_start(out=lnd[tt * 128:(tt + 1) * 128, :],
                        in_=ln_tok[:, (tt % 2) * D:(tt % 2 + 1) * D]).then_inc(s_lnst, 16)
        for dt in range(DT):
            s.dma_start(out=ln_f[:, dt * T:(dt + 1) * T],
                        in_=lnd[:, dt * 128:(dt + 1) * 128],
                        transpose=True).then_inc(s_lnf, 16)
    p.bump("dlnst", 16 * NTT)
    p.bump("dlnf", 16 * DT)

    if not with_router:
        return

    s_lg = p.sem("tlg"); s_nm = p.sem("vnm"); s_exp = p.sem("sexp"); s_cmb = p.sem("vcmb")
    b_lg = p.cnt["tlg"]; b_nm = p.cnt["vnm"]; b_exp = p.cnt["sexp"]; b_cmb = p.cnt["vcmb"]

    ps_lg = G["ps_lg"]
    rw = G["rw_bf"]; rb = G["rb_bc"]
    cwall = G["cwall"]; cw2all = G["cw2all"]
    pw = G["comb_w"]; pw2 = G["comb_w2"]; c_m = G["c_m"]
    comb_half = G["comb_half"]

    @block.tensor
    def _(t):
        t.wait_ge(s_lnf, b_lnf + 16 * DT)
        for tt in range(NTT):
            if tt >= 2:
                t.wait_ge(s_nm, b_nm + tt - 1)
            ps = ps_lg[tt % 2]
            for kt in range(DT):
                mi = t.matmul(ps[:, :], ln_f[:, kt * T + tt * 128: kt * T + (tt + 1) * 128],
                              rw[:, kt * 8:(kt + 1) * 8], start=(kt == 0), stop=(kt == DT - 1))
            mi.then_inc(s_lg, 1)
    p.bump("tlg", NTT)

    @block.vector
    def _(v):
        for tt in range(NTT):
            v.wait_ge(s_lg, b_lg + tt + 1)
            ps = ps_lg[tt % 2]
            v.tensor_tensor(cwall[:, tt * 8:(tt + 1) * 8], ps[:, :], rb[:],
                            ALU.add).then_inc(s_nm, 1)
    p.bump("vnm", NTT)

    @block.scalar
    def _(s):
        s.wait_ge(s_nm, b_nm + NTT)
        s.activation(cw2all[:], cwall[:], AF.Exp).then_inc(s_exp, 1)
    p.bump("sexp", 1)

    @block.vector
    def _(v):
        v.wait_ge(s_exp, b_exp + 1)
        for tt in range(NTT):
            E = lambda inst: p.edge(inst, v)
            ce = cw2all[:, tt * 8:(tt + 1) * 8]
            E(v.reduce_sum(c_m[:, 3:4], ce, AX))
            E(v.reciprocal(c_m[:, 3:4], c_m[:, 3:4]))
            E(v.tensor_scalar(pw[:], ce, c_m[:, 3:4], None, ALU.mult))
            E(v.reduce_max(c_m[:, 0:1], pw[:], AX))
            E(v.tensor_scalar(pw2[:], pw[:], c_m[:, 0:1], None, ALU.is_equal))
            E(v.tensor_scalar(pw2[:], pw2[:], -1.0, 1.0, ALU.mult, ALU.add))
            E(v.tensor_tensor(pw2[:], pw[:], pw2[:], ALU.mult))
            E(v.reduce_max(c_m[:, 2:3], pw2[:], AX))
            E(v.tensor_scalar(pw2[:], pw[:], c_m[:, 2:3], None, ALU.is_ge))
            E(v.tensor_tensor(pw[:], pw[:], pw2[:], ALU.mult))
            E(v.reduce_sum(c_m[:, 3:4], pw[:], AX))
            E(v.reciprocal(c_m[:, 3:4], c_m[:, 3:4]))
            v.tensor_scalar(comb_half[:, tt * 9 + 1:(tt + 1) * 9], pw[:],
                            c_m[:, 3:4], 0.5, ALU.mult, ALU.mult)
            v.memset(comb_half[:, tt * 9:tt * 9 + 1], 0.5)
            v.engine_nop().then_inc(s_cmb, 1)
    p.bump("vcmb", NTT)


def emit_ffn_experts(p, block, G, pass_idx):
    nc = p.nc
    s_w1g = [p.sem("dw1g0"), p.sem("dw1g1")]; s_w2g = [p.sem("dw2g0"), p.sem("dw2g1")]
    s_w1s = p.sem("dw1s"); s_w2s = p.sem("dw2s")
    s_cache = p.sem("dcache")
    s_mmH = p.sem("tmmH"); s_gel = p.sem("sgel"); s_mm2 = p.sem("tmm2"); s_ev = p.sem("vev")
    s_lnf = p.sem("dlnf")
    b_w1g = [p.cnt["dw1g0"], p.cnt["dw1g1"]]; b_w2g = [p.cnt["dw2g0"], p.cnt["dw2g1"]]
    b_w1s = p.cnt["dw1s"]; b_w2s = p.cnt["dw2s"]
    b_mmH = p.cnt["tmmH"]; b_gel = p.cnt["sgel"]; b_mm2 = p.cnt["tmm2"]; b_ev = p.cnt["vev"]
    b_cache = p.cnt["dcache"]
    b_lnf = p.cnt["dlnf"] - 16 * DT

    xr = G["x_res"]; ln_f = G["ln_f"]; pool_A = G["pool_A"]; pool_B = G["pool_B"]
    h1q = G["h1q"]; b1col = G["b1col"]
    comb_half = G["comb_half"]; ones_bf = G["ones_bf"]; b2row_bf = G["b2row_bf"]
    ps_y = G["ps_y"]; ps_h1 = G["ps_h1"]
    w1T = G["in_w1T"]; w2T = G["in_w2T"]; w1c = G["w1c"]; w2c = G["w2c"]

    QL = [(e, q) for e in range(NE) for q in range(NQ)]
    cached = pass_idx > 0

    def wslc(i):
        return (i % 2) * 8192

    if not cached:
        @block.gpsimd
        def _(g):
            for i, (e, q) in enumerate(QL):
                if i >= 2:
                    g.wait_ge(s_mmH, b_mmH + 16 * (i - 1))
                    g.wait_ge(s_mm2, b_mm2 + 16 * (i - 1))
                    g.wait_ge(s_cache, b_cache + 32 * (i - 1))
                src1 = w1T[e].rearrange("(kt p) f -> p kt f", p=128)[:, :, q * FQ:(q + 1) * FQ]
                dst1 = pool_A[:, wslc(i):wslc(i) + 8192].rearrange("p (kt f) -> p kt f", f=FQ)
                g.dma_start(out=dst1, in_=src1).then_inc(s_w1g[i % 2], 16)
                src2 = w2T[e, q * FQ:(q + 1) * FQ, :].rearrange("(kt p) d -> p kt d", p=128)
                dst2 = pool_B[:, wslc(i):wslc(i) + 8192].rearrange("p (kt d) -> p kt d", d=D)
                g.dma_start(out=dst2, in_=src2).then_inc(s_w2g[i % 2], 16)
        p.bump("dw1g0", 16 * 18); p.bump("dw1g1", 16 * 18)
        p.bump("dw2g0", 16 * 18); p.bump("dw2g1", 16 * 18)

        @block.sync
        def _(s):
            for i, (e, q) in enumerate(QL):
                s.wait_ge(s_w1g[i % 2], b_w1g[i % 2] + 16 * (i // 2 + 1))
                s.wait_ge(s_w2g[i % 2], b_w2g[i % 2] + 16 * (i // 2 + 1))
                s.dma_start(out=w1c[e, q], in_=pool_A[:, wslc(i):wslc(i) + 8192]).then_inc(s_cache, 16)
                s.dma_start(out=w2c[e, q], in_=pool_B[:, wslc(i):wslc(i) + 8192]).then_inc(s_cache, 16)
        p.bump("dcache", 32 * 36)
    else:
        @block.sync
        def _(s):
            for i, (e, q) in enumerate(QL):
                if i >= 2:
                    s.wait_ge(s_mmH, b_mmH + 16 * (i - 1))
                    s.wait_ge(s_mm2, b_mm2 + 16 * (i - 1))
                s.dma_start(out=pool_A[:, wslc(i):wslc(i) + 8192], in_=w1c[e, q]).then_inc(s_w1s, 16)
                s.dma_start(out=pool_B[:, wslc(i):wslc(i) + 8192], in_=w2c[e, q]).then_inc(s_w2s, 16)
        p.bump("dw1s", 16 * 36)
        p.bump("dw2s", 16 * 36)

    def w1_ready(t, i):
        if not cached:
            t.wait_ge(s_w1g[i % 2], b_w1g[i % 2] + 16 * (i // 2 + 1))
        else:
            t.wait_ge(s_w1s, b_w1s + 16 * (i + 1))

    def w2_ready(t, i):
        if not cached:
            t.wait_ge(s_w2g[i % 2], b_w2g[i % 2] + 16 * (i // 2 + 1))
        else:
            t.wait_ge(s_w2s, b_w2s + 16 * (i + 1))

    def emit_mm1(t, i, e, q, w):
        base = wslc(i)
        for ft in range(FT):
            fidx = (i * 2 + w) * 8 + ft
            if fidx >= 2:
                t.wait_ge(s_gel, b_gel + fidx - 1)
            ps = ps_h1[fidx % 2]
            for kt in range(DT):
                for h in range(2):
                    mi = t.matmul(ps[:, h * 512:(h + 1) * 512],
                                  pool_A[:, base + kt * FQ + ft * 128: base + kt * FQ + (ft + 1) * 128],
                                  ln_f[:, kt * T + w * 1024 + h * 512: kt * T + w * 1024 + (h + 1) * 512],
                                  start=(kt == 0), stop=(kt == DT - 1))
            mi.then_inc(s_mmH, 1)

    def emit_mm2(t, i, e, q, w):
        base = wslc(i)
        for ttl in range(8):
            yi = (i * 2 + w) * 8 + ttl
            if yi >= 2:
                t.wait_ge(s_ev, b_ev + yi - 1)
            psy = ps_y[yi % 2]
            first = True
            if q == 0:
                for dh in range(2):
                    t.matmul(psy[:, dh * 512:(dh + 1) * 512],
                             ones_bf[0:1, 0:128],
                             b2row_bf[0:1, e * D + dh * 512: e * D + (dh + 1) * 512],
                             start=True, stop=False)
                first = False
            for kf in range(FT):
                if ttl == 0:
                    t.wait_ge(s_gel, b_gel + (i * 2 + w) * 8 + kf + 1)
                for dh in range(2):
                    mi = t.matmul(psy[:, dh * 512:(dh + 1) * 512],
                                  h1q[:, kf * 1024 + ttl * 128: kf * 1024 + (ttl + 1) * 128],
                                  pool_B[:, base + kf * D + dh * 512: base + kf * D + (dh + 1) * 512],
                                  start=(first and kf == 0), stop=(kf == FT - 1))
            mi.then_inc(s_mm2, 1)

    @block.tensor
    def _(t):
        t.wait_ge(s_lnf, b_lnf + 16 * DT)
        for i, (e, q) in enumerate(QL):
            w1_ready(t, i)
            emit_mm1(t, i, e, q, 0)
            emit_mm2_pending = True
            w2_ready(t, i)
            emit_mm2(t, i, e, q, 0)
            emit_mm1(t, i, e, q, 1)
            emit_mm2(t, i, e, q, 1)
    p.bump("tmmH", 16 * 36)
    p.bump("tmm2", 16 * 36)

    @block.scalar
    def _(s):
        for i, (e, q) in enumerate(QL):
            for w in range(2):
                for ft in range(FT):
                    fidx = (i * 2 + w) * 8 + ft
                    s.wait_ge(s_mmH, b_mmH + fidx + 1)
                    s.wait_ge(s_mm2, b_mm2 + 16 * i + 8 * w)
                    ftg = q * FT + ft
                    s.activation(h1q[:, ft * 1024: (ft + 1) * 1024],
                                 ps_h1[fidx % 2][:, :], AF.Gelu_apprx_tanh,
                                 bias=b1col[:, ftg * 9 + e: ftg * 9 + e + 1]).then_inc(s_gel, 1)
    p.bump("sgel", 16 * 36)

    @block.vector
    def _(v):
        for i, (e, q) in enumerate(QL):
            for w in range(2):
                for ttl in range(8):
                    yi = (i * 2 + w) * 8 + ttl
                    v.wait_ge(s_mm2, b_mm2 + yi + 1)
                    tt = w * 8 + ttl
                    xt = xr[:, tt * D:(tt + 1) * D]
                    v.scalar_tensor_tensor(xt, ps_y[yi % 2][:, :],
                                           comb_half[:, tt * 9 + e: tt * 9 + e + 1],
                                           xt, ALU.mult, ALU.add).then_inc(s_ev, 1)
    p.bump("vev", 16 * 36)


def mbar(p, block, tensor_done=None, dma_waits=()):
    if not hasattr(p, "mcnt"):
        p.mcnt = 0
    p.mcnt += 3
    tgt = p.mcnt
    s = p.sem("mbar")

    def mk(e):
        if tensor_done is not None:
            e.wait_ge(p.sems[tensor_done[0]], tensor_done[1])
        for nm, c in dma_waits:
            e.wait_ge(p.sems[nm], c)
        e.nop().then_inc(s, 1)
        e.wait_ge(s, tgt)

    @block.scalar
    def _(e):
        mk(e)

    @block.vector
    def _(e):
        mk(e)

    @block.gpsimd
    def _(e):
        mk(e)

    @block.tensor
    def _(e):
        e.wait_ge(s, tgt)

    @block.sync
    def _(e):
        e.wait_ge(s, tgt)


NDT = 8
NST = 16


def emit_m_proj(p, block, G, parts=("w", "mm", "pay")):
    nc = p.nc
    s_lnf = p.sem("dlnf")
    s_mw = p.sem("dmw")
    s_mm = p.sem("txz"); s_ev = p.sem("sxe"); s_st = p.sem("dxs"); s_py = p.sem("dpy")
    s_sp = p.sem("dsp"); s_vln = p.sem("vln")
    b_lnf = p.cnt["dlnf"] - 16 * DT
    b_mm = p.cnt["txz"]; b_ev = p.cnt["sxe"]; b_st = p.cnt["dxs"]

    win = G["w_in_sb"]; lnf = G["ln_f"]; stg = G["xz_stg"]; pay = G["pay_sb"]
    ps = G["ps_xz"]

    @block.gpsimd
    def _(g):
        g.wait_ge(p.sems["dcache"], p.cnt["dcache"])
        g.dma_start(out=win[:], in_=G["in_winT"]).then_inc(s_mw, 16)
    p.bump("dmw", 16)


    @block.sync
    def _(s):
        s.wait_ge(s_vln, p.cnt["vln"])
        s.dma_start(out=G["xres_dram"].ap(), in_=G["x_res"][:]).then_inc(s_sp, 16)
    p.bump("dsp", 16)

    QL = [(it, w) for it in range(16) for w in range(4)]
    if "mm" not in parts:
        return

    @block.tensor
    def _(t):
        t.wait_ge(s_lnf, b_lnf + 16 * DT)
        t.wait_ge(s_mw, p.cnt["dmw"])
        for i, (it, w) in enumerate(QL):
            if i >= 2:
                t.wait_ge(s_ev, b_ev + i - 1)
            pp = ps[i % 2]
            for kt in range(DT):
                mi = t.matmul(pp[:, :512],
                              win[:, kt * 2048 + it * 128: kt * 2048 + (it + 1) * 128],
                              lnf[:, kt * T + w * 512: kt * T + (w + 1) * 512],
                              start=(kt == 0), stop=(kt == DT - 1))
            mi.then_inc(s_mm, 1)
    p.bump("txz", 64)

    s_pc = p.sem("spy")

    @block.scalar
    def _(s):
        for i, (it, w) in enumerate(QL):
            s.wait_ge(s_mm, b_mm + i + 1)
            if i >= 2:
                s.wait_ge(s_st, b_st + 16 * (i - 1))
            if "pay" in parts and w == 0:
                s.copy(out=pay[:, it * 3: it * 3 + 2],
                       in_=ps[i % 2][:, 0:2]).then_inc(s_pc, 1)
            if "pay" in parts and w == 3:
                s.copy(out=pay[:, it * 3 + 2: it * 3 + 3],
                       in_=ps[i % 2][:, 511:512]).then_inc(s_pc, 1)
            s.copy(out=stg[i % 2][:], in_=ps[i % 2][:, :512]).then_inc(s_ev, 1)
    p.bump("sxe", 64)
    if "pay" in parts:
        p.bump("spy", 32)

    @block.sync
    def _(s):
        for i, (it, w) in enumerate(QL):
            s.wait_ge(s_ev, b_ev + i + 1)
            s.dma_start(out=G["xz_dram"].ap()[:, it * 2048 + w * 512: it * 2048 + (w + 1) * 512],
                        in_=stg[i % 2][:]).then_inc(s_st, 16)
    p.bump("dxs", 16 * 64)

    if "pay" not in parts:
        return

    @block.sync
    def _(s):
        s.wait_ge(s_pc, p.cnt["spy"])
        s.dma_start(out=G["cc1_in"].ap(), in_=pay[:]).then_inc(s_py, 16)
    p.bump("dpy", 16)


def emit_m_halo(p, block, G):
    s_py = p.sem("dpy"); s_cc = p.sem("cc1"); s_g1 = p.sem("dg1"); s_hl = p.sem("vhl")

    @block.gpsimd
    def _(g):
        g.wait_ge(s_py, p.cnt["dpy"])
        g.collective_compute(
            "AllGather", mybir.AluOpType.bypass,
            replica_groups=[list(range(8))],
            ins=[G["cc1_in"].ap().opt()],
            outs=[G["cc1_out"].ap().opt()],
        ).then_inc(s_cc, 1)
        g.wait_ge(s_cc, p.cnt["cc1"] + 1)
        g.dma_start(out=G["gath1_sb"][:].rearrange("p (r c) -> p r c", c=48),
                    in_=G["cc1_out"].ap().rearrange("(r p) c -> p r c", p=128)).then_inc(s_g1, 16)
    p.bump("cc1", 1)
    p.bump("dg1", 16)

    @block.vector
    def _(v):
        v.wait_ge(s_g1, p.cnt["dg1"])
        hL = G["haloL"]; hR = G["haloR"]; gg = G["gath1_sb"]; m2 = G["msel2_sb"]
        p.edge(v.memset(hL[:], 0.0), v)
        p.edge(v.memset(hR[:], 0.0), v)
        for r in range(8):
            p.edge(v.scalar_tensor_tensor(hL[:], gg[:, r * 48:(r + 1) * 48],
                                          m2[:, r:r + 1], hL[:],
                                          ALU.mult, ALU.add), v)
            p.edge(v.scalar_tensor_tensor(hR[:], gg[:, r * 48:(r + 1) * 48],
                                          m2[:, 8 + r:9 + r], hR[:],
                                          ALU.mult, ALU.add), v)
        v.engine_nop().then_inc(s_hl, 1)
    p.bump("vhl", 1)


def emit_m_conv(p, block, G):
    s_xl = p.sem("dxl"); s_hc = p.sem("ghl"); s_cv = p.sem("vcv")
    s_sl = p.sem("ssl"); s_zs = p.sem("dzs"); s_hl = p.sem("vhl")
    b_xl = p.cnt["dxl"]; b_hc = p.cnt["ghl"]; b_cv = p.cnt["vcv"]
    b_sl = p.cnt["ssl"]; b_zs = p.cnt["dzs"]
    xpre = G["xpre"]; cva = G["cv_acc"]; zst = G["z_stg"]
    cc = G["conv_c_sb"]; cb = G["conv_b_sb"]

    @block.sync
    def _(s):
        for it in range(16):
            if it >= 2:
                s.wait_ge(s_cv, b_cv + it - 1)
            s.dma_start(out=xpre[it % 2][:, 1:2049],
                        in_=G["xz_dram"].ap()[:, it * 2048:(it + 1) * 2048]).then_inc(s_xl, 16)
    p.bump("dxl", 16 * 16)

    @block.gpsimd
    def _(g):
        g.wait_ge(s_hl, p.cnt["vhl"])
        for it in range(16):
            g.wait_ge(s_xl, b_xl + 16 * (it + 1))
            if it >= 2:
                g.wait_ge(s_cv, b_cv + it - 1)
            g.tensor_copy(out=xpre[it % 2][:, 0:1], in_=G["haloL"][:, it * 3 + 2: it * 3 + 3])
            g.tensor_copy(out=xpre[it % 2][:, 2049:2051],
                          in_=G["haloR"][:, it * 3: it * 3 + 2]).then_inc(s_hc, 1)
    p.bump("ghl", 16)

    @block.vector
    def _(v):
        v.memset(G["zeros_bf"][:], 0.0)
        for it in range(16):
            v.wait_ge(s_xl, b_xl + 16 * (it + 1))
            v.wait_ge(s_hc, b_hc + it + 1)
            if it >= 2:
                v.wait_ge(s_sl, b_sl + it - 1)
            x = xpre[it % 2]
            a = cva[it % 2]
            v.tensor_scalar(a[:], x[:, 0:2048], cc[:, it * 4: it * 4 + 1], None, ALU.mult)
            for j in range(1, 4):
                v.scalar_tensor_tensor(a[:], x[:, j:j + 2048], cc[:, it * 4 + j: it * 4 + j + 1],
                                       a[:], ALU.mult, ALU.add)
            v.engine_nop().then_inc(s_cv, 1)
    p.bump("vcv", 16)

    @block.scalar
    def _(s):
        for it in range(16):
            s.wait_ge(s_cv, b_cv + it + 1)
            if it < 8:
                s.activation(out=G["u_bf"][:, it * T:(it + 1) * T], in_=cva[it % 2][:],
                             func=AF.Silu, bias=cb[:, it:it + 1]).then_inc(s_sl, 1)
            else:
                if it >= 12:
                    s.wait_ge(s_zs, b_zs + 16 * (it - 11))
                s.activation(out=zst[it % 4][:], in_=cva[it % 2][:],
                             func=AF.Silu, bias=cb[:, it:it + 1]).then_inc(s_sl, 1)
    p.bump("ssl", 16)

    @block.sync
    def _(s):
        for dt in range(8):
            it = 8 + dt
            s.wait_ge(s_sl, b_sl + it + 1)
            s.dma_start(out=G["z_dram"].ap()[:, dt * T:(dt + 1) * T],
                        in_=zst[it % 4][:]).then_inc(s_zs, 16)
    p.bump("dzs", 16 * 8)


def emit_m_xdbl(p, block, G):
    nc = p.nc
    s_sl = p.sem("ssl")
    s_x2 = p.sem("tx2"); s_xe = p.sem("sxd")
    s_dt = p.sem("tdt"); s_sab = p.sem("ssp"); s_vsp = p.sem("vsp")
    s_dd = p.sem("vdd"); s_dm = p.sem("ddm"); s_gt = p.sem("gdt"); s_pp = p.sem("spp")
    b_sl = p.cnt["ssl"] - 16
    b_x2 = p.cnt["tx2"]; b_xe = p.cnt["sxd"]; b_dt = p.cnt["tdt"]
    b_sab = p.cnt["ssp"]; b_vsp = p.cnt["vsp"]; b_dd = p.cnt["vdd"]
    b_dm = p.cnt["ddm"]; b_gt = p.cnt["gdt"]

    psx = G["ps_xd"]; psd = G["ps_dt"]
    dtl = G["dtlow_bf"]; bc = G["bc_bf"]
    u = G["u_bf"]; dlt = G["dlt_bf"]
    inv = G["invdt_sb"]; inv2 = G["inv2_sb"]

    s_iv = p.sem("viv")
    @block.vector
    def _(v):
        p.edge(v.tensor_tensor(inv2[:], inv[:], inv[:], ALU.add), v)
        v.engine_nop().then_inc(s_iv, 1)
    p.bump("viv", 1)

    @block.tensor
    def _(t):
        t.wait_ge(s_sl, b_sl + 8)
        t.wait_ge(p.sems["dmw"], p.cnt["dmw"])
        for w in range(4):
            if w >= 2:
                t.wait_ge(s_xe, b_xe + 2 * (w - 1))
            pp = psx[w % 2]
            for kt in range(DT):
                mi = t.matmul(pp[0:96, :512],
                              G["w_xd_sb"][:, kt * 96:(kt + 1) * 96],
                              u[:, kt * T + w * 512: kt * T + (w + 1) * 512],
                              start=(kt == 0), stop=(kt == DT - 1))
            mi.then_inc(s_x2, 1)
    p.bump("tx2", 4)

    @block.scalar
    def _(s):
        for w in range(4):
            s.wait_ge(s_x2, b_x2 + w + 1)
            s.copy(out=dtl[0:64, w * 512:(w + 1) * 512],
                   in_=psx[w % 2][0:64, :512]).then_inc(s_xe, 1)
            s.copy(out=bc[0:32, w * 512:(w + 1) * 512],
                   in_=psx[w % 2][64:96, :512]).then_inc(s_xe, 1)
    p.bump("sxd", 8)

    QL = [(mt, w) for mt in range(8) for w in range(4)]

    @block.tensor
    def _(t):
        t.wait_ge(s_xe, b_xe + 8)
        for i, (mt, w) in enumerate(QL):
            if i >= 2:
                t.wait_ge(s_sab, b_sab + i - 1)
                t.wait_ge(s_vsp, b_vsp + i - 1)
            t.matmul(psd[i % 2][:, :512],
                     G["w_dt_sb"][:, mt * 128:(mt + 1) * 128],
                     dtl[0:64, w * 512:(w + 1) * 512],
                     start=True, stop=True).then_inc(s_dt, 1)
    p.bump("tdt", 32)

    @block.scalar
    def _(s):
        s.wait_ge(s_iv, p.cnt["viv"])
        for i, (mt, w) in enumerate(QL):
            s.wait_ge(s_dt, b_dt + i + 1)
            if i >= 2:
                s.wait_ge(s_vsp, b_vsp + i - 1)
            ab = G["sp_ab"]
            p.edge(s.activation(out=ab[i % 2][:], in_=psd[i % 2][:, :512], func=AF.Abs,
                                bias=inv2[:, mt:mt + 1]), s)
            p.edge(s.activation(out=ab[i % 2][:], in_=ab[i % 2][:], func=AF.Exp, scale=-1.0), s)
            s.activation(out=ab[i % 2][:], in_=ab[i % 2][:], func=AF.Ln,
                         bias=1.0).then_inc(s_sab, 1)
    p.bump("ssp", 32)

    @block.vector
    def _(v):
        for i, (mt, w) in enumerate(QL):
            v.wait_ge(s_dt, b_dt + i + 1)
            rl = G["sp_rl"]
            v.tensor_scalar(rl[:], psd[i % 2][:, :512], inv2[:, mt:mt + 1], 0.0,
                            ALU.add, ALU.max)
            v.wait_ge(s_sab, b_sab + i + 1)
            v.tensor_tensor(dlt[:, mt * T + w * 512: mt * T + (w + 1) * 512],
                            rl[:], G["sp_ab"][i % 2][:], ALU.add).then_inc(s_vsp, 1)
        for mt in range(8):
            if mt >= 2:
                v.wait_ge(s_dm, b_dm + 16 * (mt - 1))
                v.wait_ge(s_gt, b_gt + mt - 1)
            v.tensor_tensor_scan(out=G["dl_stg"][mt % 2][:],
                                 data0=G["zeros_bf"][:],
                                 data1=dlt[:, mt * T:(mt + 1) * T],
                                 initial=0.0, op0=ALU.add, op1=ALU.add).then_inc(s_dd, 1)
    p.bump("vsp", 32)
    p.bump("vdd", 8)

    @block.gpsimd
    def _(g):
        for mt in range(8):
            g.wait_ge(s_dd, b_dd + mt + 1)
            g.tensor_copy(out=G["dTot_col"][:, mt:mt + 1],
                          in_=G["dl_stg"][mt % 2][:, 2047:2048]).then_inc(s_gt, 1)
    p.bump("gdt", 8)

    @block.sync
    def _(s):
        for mt in range(8):
            s.wait_ge(s_dd, b_dd + mt + 1)
            s.dma_start(out=G["dlt_dram"].ap()[:, mt * T:(mt + 1) * T],
                        in_=G["dl_stg"][mt % 2][:]).then_inc(s_dm, 16)
    p.bump("ddm", 16 * 8)

    @block.scalar
    def _(s):
        for mt in range(8):
            s.wait_ge(s_gt, b_gt + mt + 1)
            for n in range(1, NST + 1):
                mi = s.activation(out=G["p_all"][:, mt * 16 + n - 1: mt * 16 + n],
                                  in_=G["dTot_col"][:, mt:mt + 1], func=AF.Exp,
                                  scale=-float(n))
        mi.then_inc(s_pp, 1)
    p.bump("spp", 1)


def emit_m_scan(p, block, G):
    nc = p.nc
    s_bp = p.sem("tbp"); s_cp = p.sem("tcp"); s_da = p.sem("sda"); s_ce = p.sem("sce")
    s_du = p.sem("vdu"); s_sc = p.sem("vsc"); s_hc = p.sem("ghc"); s_ye = p.sem("sye")
    s_ya = p.sem("vya")
    b_bp = p.cnt["tbp"]; b_cp = p.cnt["tcp"]; b_da = p.cnt["sda"]; b_ce = p.cnt["sce"]
    b_du = p.cnt["vdu"]; b_sc = p.cnt["vsc"]; b_hc = p.cnt["ghc"]; b_ye = p.cnt["sye"]
    b_ya = p.cnt["vya"]

    psB = G["ps_B"]; psC = G["ps_C"]
    sel = G["sel_sb"]; bc = G["bc_bf"]; dlt = G["dlt_bf"]; u = G["u_bf"]
    dA = G["dA_bf"]; dBu = G["dBu_bf"]; hh = G["h_bf"]; hC = G["hC_bf"]
    du = G["du_f"]; yd = G["y_dt"]; yl = G["y_locbf"]
    QL = [(dt, n) for dt in range(8) for n in range(1, NST + 1)]

    @block.tensor
    def _(t):
        for i, (dt, n) in enumerate(QL):
            if i >= 1:
                t.wait_ge(s_du, b_du + i)
            for c in range(4):
                mi = t.matmul(psB[:, c * 512:(c + 1) * 512],
                              sel[:, (n - 1) * 128: n * 128],
                              bc[0:32, c * 512:(c + 1) * 512],
                              start=True, stop=True)
            mi.then_inc(s_bp, 1)
            if i >= 1:
                t.wait_ge(s_hc, b_hc + i)
            for c in range(4):
                mi = t.matmul(psC[:, c * 512:(c + 1) * 512],
                              sel[:, (16 + n - 1) * 128: (16 + n) * 128],
                              bc[0:32, c * 512:(c + 1) * 512],
                              start=True, stop=True)
            mi.then_inc(s_cp, 1)
    p.bump("tbp", 128)
    p.bump("tcp", 128)

    @block.scalar
    def _(s):
        for i, (dt, n) in enumerate(QL):
            if i >= 2:
                s.wait_ge(s_sc, b_sc + i - 1)
            s.activation(out=dA[i % 2][:], in_=dlt[:, dt * T:(dt + 1) * T],
                         func=AF.Exp, scale=-float(n)).then_inc(s_da, 1)
            s.wait_ge(s_cp, b_cp + i + 1)
            if i >= 2:
                s.wait_ge(s_hc, b_hc + i - 1)
            s.copy(out=G["Cbc_bf"][i % 2][:], in_=psC[:]).then_inc(s_ce, 1)
            if n == 4 and dt >= 1:
                s.wait_ge(s_ya, b_ya + 16 * dt)
                s.copy(out=yl[:, (dt - 1) * T: dt * T],
                       in_=yd[(dt - 1) % 2][:]).then_inc(s_ye, 1)
        s.wait_ge(s_ya, b_ya + 128)
        s.copy(out=yl[:, 7 * T: 8 * T], in_=yd[7 % 2][:]).then_inc(s_ye, 1)
    p.bump("sda", 128)
    p.bump("sce", 128)
    p.bump("sye", 8)

    @block.vector
    def _(v):
        for i, (dt, n) in enumerate(QL):
            if n == 1:
                v.tensor_tensor(du[:], dlt[:, dt * T:(dt + 1) * T],
                                u[:, dt * T:(dt + 1) * T], ALU.mult)
            v.wait_ge(s_bp, b_bp + i + 1)
            v.tensor_tensor(dBu[i % 2][:], psB[:], du[:], ALU.mult).then_inc(s_du, 1)
            if i >= 2:
                j = i - 2
                dtp, np_ = QL[j]
                v.wait_ge(s_hc, b_hc + j + 1)
                yb = yd[dtp % 2]
                if np_ == 1:
                    if dtp >= 2:
                        v.wait_ge(s_ye, b_ye + dtp - 1)
                    v.tensor_copy(out=yb[:], in_=hC[j % 3][:]).then_inc(s_ya, 1)
                else:
                    v.tensor_tensor(yb[:], yb[:], hC[j % 3][:],
                                    ALU.add).then_inc(s_ya, 1)
            v.wait_ge(s_da, b_da + i + 1)
            if i >= 2:
                v.wait_ge(s_hc, b_hc + i - 1)
            v.tensor_tensor_scan(out=hh[i % 2][:], data0=dA[i % 2][:], data1=dBu[i % 2][:],
                                 initial=0.0, op0=ALU.mult, op1=ALU.add).then_inc(s_sc, 1)
        for j in (126, 127):
            dtp, np_ = QL[j]
            v.wait_ge(s_hc, b_hc + j + 1)
            v.tensor_tensor(yd[7 % 2][:], yd[7 % 2][:], hC[j % 3][:],
                            ALU.add).then_inc(s_ya, 1)
    p.bump("vdu", 128)
    p.bump("vsc", 128)
    p.bump("vya", 128)

    @block.gpsimd
    def _(g):
        for i, (dt, n) in enumerate(QL):
            g.wait_ge(s_sc, b_sc + i + 1)
            g.wait_ge(s_ce, b_ce + i + 1)
            if i >= 3:
                g.wait_ge(s_ya, b_ya + i - 2)
            g.tensor_copy(out=G["s_all"][:, i:i + 1], in_=hh[i % 2][:, 2047:2048])
            g.tensor_tensor(hC[i % 3][:], hh[i % 2][:], G["Cbc_bf"][i % 2][:],
                            ALU.mult).then_inc(s_hc, 1)
    p.bump("ghc", 128)


def emit_m_exchange(p, block, G):
    s_hc = p.sem("ghc"); s_pp = p.sem("spp"); s_sf = p.sem("gsf")
    s_ps = p.sem("dps"); s_cc = p.sem("cc2"); s_g2 = p.sem("dg2"); s_h0 = p.sem("vh0")

    @block.gpsimd
    def _(g):
        g.wait_ge(s_hc, p.cnt["ghc"])
        g.tensor_copy(out=G["s_fin"][:], in_=G["s_all"][:]).then_inc(s_sf, 1)
    p.bump("gsf", 1)

    @block.sync
    def _(s):
        s.wait_ge(s_sf, p.cnt["gsf"])
        s.wait_ge(s_pp, p.cnt["spp"])
        s.dma_start(out=G["cc2_in"].ap()[:, 0:128], in_=G["s_fin"][:]).then_inc(s_ps, 16)
        s.dma_start(out=G["cc2_in"].ap()[:, 128:256], in_=G["p_all"][:]).then_inc(s_ps, 16)
    p.bump("dps", 32)

    @block.gpsimd
    def _(g):
        g.wait_ge(s_ps, p.cnt["dps"])
        g.collective_compute(
            "AllGather", mybir.AluOpType.bypass,
            replica_groups=[list(range(8))],
            ins=[G["cc2_in"].ap().opt()],
            outs=[G["cc2_out"].ap().opt()],
        ).then_inc(s_cc, 1)
        g.wait_ge(s_cc, p.cnt["cc2"] + 1)
        g.dma_start(out=G["gath2_sb"][:].rearrange("p (r c) -> p r c", c=256),
                    in_=G["cc2_out"].ap().rearrange("(r p) c -> p r c", p=128)).then_inc(s_g2, 16)
    p.bump("cc2", 1)
    p.bump("dg2", 16)

    @block.vector
    def _(v):
        v.wait_ge(s_g2, p.cnt["dg2"])
        h0 = G["h0_all"]; gg = G["gath2_sb"]; ms = G["msel_sb"]
        Ps = G["psel"]; Ss = G["ssel"]
        p.edge(v.memset(h0[:], 0.0), v)
        for j in (2, 1, 0):
            p.edge(v.memset(Ps[:], 0.0), v)
            p.edge(v.memset(Ss[:], 0.0), v)
            for r in range(8):
                p.edge(v.scalar_tensor_tensor(Ps[:], gg[:, r * 256 + 128: r * 256 + 256],
                                              ms[:, j * 8 + r: j * 8 + r + 1], Ps[:],
                                              ALU.mult, ALU.add), v)
                p.edge(v.scalar_tensor_tensor(Ss[:], gg[:, r * 256: r * 256 + 128],
                                              ms[:, j * 8 + r: j * 8 + r + 1], Ss[:],
                                              ALU.mult, ALU.add), v)
            p.edge(v.tensor_tensor(h0[:], h0[:], Ps[:], ALU.mult), v)
            p.edge(v.tensor_tensor(h0[:], h0[:], Ss[:], ALU.add), v)
        v.engine_nop().then_inc(s_h0, 1)
    p.bump("vh0", 1)


def emit_m_corr(p, block, G):
    nc = p.nc
    s_h0 = p.sem("vh0")
    s_dl = p.sem("ddl"); s_wn = p.sem("swn"); s_c2 = p.sem("tc2")
    s_ac = p.sem("vac"); s_fn = p.sem("gfn"); s_yw = p.sem("dyw")
    b_dl = p.cnt["ddl"]; b_wn = p.cnt["swn"]; b_c2 = p.cnt["tc2"]
    b_ac = p.cnt["vac"]; b_fn = p.cnt["gfn"]

    psC = G["ps_c2"]; sel = G["sel_sb"]; bc = G["bc_bf"]
    dl = G["dl_t"]; wn = G["wn_bf"]; acc = G["acc_f"]; t1 = G["t1_bf"]
    h0 = G["h0_all"]; u = G["u_bf"]; yl = G["y_locbf"]
    NSEQ = [(dt, n) for dt in range(8) for n in range(1, NST + 1)]

    @block.sync
    def _(s):
        for dt in range(8):
            if dt >= 2:
                s.wait_ge(s_wn, b_wn + 16 * (dt - 1))
            s.dma_start(out=dl[dt % 2][:],
                        in_=G["dlt_dram"].ap()[:, dt * T:(dt + 1) * T]).then_inc(s_dl, 16)
    p.bump("ddl", 16 * 8)

    @block.scalar
    def _(s):
        for i, (dt, n) in enumerate(NSEQ):
            if n == 1:
                s.wait_ge(s_dl, b_dl + 16 * (dt + 1))
            if i >= 2:
                s.wait_ge(s_ac, b_ac + i - 1)
            s.activation(out=wn[i % 2][:], in_=dl[dt % 2][:], func=AF.Exp,
                         scale=-float(n)).then_inc(s_wn, 1)
    p.bump("swn", 128)

    @block.tensor
    def _(t):
        for i, (dt, n) in enumerate(NSEQ):
            if i >= 2:
                t.wait_ge(s_ac, b_ac + i - 1)
            for c in range(4):
                mi = t.matmul(psC[i % 2][:, c * 512:(c + 1) * 512],
                              sel[:, (16 + n - 1) * 128: (16 + n) * 128],
                              bc[0:32, c * 512:(c + 1) * 512],
                              start=True, stop=True)
            mi.then_inc(s_c2, 1)
    p.bump("tc2", 128)

    @block.vector
    def _(v):
        v.wait_ge(s_h0, p.cnt["vh0"])
        for i, (dt, n) in enumerate(NSEQ):
            v.wait_ge(s_c2, b_c2 + i + 1)
            v.wait_ge(s_wn, b_wn + i + 1)
            v.tensor_tensor(t1[:], psC[i % 2][:], wn[i % 2][:], ALU.mult)
            col = h0[:, dt * 16 + n - 1: dt * 16 + n]
            if n == 1:
                v.scalar_tensor_tensor(acc[:], t1[:], col, G["zeros_bf"][:],
                                       ALU.mult, ALU.add).then_inc(s_ac, 1)
            else:
                v.scalar_tensor_tensor(acc[:], t1[:], col, acc[:],
                                       ALU.mult, ALU.add).then_inc(s_ac, 1)
            if n == NST:
                v.scalar_tensor_tensor(G["tmp_f"][:], u[:, dt * T:(dt + 1) * T],
                                       G["d_col_sb"][:, dt:dt + 1], acc[:],
                                       ALU.mult, ALU.add)
                v.tensor_tensor(yl[:, dt * T:(dt + 1) * T], yl[:, dt * T:(dt + 1) * T],
                                G["tmp_f"][:], ALU.add).then_inc(s_fn, 1)
    p.bump("vac", 128)
    p.bump("gfn", 8)

    @block.sync
    def _(s):
        for dt in range(8):
            s.wait_ge(s_fn, b_fn + dt + 1)
            s.dma_start(out=G["y_dram"].ap()[:, dt * T:(dt + 1) * T],
                        in_=yl[:, dt * T:(dt + 1) * T]).then_inc(s_yw, 16)
    p.bump("dyw", 16 * 8)


def emit_m_out(p, block, G):
    nc = p.nc
    s_ld = p.sem("dm8"); s_mw = p.sem("dmw")
    s_om = p.sem("tom"); s_oe = p.sem("voe")
    b_om = p.cnt["tom"]; b_oe = p.cnt["voe"]

    @block.sync
    def _(s):
        s.dma_start(out=G["x_res"][:], in_=G["xres_dram"].ap()).then_inc(s_ld, 16)
        s.dma_start(out=G["y_sb"][:], in_=G["y_dram"].ap()).then_inc(s_ld, 16)
        s.dma_start(out=G["z_sb"][:], in_=G["z_dram"].ap()).then_inc(s_ld, 16)
    p.bump("dm8", 16 * 3)

    @block.gpsimd
    def _(g):
        g.dma_start(out=G["w_out_sb"][:], in_=G["in_woutT"]).then_inc(s_mw, 16)
        g.dma_start(out=G["bout_bf"][:], in_=G["in_bout"]).then_inc(s_mw, 16)
    p.bump("dmw", 16 * 2)

    ps = G["ps_o"]

    @block.tensor
    def _(t):
        t.wait_ge(s_ld, p.cnt["dm8"])
        t.wait_ge(s_mw, p.cnt["dmw"])
        for tt in range(NTT):
            if tt >= 2:
                t.wait_ge(s_oe, b_oe + tt - 1)
            pp = ps[tt % 2]
            for dh in range(2):
                t.matmul(pp[:, dh * 512:(dh + 1) * 512],
                         G["ones_bf"][0:1, 0:128],
                         G["bout_bf"][0:1, dh * 512:(dh + 1) * 512],
                         start=True, stop=False)
                for kt in range(16):
                    src = G["y_sb"] if kt < 8 else G["z_sb"]
                    mi = t.matmul(pp[:, dh * 512:(dh + 1) * 512],
                                  src[:, (kt % 8) * T + tt * 128: (kt % 8) * T + (tt + 1) * 128],
                                  G["w_out_sb"][:, kt * 1024 + dh * 512: kt * 1024 + (dh + 1) * 512],
                                  start=False, stop=(kt == 15))
            mi.then_inc(s_om, 1)
    p.bump("tom", NTT)

    @block.vector
    def _(v):
        v.wait_ge(s_ld, p.cnt["dm8"])
        for tt in range(NTT):
            v.wait_ge(s_om, b_om + tt + 1)
            xt = G["x_res"][:, tt * D:(tt + 1) * D]
            v.tensor_tensor(xt, ps[tt % 2][:, :], xt, ALU.add).then_inc(s_oe, 1)
    p.bump("voe", NTT)


def emit_output_dram(p, block, G):
    s_st = p.sem("dst")

    @block.sync
    def _(s):
        for nm, c in sorted(p.cnt.items()):
            if c > 0 and nm != "dst":
                s.wait_ge(p.sems[nm], c)
        src = G["xres_dram"].ap().rearrange("p (tt d) -> p tt d", d=D)
        dst = G["in_out"].rearrange("(tt p) d -> p tt d", p=128)
        s.dma_start(out=dst, in_=src).then_inc(s_st, 16)
        s.wait_ge(s_st, 16)
    p.bump("dst", 16)


def emit_output(p, block, G):
    s_st = p.sem("dst")
    s_ev = p.sem("vev")
    dma_final = [(nm, c) for nm, c in sorted(p.cnt.items())
                 if nm.startswith("d") and nm != "dst" and c > 0]

    @block.sync
    def _(s):
        s.wait_ge(s_ev, p.cnt["vev"])
        for nm, c in dma_final:
            s.wait_ge(p.sems[nm], c)
        src = G["x_res"][:].rearrange("p (tt d) -> p tt d", d=D)
        dst = G["in_out"].rearrange("(tt p) d -> p tt d", p=128)
        s.dma_start(out=dst, in_=src).then_inc(s_st, 16)
        s.wait_ge(s_st, 16)
    p.bump("dst", 16)


def alloc_ln(p, G, scope, tag):
    G["ln_tok"] = p.sbuf(f"ln_tok{tag}", [128, 2 * D], BF16, scope)
    G["ln_f"] = p.sbuf(f"ln_f{tag}", [128, DT * T], BF16, scope)
    for nm, w in [("stats", 16), ("s1_all", 16), ("s2_all", 16), ("mean_all", 16),
                  ("var_all", 16), ("rstd_all", 16), ("nmr_all", 16)]:
        G[nm] = p.sbuf(f"{nm}{tag}", [128, w], F32, scope)
    G["sq"] = p.sbuf(f"sq{tag}", [128, D], F32, scope)


def alloc_router(p, G, scope, tag):
    G["cwall"] = p.sbuf(f"cwall{tag}", [128, NTT * 8], F32, scope)
    G["cw2all"] = p.sbuf(f"cw2all{tag}", [128, NTT * 8], F32, scope)
    G["comb_w"] = p.sbuf(f"comb_w{tag}", [128, 8], F32, scope)
    G["comb_w2"] = p.sbuf(f"comb_w2{tag}", [128, 8], F32, scope)
    G["c_m"] = p.sbuf(f"c_m{tag}", [128, 4], F32, scope)
    G["comb_half"] = p.sbuf(f"comb_half{tag}", [128, NTT * 9], F32, scope)


def emit_ffn_pass(p, block, G, scope, pass_idx, tag):
    alloc_ln(p, G, scope, tag)
    alloc_router(p, G, scope, tag)
    G["h1q"] = p.sbuf(f"h1q{tag}", [128, 8 * 1024], BF16, scope)
    G["pool_A"] = p.sbuf(f"pool_A{tag}", [128, 2 * 8192], BF16, scope)
    G["pool_B"] = p.sbuf(f"pool_B{tag}", [128, 2 * 8192], BF16, scope)
    with contextlib.ExitStack() as lnscope:
        G["ps_lg"] = [p.psum(f"ps_lg0{tag}", [128, 8], F32, lnscope),
                      p.psum(f"ps_lg1{tag}", [128, 8], F32, lnscope)]
        emit_ln_router(p, block, G, pass_idx, with_router=True)
        barrier(p, block, tensor_done=("tlg", p.cnt["tlg"]))
    with contextlib.ExitStack() as exscope:
        G["ps_y"] = [p.psum(f"ps_y{k}{tag}", [128, 1024], F32, exscope) for k in range(2)]
        G["ps_h1"] = [p.psum(f"ps_h1{k}{tag}", [128, 1024], F32, exscope) for k in range(2)]
        emit_ffn_experts(p, block, G, pass_idx)
        barrier(p, block, tensor_done=("tmm2", p.cnt["tmm2"]))


def build(stage=STAGE):
    p = Prog()
    nc = p.nc
    G = {}

    G["in_x"] = nc.dram_tensor("x_sh", [T, D], F32, kind="ExternalInput").ap()
    G["in_w1T"] = nc.dram_tensor("w1T", [NE, D, FF], F32, kind="ExternalInput").ap()
    G["in_w2T"] = nc.dram_tensor("w2T", [NE, FF, D], F32, kind="ExternalInput").ap()
    G["in_b1col"] = nc.dram_tensor("b1col", [128, 32 * 9], F32, kind="ExternalInput").ap()
    G["in_b2r"] = nc.dram_tensor("b2row", [1, 9 * D], F32, kind="ExternalInput").ap()
    G["in_ones"] = nc.dram_tensor("ones_r", [1, 128], F32, kind="ExternalInput").ap()
    G["in_rw"] = nc.dram_tensor("rw_prep", [128, 64], F32, kind="ExternalInput").ap()
    G["in_rb"] = nc.dram_tensor("rb_bc", [128, 8], F32, kind="ExternalInput").ap()
    G["in_out"] = nc.dram_tensor("out", [T, D], F32, kind="ExternalOutput").ap()
    G["w1c"] = nc.dram_tensor("w1c", [NE, NQ, 128, 8192], BF16, kind="Internal")
    G["lnd"] = nc.dram_tensor("lnd", [T, D], BF16, kind="Internal")
    G["w2c"] = nc.dram_tensor("w2c", [NE, NQ, 128, 8192], BF16, kind="Internal")
    G["in_winT"] = nc.dram_tensor("winT", [128, 8 * 2048], F32, kind="ExternalInput").ap()
    G["in_wxdT"] = nc.dram_tensor("wxdT", [128, 8 * 96], F32, kind="ExternalInput").ap()
    G["in_wdtT"] = nc.dram_tensor("wdtT", [64, 1024], F32, kind="ExternalInput").ap()
    G["in_woutT"] = nc.dram_tensor("woutT", [128, 16 * 1024], F32, kind="ExternalInput").ap()
    G["in_sel"] = nc.dram_tensor("sel32", [32, 32 * 128], F32, kind="ExternalInput").ap()
    G["in_bout"] = nc.dram_tensor("bout_r", [1, 1024], F32, kind="ExternalInput").ap()
    G["in_convc"] = nc.dram_tensor("convc", [128, 64], F32, kind="ExternalInput").ap()
    G["in_convb"] = nc.dram_tensor("convb", [128, 16], F32, kind="ExternalInput").ap()
    G["in_invdt"] = nc.dram_tensor("invdtc", [128, 8], F32, kind="ExternalInput").ap()
    G["in_dcol"] = nc.dram_tensor("dcolc", [128, 8], F32, kind="ExternalInput").ap()
    G["in_msel"] = nc.dram_tensor("mselc", [128, 24], F32, kind="ExternalInput").ap()
    G["in_msel2"] = nc.dram_tensor("msel2c", [128, 16], F32, kind="ExternalInput").ap()
    G["xz_dram"] = nc.dram_tensor("xz_d", [128, 16 * 2048], BF16, kind="Internal")
    G["z_dram"] = nc.dram_tensor("z_d", [128, 8 * 2048], BF16, kind="Internal")
    G["dlt_dram"] = nc.dram_tensor("dlt_d", [128, 8 * 2048], BF16, kind="Internal")
    G["y_dram"] = nc.dram_tensor("y_d", [128, 8 * 2048], BF16, kind="Internal")
    G["xres_dram"] = nc.dram_tensor("xres_d", [128, NTT * D], F32, kind="Internal")
    G["cc1_in"] = nc.dram_tensor("cc1_in", [128, 48], BF16, kind="Internal")
    G["cc1_out"] = nc.dram_tensor("cc1_out", [8 * 128, 48], BF16, kind="Internal")
    G["cc2_in"] = nc.dram_tensor("cc2_in", [128, 256], F32, kind="Internal")
    G["cc2_out"] = nc.dram_tensor("cc2_out", [8 * 128, 256], F32, kind="Internal")

    p.setup_consts([1e-5, -0.5, 1.0] + [-(n + 1.0) for n in range(16)])
    G["b1col"] = p.sbuf("b1col_sb", [128, 32 * 9], F32)
    G["b2row_bf"] = p.sbuf("b2rowbf", [1, 9 * D], BF16)
    G["ones_bf"] = p.sbuf("onesbf", [1, 128], BF16)
    G["rw_bf"] = p.sbuf("rwbf", [128, 64], BF16)
    G["rb_bc"] = p.sbuf("rbbc", [128, 8], F32)
    G["haloL"] = p.sbuf("haloL", [128, 48], BF16)
    G["haloR"] = p.sbuf("haloR", [128, 48], BF16)
    G["conv_c_sb"] = p.sbuf("convc_sb", [128, 64], F32)
    G["conv_b_sb"] = p.sbuf("convb_sb", [128, 16], F32)
    G["invdt_sb"] = p.sbuf("invdt_sb", [128, 8], F32)
    G["inv2_sb"] = p.sbuf("inv2_sb", [128, 8], F32)
    G["d_col_sb"] = p.sbuf("dcol_sb", [128, 8], F32)
    G["msel_sb"] = p.sbuf("msel_sb", [128, 24], F32)
    G["msel2_sb"] = p.sbuf("msel2_sb", [128, 16], F32)

    s_ldx = p.sem("dldx")
    s_ldxg = p.sem("dldxg")

    with nc.Block() as block:
        @block.gpsimd
        def _(g):
            g.dma_start(out=G["b2row_bf"][:], in_=G["in_b2r"]).then_inc(s_ldxg, 16)
            g.dma_start(out=G["ones_bf"][:], in_=G["in_ones"]).then_inc(s_ldxg, 16)
            g.dma_start(out=G["rw_bf"][:], in_=G["in_rw"]).then_inc(s_ldxg, 16)
        p.bump("dldxg", 16 * 3)

        s_vc = p.sem("vconst")
        done = False

        with contextlib.ExitStack() as sA:
            G["x_res"] = p.sbuf("x_res_a", [128, NTT * D], F32, sA)

            @block.sync
            def _(s):
                s.dma_start(out=G["x_res"][:].rearrange("p (tt d) -> p tt d", d=D),
                            in_=G["in_x"].rearrange("(tt p) d -> p tt d", p=128)).then_inc(s_ldx, 16)
                for dst, src in [("b1col", "in_b1col"), ("rb_bc", "in_rb"),
                                 ("conv_c_sb", "in_convc"), ("conv_b_sb", "in_convb"),
                                 ("invdt_sb", "in_invdt"), ("d_col_sb", "in_dcol"),
                                 ("msel_sb", "in_msel"), ("msel2_sb", "in_msel2")]:
                    s.dma_start(out=G[dst][:], in_=G[src]).then_inc(s_ldx, 16)
            p.bump("dldx", 16 * 9)

            @block.vector
            def _(v):
                v.wait_ge(s_ldx, p.cnt["dldx"])
                v.wait_ge(s_ldxg, p.cnt["dldxg"])
                p.emit_const_init(v)
                v.engine_nop().then_inc(s_vc, 1)
            p.bump("vconst")

            @block.tensor
            def _(t):
                t.wait_ge(s_ldx, p.cnt["dldx"])
                t.wait_ge(s_ldxg, p.cnt["dldxg"])
                t.wait_ge(s_vc, p.cnt["vconst"])

            @block.scalar
            def _(s):
                s.wait_ge(s_ldx, p.cnt["dldx"])
                s.wait_ge(s_ldxg, p.cnt["dldxg"])
                s.wait_ge(s_vc, p.cnt["vconst"])

            with contextlib.ExitStack() as f1:
                emit_ffn_pass(p, block, G, f1, 0, "_a")
            mbar(p, block, tensor_done=("tmm2", p.cnt["tmm2"]),
                 dma_waits=[("dcache", p.cnt["dcache"])])

            if stage == "ffn1":
                emit_output(p, block, G)
                done = True

            if stage == "mnull":
                s_sp = p.sem("dsp")

                @block.sync
                def _(s):
                    s.wait_ge(p.sems["vev"], p.cnt["vev"])
                    s.dma_start(out=G["xres_dram"].ap(), in_=G["x_res"][:]).then_inc(s_sp, 16)
                p.bump("dsp", 16)
                emit_output_dram(p, block, G)
                done = True

            if not done:
                with contextlib.ExitStack() as mln:
                    alloc_ln(p, G, mln, "_m")
                    G["w_in_sb"] = p.sbuf("w_in_sb", [128, 8 * 2048], BF16, mln)
                    G["xz_stg"] = [p.sbuf(f"xz_stg{k}", [128, 512], BF16, mln) for k in range(2)]
                    G["pay_sb"] = p.sbuf("pay_sb", [128, 48], BF16, mln)
                    G["gath1_sb"] = p.sbuf("gath1_sb", [128, 8 * 48], BF16, mln)
                    G["ps_xz"] = [p.psum(f"ps_xz{k}", [128, 512], F32, mln) for k in range(2)]
                    emit_ln_router(p, block, G, 2, with_router=False)
                    if stage == "mln":
                        s_sp = p.sem("dsp")

                        @block.sync
                        def _(s):
                            s.wait_ge(p.sems["vln"], p.cnt["vln"])
                            s.dma_start(out=G["xres_dram"].ap(),
                                        in_=G["x_res"][:]).then_inc(s_sp, 16)
                        p.bump("dsp", 16)
                        emit_output_dram(p, block, G)
                        done = True
                    if not done:
                        mp_parts = {"mp_w": ("w",), "mp_mm": ("w", "mm")}.get(
                            stage, ("w", "mm", "pay"))
                        emit_m_proj(p, block, G, parts=mp_parts)
                    if stage in ("mproj", "mp_w", "mp_mm"):
                        emit_output_dram(p, block, G)
                        done = True
                    if not done:
                        emit_m_halo(p, block, G)
                        if stage == "mhalo":
                            emit_output_dram(p, block, G)
                            done = True
                    if not done:
                        mbar(p, block, tensor_done=("txz", p.cnt["txz"]),
                             dma_waits=[("dsp", p.cnt["dsp"]), ("dxs", p.cnt["dxs"]),
                                        ("dg1", p.cnt["dg1"]), ("dpy", p.cnt["dpy"])])

        if not done:
            with contextlib.ExitStack() as s2:
                G["u_bf"] = p.sbuf("u_bf", [128, 8 * T], BF16, s2)
                G["y_locbf"] = p.sbuf("y_locbf", [128, 8 * T], BF16, s2)
                G["dlt_bf"] = p.sbuf("dlt_bf", [128, 8 * T], BF16, s2)
                G["bc_bf"] = p.sbuf("bc_bf", [32, T], BF16, s2)
                G["dtlow_bf"] = p.sbuf("dtlow_bf", [64, T], BF16, s2)
                G["sel_sb"] = p.sbuf("sel_sb", [32, 32 * 128], BF16, s2)
                G["zeros_bf"] = p.sbuf("zeros_bf", [128, T], BF16, s2)
                G["w_xd_sb"] = p.sbuf("w_xd_sb", [128, 8 * 96], BF16, s2)
                G["w_dt_sb"] = p.sbuf("w_dt_sb", [64, 1024], BF16, s2)
                G["s_all"] = p.sbuf("s_all", [128, 128], F32, s2)
                G["s_fin"] = p.sbuf("s_fin", [128, 128], F32, s2)
                G["p_all"] = p.sbuf("p_all", [128, 128], F32, s2)
                G["dTot_col"] = p.sbuf("dTot_col", [128, 8], F32, s2)

                s_mw = p.sem("dmw")

                @block.gpsimd
                def _(g):
                    g.dma_start(out=G["w_xd_sb"][:], in_=G["in_wxdT"]).then_inc(s_mw, 16)
                    g.dma_start(out=G["w_dt_sb"][:], in_=G["in_wdtT"]).then_inc(s_mw, 16)
                    g.dma_start(out=G["sel_sb"][:], in_=G["in_sel"]).then_inc(s_mw, 16)
                p.bump("dmw", 16 * 3)

                with contextlib.ExitStack() as s2a:
                    G["xpre"] = [p.sbuf(f"xpre{k}", [128, 2051], BF16, s2a) for k in range(2)]
                    G["cv_acc"] = [p.sbuf(f"cvacc{k}", [128, T], F32, s2a) for k in range(2)]
                    G["z_stg"] = [p.sbuf(f"z_stg{k}", [128, T], BF16, s2a) for k in range(4)]
                    G["sp_ab"] = [p.sbuf(f"sp_ab{k}", [128, 512], F32, s2a) for k in range(2)]
                    G["sp_rl"] = p.sbuf("sp_rl", [128, 512], F32, s2a)
                    G["dl_stg"] = [p.sbuf(f"dl_stg{k}", [128, T], BF16, s2a) for k in range(2)]
                    G["ps_xd"] = [p.psum(f"ps_xd{k}", [128, 512], F32, s2a) for k in range(2)]
                    G["ps_dt"] = [p.psum(f"ps_dt{k}", [128, 512], F32, s2a) for k in range(2)]
                    emit_m_conv(p, block, G)
                    if stage == "mconv":
                        emit_output_dram(p, block, G)
                        done = True
                    if not done:
                        emit_m_xdbl(p, block, G)
                        if stage == "mxdbl":
                            emit_output_dram(p, block, G)
                            done = True
                    if not done:
                        mbar(p, block, tensor_done=("tdt", p.cnt["tdt"]),
                             dma_waits=[("ddm", p.cnt["ddm"]), ("dzs", p.cnt["dzs"]),
                                        ("dxl", p.cnt["dxl"])])

                if not done:
                    with contextlib.ExitStack() as s2c:
                        G["dA_bf"] = [p.sbuf(f"dA_bf{k}", [128, T], BF16, s2c) for k in range(2)]
                        G["dBu_bf"] = [p.sbuf("dBu_bf0", [128, T], BF16, s2c)] * 2
                        G["h_bf"] = [p.sbuf(f"h_bf{k}", [128, T], BF16, s2c) for k in range(2)]
                        G["hC_bf"] = [p.sbuf(f"hC_bf{k}", [128, T], BF16, s2c) for k in range(3)]
                        G["Cbc_bf"] = [p.sbuf(f"Cbc_bf{k}", [128, T], BF16, s2c) for k in range(2)]
                        G["du_f"] = p.sbuf("du_f", [128, T], BF16, s2c)
                        G["y_dt"] = [p.sbuf(f"y_dt{k}", [128, T], F32, s2c) for k in range(2)]
                        G["ps_B"] = p.psum("ps_B", [128, 2048], F32, s2c)
                        G["ps_C"] = p.psum("ps_C", [128, 2048], F32, s2c)
                        emit_m_scan(p, block, G)
                        if stage == "mscan":
                            emit_output_dram(p, block, G)
                            done = True
                        if not done:
                            mbar(p, block, tensor_done=("tcp", p.cnt["tcp"]))

                if not done:
                    with contextlib.ExitStack() as s2d:
                        G["gath2_sb"] = p.sbuf("gath2_sb", [128, 8 * 256], F32, s2d)
                        G["h0_all"] = p.sbuf("h0_all", [128, 128], F32, s2d)
                        G["psel"] = p.sbuf("psel", [128, 128], F32, s2d)
                        G["ssel"] = p.sbuf("ssel", [128, 128], F32, s2d)
                        G["dl_t"] = [p.sbuf(f"dl_t{k}", [128, T], BF16, s2d) for k in range(2)]
                        G["wn_bf"] = [p.sbuf(f"wn_bf{k}", [128, T], BF16, s2d) for k in range(2)]
                        G["t1_bf"] = p.sbuf("t1_bf", [128, T], BF16, s2d)
                        G["acc_f"] = p.sbuf("acc_f", [128, T], F32, s2d)
                        G["tmp_f"] = p.sbuf("tmp_f", [128, T], F32, s2d)
                        G["ps_c2"] = [p.psum(f"ps_c2{k}", [128, 2048], F32, s2d) for k in range(2)]
                        emit_m_exchange(p, block, G)
                        if stage == "mexch":
                            emit_output_dram(p, block, G)
                            done = True
                        if not done:
                            emit_m_corr(p, block, G)
                            if stage == "mcorr":
                                emit_output_dram(p, block, G)
                                done = True
                        if not done:
                            mbar(p, block, tensor_done=("tc2", p.cnt["tc2"]),
                                 dma_waits=[("dyw", p.cnt["dyw"])])

        if not done:
            G["x_res"] = p.sbuf("x_res_b", [128, NTT * D], F32)
            with contextlib.ExitStack() as s3a:
                G["y_sb"] = p.sbuf("y_sb", [128, 8 * T], BF16, s3a)
                G["z_sb"] = p.sbuf("z_sb", [128, 8 * T], BF16, s3a)
                G["w_out_sb"] = p.sbuf("w_out_sb", [128, 16 * 1024], BF16, s3a)
                G["bout_bf"] = p.sbuf("bout_bf", [1, 1024], BF16, s3a)
                G["ps_o"] = [p.psum(f"ps_o{k}", [128, 1024], F32, s3a) for k in range(2)]
                emit_m_out(p, block, G)
                mbar(p, block, tensor_done=("tom", p.cnt["tom"]))

            if stage == "mamba":
                emit_output(p, block, G)
                done = True

            if not done:
                with contextlib.ExitStack() as f2:
                    emit_ffn_pass(p, block, G, f2, 1, "_b")
                emit_output(p, block, G)

    p.stack.close()
    return p, G


_CACHE = {}


def _host_prep(inputs):
    f32 = np.float32
    L = {}
    w1 = [np.ascontiguousarray(inputs["L0_ffn_shared_w1"].T)] + \
         [np.ascontiguousarray(inputs["L0_ffn_spec_w1"][e].T) for e in range(8)]
    L["w1T"] = np.stack(w1).astype(f32, copy=False)
    w2 = [np.ascontiguousarray(inputs["L0_ffn_shared_w2"].T)] + \
         [np.ascontiguousarray(inputs["L0_ffn_spec_w2"][e].T) for e in range(8)]
    L["w2T"] = np.stack(w2).astype(f32, copy=False)
    b1 = np.stack([inputs["L0_ffn_shared_b1"]] + [inputs["L0_ffn_spec_b1"][e] for e in range(8)])
    L["b1col"] = np.ascontiguousarray(b1.reshape(9, 32, 128).transpose(2, 1, 0).reshape(128, 32 * 9)).astype(f32)
    L["b2row"] = np.ascontiguousarray(
        np.stack([inputs["L0_ffn_shared_b2"]] + [inputs["L0_ffn_spec_b2"][e] for e in range(8)]).reshape(1, -1)).astype(f32)
    L["ones_r"] = np.ones((1, 128), dtype=f32)
    rw = inputs["L0_ffn_router_w"]
    L["rw_prep"] = np.ascontiguousarray(rw.reshape(8, 8, 128).transpose(2, 1, 0).reshape(128, 64)).astype(f32)
    L["rb_bc"] = np.ascontiguousarray(np.tile(inputs["L0_ffn_router_b"][None, :], (128, 1))).astype(f32)

    L["winT"] = np.ascontiguousarray(
        inputs["L0_attn_W_in"].reshape(8, 128, 2048).transpose(1, 0, 2).reshape(128, 8 * 2048)).astype(f32)
    L["wxdT"] = np.ascontiguousarray(
        inputs["L0_attn_W_xdbl"].reshape(8, 128, 96).transpose(1, 0, 2).reshape(128, 8 * 96)).astype(f32)
    L["wdtT"] = np.ascontiguousarray(inputs["L0_attn_W_dt"]).astype(f32)
    L["woutT"] = np.ascontiguousarray(
        inputs["L0_attn_W_out"].reshape(16, 128, 1024).transpose(1, 0, 2).reshape(128, 16 * 1024)).astype(f32)
    L["sel32"] = np.ascontiguousarray(np.kron(np.eye(32, dtype=f32), np.ones((1, 128), f32)))
    L["bout_r"] = np.ascontiguousarray(inputs["L0_attn_b_out"].reshape(1, 1024)).astype(f32)
    cxw = inputs["L0_attn_conv_x_w"].reshape(4, 1024)
    czw = inputs["L0_attn_conv_z_w"].reshape(4, 1024)
    cc = np.empty((128, 64), f32)
    for dt in range(8):
        for j in range(4):
            cc[:, dt * 4 + j] = cxw[j, dt * 128:(dt + 1) * 128]
            cc[:, 32 + dt * 4 + j] = czw[j, dt * 128:(dt + 1) * 128]
    L["convc"] = cc
    cb = np.empty((128, 16), f32)
    for dt in range(8):
        cb[:, dt] = inputs["L0_attn_conv_x_b"][dt * 128:(dt + 1) * 128]
        cb[:, 8 + dt] = inputs["L0_attn_conv_z_b"][dt * 128:(dt + 1) * 128]
    L["convb"] = cb
    L["invdtc"] = np.ascontiguousarray(
        inputs["L0_attn_inv_dt"].reshape(8, 128).T).astype(f32)
    L["dcolc"] = np.ascontiguousarray(inputs["L0_attn_D"].reshape(8, 128).T).astype(f32)
    return L


def _core_masks(core):
    b, c = core // 4, core % 4
    f32 = np.float32
    msel = np.zeros((1, 24), f32)
    for j in range(3):
        cc = c - 1 - j
        if cc >= 0:
            msel[0, j * 8 + (b * 4 + cc)] = 1.0
    msel2 = np.zeros((1, 16), f32)
    if c > 0:
        msel2[0, b * 4 + c - 1] = 1.0
    if c < 3:
        msel2[0, 8 + b * 4 + c + 1] = 1.0
    return (np.ascontiguousarray(np.tile(msel, (128, 1))),
            np.ascontiguousarray(np.tile(msel2, (128, 1))))


def kernel(**inputs):
    key = ("prog", STAGE)
    if key not in _CACHE:
        _CACHE[key] = build(STAGE)
    p, G = _CACHE[key]

    L = _host_prep(inputs)
    x = np.asarray(inputs["L0_attn_x"], dtype=np.float32)

    in_maps = []
    for core in range(8):
        b, c = core // 4, core % 4
        msel, msel2 = _core_masks(core)
        m = {
            "x_sh": np.ascontiguousarray(x[b, c * T:(c + 1) * T, :]),
            "w1T": L["w1T"], "w2T": L["w2T"], "b1col": L["b1col"], "b2row": L["b2row"],
            "ones_r": L["ones_r"],
            "rw_prep": L["rw_prep"], "rb_bc": L["rb_bc"],
            "winT": L["winT"], "wxdT": L["wxdT"], "wdtT": L["wdtT"],
            "woutT": L["woutT"], "sel32": L["sel32"], "bout_r": L["bout_r"],
            "convc": L["convc"], "convb": L["convb"], "invdtc": L["invdtc"],
            "dcolc": L["dcolc"], "mselc": msel, "msel2c": msel2,
        }
        in_maps.append(m)

    res = run_bass_kernel_spmd(p.nc, in_maps, core_ids=list(range(8)))
    out = np.empty((2, 8192, D), dtype=np.float32)
    for core in range(8):
        b, c = core // 4, core % 4
        out[b, c * T:(c + 1) * T, :] = res.results[core]["out"]
    return out
